# version 24
# baseline (speedup 1.0000x reference)
import numpy as np
import ml_dtypes
from contextlib import ExitStack

import concourse.bass as bass
import concourse.mybir as mybir
from concourse.bass_utils import run_bass_kernel_spmd

F32 = mybir.dt.float32
BF16 = mybir.dt.bfloat16
AF = mybir.ActivationFunctionType
ALU = mybir.AluOpType
AX = mybir.AxisListType
BF = ml_dtypes.bfloat16

T = 2048
D = 1024
NT = T // 128
I0 = 2816
I1 = 3072
H = 16
HD = 64
FEAT = 16
NF = 153
CH = 128


STREAMS = ("pe", "dve", "act", "pool", "sp")
ENG_DOMS = ("pe", "dve", "act", "pool")
DMA_RING = {"sp": 6, "act": 2, "pool": 2}


class Em:
    def __init__(self, nc):
        self.nc = nc
        self.items = {s: [] for s in STREAMS}
        self.domains = list(ENG_DOMS)
        self.rings = {}
        for s, n in DMA_RING.items():
            self.rings[s] = [f"{s}q{i}" for i in range(n)]
            self.domains += self.rings[s]
        self.dma_idx = {s: 0 for s in DMA_RING}
        self.count = {d: 0 for d in self.domains}
        self.hwm = {(s, d): 0 for s in STREAMS for d in self.domains}
        self.tens = {}

    @staticmethod
    def _ivl(ap):
        lo = int(ap.offset)
        span = 0
        for step, cnt in list(ap.ap):
            span += abs(int(step)) * (int(cnt) - 1)
        hi = lo + span + 1
        if type(ap.tensor).__name__ == "PSumTensorHandle":
            ge = 2048 // mybir.dt.size(ap.dtype)
            return (ap.tensor.name, (lo // ge) * ge,
                    ((hi + ge - 1) // ge) * ge)
        return (ap.tensor.name, lo, hi)

    def _dep_read(self, key, need):
        name, lo, hi = key
        t = self.tens.get(name)
        if not t:
            return
        for (wlo, whi, dom, cnt) in t["w"]:
            if wlo < hi and lo < whi:
                if cnt > need.get(dom, 0):
                    need[dom] = cnt

    def _dep_write(self, key, need):
        name, lo, hi = key
        t = self.tens.get(name)
        if not t:
            return
        for rec in t["w"] + t["r"]:
            (xlo, xhi, dom, cnt) = rec
            if xlo < hi and lo < xhi:
                if cnt > need.get(dom, 0):
                    need[dom] = cnt

    def _record(self, kind, key, dom, cnt):
        name, lo, hi = key
        t = self.tens.setdefault(name, {"w": [], "r": []})
        if kind == "w":
            t["w"] = [r for r in t["w"] if not (lo <= r[0] and r[1] <= hi)]
            t["r"] = [r for r in t["r"] if not (lo <= r[0] and r[1] <= hi)]
            t["w"].append((lo, hi, dom, cnt))
        else:
            t["r"].append((lo, hi, dom, cnt))
            if len(t["r"]) > 400:
                by = {}
                for (xlo, xhi, d2, c2) in t["r"]:
                    cur = by.get(d2)
                    by[d2] = (min(xlo, cur[0]), max(xhi, cur[1]),
                              max(c2, cur[2])) if cur else (xlo, xhi, c2)
                t["r"] = [(l2, h2, d2, c2) for d2, (l2, h2, c2) in by.items()]

    def _wait(self, stream, d, val):
        if val > self.hwm[(stream, d)]:
            self.items[stream].append(("w", d, val))
            self.hwm[(stream, d)] = val

    def op(self, stream, fn, ins=(), outs=(), dom=None, inc=1, tag=""):
        if dom is None:
            dom = stream
        need = {}
        rkeys = [self._ivl(ap) for ap in ins]
        wkeys = [self._ivl(ap) for ap in outs]
        for k in rkeys:
            self._dep_read(k, need)
        for k in wkeys:
            self._dep_write(k, need)
        for d, val in sorted(need.items()):
            self._wait(stream, d, val)
        self.count[dom] += inc
        cnt = self.count[dom]
        self.items[stream].append(("op", fn, dom, inc, tag))
        for k in rkeys:
            self._record("r", k, dom, cnt)
        for k in wkeys:
            self._record("w", k, dom, cnt)

    def dma(self, stream, out, in_, tag=""):
        ring = self.rings[stream]
        dom = ring[self.dma_idx[stream] % len(ring)]
        self.dma_idx[stream] += 1
        self._wait(stream, dom, self.count[dom])
        self.op(stream, lambda e: e.dma_start(out=out, in_=in_),
                ins=[in_], outs=[out], dom=dom, inc=16, tag=tag)

    def barrier(self):
        for s in STREAMS:
            for d in self.domains:
                self._wait(s, d, self.count[d])
        self.tens = {}

    def finalize(self, block, sems):
        eng_attr = {"pe": "tensor", "dve": "vector", "act": "scalar",
                    "pool": "gpsimd", "sp": "sync"}

        def mk(stream):
            def body(e):
                for it in self.items[stream]:
                    if it[0] == "w":
                        e.wait_ge(sems[it[1]], it[2])
                    else:
                        _, fn, dom, inc, tag = it
                        instr = fn(e)
                        instr.then_inc(sems[dom], inc)
            return body

        for stream in STREAMS:
            getattr(block, eng_attr[stream])(mk(stream))


class Pipe:

    def __init__(self, depth=1):
        self.q = []
        self.depth = depth

    def push(self, front, tail):
        front()
        self.q.append(tail)
        if len(self.q) > self.depth:
            self.q.pop(0)()

    def flush(self):
        for t in self.q:
            t()
        self.q = []



MID = [0.25, 0.75, 1.25, 1.75, 2.5, 3.5, 5.0]
WJ = [0.5, 0.5, 0.5, 0.5, 1.0, 1.0, 2.0]
P = 128


def emit_ln(em, x_ap, xn_ap, s, si=0, dump=None):
    st = s["stats"]
    ssq = st[:, 32:33]
    sm = st[:, 33:34]
    rstd = st[:, 2 * si:2 * si + 1]
    mneg = st[:, 2 * si + 1:2 * si + 2]
    mu, mu2, var, sd = s["mu"], s["mu2"], s["var"], s["sd"]
    if dump is None:
        dump = s["dump"]
    em.op("act", lambda e: e.activation(out=dump, in_=x_ap, func=AF.Square,
                                        accum_out=ssq),
          ins=[x_ap], outs=[dump, ssq])
    em.op("dve", lambda e: e.tensor_reduce(out=sm, in_=x_ap, axis=AX.X, op=ALU.add),
          ins=[x_ap], outs=[sm])
    em.op("dve", lambda e: e.tensor_scalar(out=mu, in0=sm, scalar1=1.0 / D,
                                           scalar2=None, op0=ALU.mult),
          ins=[sm], outs=[mu])
    em.op("dve", lambda e: e.tensor_scalar(out=mu2, in0=mu, scalar1=mu,
                                           scalar2=None, op0=ALU.mult),
          ins=[mu], outs=[mu2])
    em.op("dve", lambda e: e.scalar_tensor_tensor(out=var, in0=ssq, scalar=1.0 / D,
                                                  in1=mu2, op0=ALU.mult,
                                                  op1=ALU.subtract),
          ins=[ssq, mu2], outs=[var])
    em.op("dve", lambda e: e.tensor_scalar(out=sd, in0=var, scalar1=1e-5,
                                           scalar2=None, op0=ALU.add),
          ins=[var], outs=[sd])
    em.op("act", lambda e: e.activation(out=s["sd2"], in_=sd, func=AF.Sqrt),
          ins=[sd], outs=[s["sd2"]])
    em.op("dve", lambda e: e.reciprocal(out=rstd, in_=s["sd2"]),
          ins=[s["sd2"]], outs=[rstd])
    em.op("dve", lambda e: e.tensor_scalar(out=mneg, in0=mu, scalar1=rstd,
                                           scalar2=-1.0, op0=ALU.mult, op1=ALU.mult),
          ins=[mu, rstd], outs=[mneg])
    em.op("dve", lambda e: e.tensor_scalar(out=xn_ap, in0=x_ap, scalar1=rstd,
                                           scalar2=mneg, op0=ALU.mult, op1=ALU.add),
          ins=[x_ap, rstd, mneg], outs=[xn_ap])


QPAR = [0]


def emit_quant(em, src, dst, W, q):
    off = 0
    while off < W:
        w = min(512, W - off)
        _quant_slice(em, src[:, off:off + w], dst[:, off:off + w], w, q,
                     QPAR[0] % 2)
        QPAR[0] += 1
        off += w


def _quant_slice(em, src, dst, W, q, par):
    nb = W // 16
    o = par * 512
    ob = par * 32
    amax = q["amax"][:, ob:ob + nb]
    scaleb = q["scaleb"][:, ob:ob + nb]
    r = q["r"][:, ob:ob + nb]
    z = q["z"][:, o:o + W]
    y = q["y"][:, o:o + W]
    b0, b1, b2, b3, b4 = (q[k][:, o:o + W] for k in ("b0", "b1", "b2", "b3", "b4"))
    src3 = src.rearrange("p (f b) -> p f b", b=16)
    z3 = z.rearrange("p (f b) -> p f b", b=16)
    em.op("dve", lambda e: e.tensor_reduce(out=amax, in_=src3, axis=AX.X,
                                           op=ALU.max, apply_absolute_value=True),
          ins=[src], outs=[amax])
    em.op("dve", lambda e: e.tensor_scalar(out=scaleb, in0=amax, scalar1=6e-12,
                                           scalar2=None, op0=ALU.max),
          ins=[amax], outs=[scaleb])
    em.op("dve", lambda e: e.reciprocal(out=r, in_=scaleb), ins=[scaleb], outs=[r])
    em.op("dve", lambda e: e.tensor_scalar(out=r, in0=r, scalar1=6.0,
                                           scalar2=None, op0=ALU.mult),
          ins=[r], outs=[r])
    em.op("dve", lambda e: e.reciprocal(out=scaleb, in_=r), ins=[r], outs=[scaleb])
    em.op("dve", lambda e: e.tensor_tensor(out=z3, in0=src3,
                                           in1=r.broadcast_to([P, nb, 16]),
                                           op=ALU.mult),
          ins=[src, r], outs=[z])
    em.op("act", lambda e: e.activation(out=y, in_=z, func=AF.Abs),
          ins=[z], outs=[y])
    em.op("act", lambda e: e.activation(out=b4, in_=z, func=AF.Sign),
          ins=[z], outs=[b4])

    def cmp(dstb, j, eng="dve"):
        em.op(eng, lambda e: e.tensor_scalar(out=dstb, in0=y, scalar1=MID[j],
                                             scalar2=WJ[j], op0=ALU.is_gt,
                                             op1=ALU.mult),
              ins=[y], outs=[dstb])

    def add(dstb, a, bb, eng="dve"):
        em.op(eng, lambda e: e.tensor_tensor(out=dstb, in0=a, in1=bb, op=ALU.add),
              ins=[a, bb], outs=[dstb])

    cmp(b0, 0)
    cmp(b1, 1)
    add(b2, b0, b1)
    cmp(b0, 2)
    cmp(b1, 3)
    add(b3, b0, b1)
    add(b2, b2, b3)
    em.op("act", lambda e: e.activation(out=b0, in_=y, func=AF.Sign,
                                        bias=q["s25"]),
          ins=[y, q["s25"]], outs=[b0])
    em.op("act", lambda e: e.activation(out=b1, in_=y, func=AF.Sign,
                                        bias=q["s35"]),
          ins=[y, q["s35"]], outs=[b1])
    em.op("act", lambda e: e.activation(out=b3, in_=y, func=AF.Sign,
                                        bias=q["s50"]),
          ins=[y, q["s50"]], outs=[b3])
    em.op("dve", lambda e: e.tensor_scalar(out=b0, in0=b0, scalar1=0.5,
                                           scalar2=2.0, op0=ALU.mult,
                                           op1=ALU.add),
          ins=[b0], outs=[b0])
    em.op("dve", lambda e: e.scalar_tensor_tensor(out=b1, in0=b1, scalar=0.5,
                                                  in1=b3, op0=ALU.mult,
                                                  op1=ALU.add),
          ins=[b1, b3], outs=[b1])
    add(b0, b0, b1)
    add(b0, b2, b0)
    em.op("dve", lambda e: e.tensor_tensor(
        out=b1.rearrange("p (f b) -> p f b", b=16),
        in0=b0.rearrange("p (f b) -> p f b", b=16),
        in1=scaleb.broadcast_to([P, nb, 16]), op=ALU.mult),
        ins=[b0, scaleb], outs=[b1])
    em.op("dve", lambda e: e.tensor_tensor(out=dst, in0=b1, in1=b4, op=ALU.mult),
          ins=[b1, b4], outs=[dst])


def emit_tp8(em, src_bf, ps_tp, dst3, ident, nblk=8, evac="act"):
    def tp(e):
        last = None
        for j in range(nblk):
            last = e.transpose(ps_tp[:, j * 128:(j + 1) * 128],
                               in_=src_bf[:, j * 128:(j + 1) * 128],
                               identity=ident)
        return last
    em.op("pe", tp, ins=[src_bf, ident], outs=[ps_tp[:]])
    src3 = ps_tp[:, 0:nblk * 128].rearrange("p (c j) -> p c j", c=nblk)
    em.op(evac, lambda e: e.tensor_copy(out=dst3, in_=src3)
          if evac != "act" else e.activation(out=dst3, in_=src3, func=AF.Copy),
          ins=[ps_tp[:]], outs=[dst3])


def emit_mm(em, ps_ap, pairs, start=True, stop=True, tag=""):
    n = len(pairs)
    def fn(e):
        last = None
        for i, (l, r) in enumerate(pairs):
            last = e.matmul(ps_ap, lhsT=l, rhs=r,
                            start=(start and i == 0),
                            stop=(stop and i == n - 1))
        return last
    ins = [ap for pr in pairs for ap in pr]
    em.op("pe", fn, ins=ins, outs=[ps_ap], tag=tag)


def phase_l0f(nc, em, es, dd, C):
    sb = lambda name, shape, dt=F32: es.enter_context(
        nc.sbuf_tensor("l0f_" + name, shape, dt))
    pst = lambda name, shape, dt=F32: es.enter_context(
        nc.psum_tensor("l0f_" + name, shape, dt))
    wgq = sb("wgq", [P, 8 * I0], BF16)
    wuq = sb("wuq", [P, 8 * I0], BF16)
    wdq = sb("wdq", [P, 22 * D], BF16)
    xq_fm = sb("xqfm", [P, 8 * T], BF16)
    scrf = sb("scrf", [P, 3072])
    scrb = sb("scrb", [P, 2048], BF16)
    qs = {"z": sb("qz", [P, 1024]), "y": sb("qy", [P, 1024]),
          "amax": sb("qam", [P, 64]), "scaleb": sb("qsc", [P, 64]),
          "r": sb("qr", [P, 64]),
          "s25": C["qb"][:, 0:1], "s35": C["qb"][:, 1:2],
          "s50": C["qb"][:, 2:3]}
    for k in ("b0", "b1", "b2", "b3", "b4"):
        qs[k] = sb("q" + k, [P, 1024], BF16)
    stats = sb("stats", [P, 36])
    small = sb("small", [P, 8])
    lns = {"stats": stats, "mu": small[:, 0:1], "mu2": small[:, 1:2],
           "var": small[:, 2:3], "sd": small[:, 3:4], "sd2": small[:, 4:5]}
    ps_ab = [pst("psab%d" % i, [P, 512]) for i in range(4)]
    ps_dn = [pst("psdn%d" % i, [P, 512]) for i in range(2)]
    ps_tps = [pst("pstp%d" % i, [P, 1024], BF16) for i in range(2)]
    tppar = [0]
    def ps_tp_next():
        tppar[0] += 1
        return ps_tps[tppar[0] % 2]
    ident = C["ident"]

    s01 = [scrf[:, 0:1024], scrf[:, 1024:2048]]
    wq_tms = [scrb[:, 0:1024], scrb[:, 1024:2048]]

    def wtiles(wsrc, nrow):
        for k in range(nrow):
            yield k, wsrc[k * 128:(k + 1) * 128, :]

    for wi, (wsrc, dst, dcol) in enumerate([(dd["wg"], wgq, I0), (dd["wu"], wuq, I0)]):
        for k in range(22):
            stg = s01[k % 2]
            wq_tm = wq_tms[k % 2]
            em.dma("sp", stg, wsrc[k * 128:(k + 1) * 128, :], tag=f"w{wi}t{k}")
            emit_quant(em, stg, wq_tm, 1024, qs)
            dst3 = dst[:, :].rearrange("p (c n) -> p c n", c=8)[:, :, k * 128:(k + 1) * 128]
            emit_tp8(em, wq_tm, ps_tp_next(), dst3, ident)
    for r in range(8):
        for s, W in ((0, 1024), (1, 1024), (2, 768)):
            stg = s01[(r * 3 + s) % 2]
            wq_tm = wq_tms[(r * 3 + s) % 2]
            em.dma("sp", stg[:, 0:W],
                   dd["wd"][r * 128:(r + 1) * 128, s * 1024:s * 1024 + W],
                   tag=f"wdt{r}.{s}")
            emit_quant(em, stg[:, 0:W], wq_tm[:, 0:W], W, qs)
            nb = W // 128
            dst3 = wdq[:, :].rearrange("p (k n) -> p k n", k=22)[
                :, s * 8:s * 8 + nb, r * 128:(r + 1) * 128]
            emit_tp8(em, wq_tm, ps_tp_next(), dst3, ident, nblk=nb)

    xn = scrf[:, 2048:3072]
    for tt in range(NT):
        stg = s01[tt % 2]
        xq_tm = wq_tms[tt % 2]
        em.dma("sp", stg, dd["x"][tt * 128:(tt + 1) * 128, :], tag=f"xt{tt}")
        emit_ln(em, stg, xn, lns, si=tt, dump=xn)
        emit_quant(em, xn, xq_tm, 1024, qs)
        dst3 = xq_fm[:, :].rearrange("p (c n) -> p c n", c=8)[:, :, tt * 128:(tt + 1) * 128]
        emit_tp8(em, xq_tm, ps_tp_next(), dst3, ident)

    h3fs = [scrf[:, 0:512], scrf[:, 2048:2560]]
    sgs = [scrf[:, 512:1024], scrf[:, 2560:3072]]
    gtile = scrf[:, 1024:2048]
    h3qs = [scrb[:, 0:512], scrb[:, 1024:1536]]
    h3qfms = [scrb[:, 512:1024], scrb[:, 1536:2048]]
    HGS = [(hg, 512 if hg < 5 else 256) for hg in range(6)]
    for tt in range(NT):
        xcol = lambda c: xq_fm[:, c * T + tt * 128: c * T + (tt + 1) * 128]
        for hg, W in HGS:
            pa = ps_ab[(hg % 2) * 2]
            pb = ps_ab[(hg % 2) * 2 + 1]
            sg = sgs[hg % 2]
            h3f = h3fs[hg % 2]
            h3q = h3qs[hg % 2]
            h3qfm = h3qfms[hg % 2]
            emit_mm(em, pa[:, 0:W],
                    [(xcol(c), wgq[:, c * I0 + hg * 512: c * I0 + hg * 512 + W])
                     for c in range(8)], tag=f"gate{tt}.{hg}")
            emit_mm(em, pb[:, 0:W],
                    [(xcol(c), wuq[:, c * I0 + hg * 512: c * I0 + hg * 512 + W])
                     for c in range(8)], tag=f"up{tt}.{hg}")
            em.op("act", lambda e, pa=pa, W=W, sg=sg: e.activation(
                out=sg[:, 0:W], in_=pa[:, 0:W], func=AF.Silu),
                ins=[pa[:]], outs=[sg[:, 0:W]])
            em.op("dve", lambda e, pb=pb, W=W, sg=sg, h3f=h3f: e.tensor_tensor(
                out=h3f[:, 0:W], in0=sg[:, 0:W], in1=pb[:, 0:W], op=ALU.mult),
                ins=[sg[:, 0:W], pb[:]], outs=[h3f[:, 0:W]])
            emit_quant(em, h3f[:, 0:W], h3q[:, 0:W], W, qs)
            nb = W // 128
            dst3 = h3qfm[:, 0:W].rearrange("p (c j) -> p c j", c=nb)
            emit_tp8(em, h3q, ps_tp_next(), dst3, ident, nblk=nb)
            for half in range(2):
                emit_mm(em, ps_dn[half][:],
                        [(h3qfm[:, j * 128:(j + 1) * 128],
                          wdq[:, (hg * 4 + j) * D + half * 512:
                              (hg * 4 + j) * D + half * 512 + 512])
                         for j in range(nb)],
                        start=(hg == 0), stop=(hg == 5), tag=f"dn{tt}.{hg}.{half}")
        for half in range(2):
            em.op("act", lambda e, half=half: e.activation(
                out=gtile[:, half * 512:(half + 1) * 512], in_=ps_dn[half][:],
                func=AF.Silu),
                ins=[ps_dn[half][:]], outs=[gtile[:, half * 512:(half + 1) * 512]])
        em.dma("sp", dd["g0"][tt * 128:(tt + 1) * 128, :], gtile, tag=f"g0t{tt}")


def phase_l0a(nc, em, es, dd, C):
    sb = lambda name, shape, dt=F32: es.enter_context(
        nc.sbuf_tensor("l0a_" + name, shape, dt))
    pst = lambda name, shape, dt=F32: es.enter_context(
        nc.psum_tensor("l0a_" + name, shape, dt))
    wq_sb = sb("wq", [P, 8 * D], BF16)
    wk_sb = sb("wk", [P, 8 * D], BF16)
    wv_sb = sb("wv", [P, 8 * D], BF16)
    wo_sb = sb("wo", [P, 8 * D], BF16)
    bqkv = sb("bqkv", [P, 24])
    bvrep = sb("bvrep", [P, D])
    borep = sb("borep", [P, D])
    xn_fm = sb("xnfm", [P, 8 * 512], BF16)
    k_fm = sb("kfm", [P, 8 * T], BF16)
    v_tm = sb("vtm", [P, 16 * 1024], BF16)
    q_col = sb("qcol", [P, 8 * 512], BF16)
    a_col = q_col
    Pb = sb("p", [P, 6144], BF16)
    PT = sb("pt", [P, 6144], BF16)
    sdm = sb("sdm", [P, 384])
    xwork = [sb("xw0", [P, D])[:], sb("xw1", [P, D])[:]]
    x_bf = sb("xbf", [P, D], BF16)
    gwork = sb("gw", [P, D])
    swork = sb("sw", [P, D])
    stats = sb("stats", [P, 36])
    small = sb("small", [P, 8])
    lns = {"stats": stats, "mu": small[:, 0:1], "mu2": small[:, 1:2],
           "var": small[:, 2:3], "sd": small[:, 3:4], "sd2": small[:, 4:5],
           "dump": swork[:, 0:1024]}
    mrow = sb("mrow", [P, 36])
    ps_s0 = pst("pss0", [P, 1024])
    ps_s1 = pst("pss1", [P, 1024])
    ps_pr = [pst("pspr%d" % i, [P, 512]) for i in range(2)]
    ps_tp = [pst("pstp%d" % i, [P, 1024], BF16) for i in range(2)]
    ident = C["ident"]
    mdiag = C["mdiag"]

    for wt, (src, dst) in enumerate([(dd["wqT"], wq_sb), (dd["wkT"], wk_sb),
                                     (dd["wvT"], wv_sb), (dd["woT"], wo_sb)]):
        for d8 in range(8):
            stg = xwork[d8 % 2]
            em.dma("sp", stg, src[d8 * 128:(d8 + 1) * 128, :], tag=f"w{wt}.{d8}")
            em.op("act", lambda e, stg=stg, dst=dst, d8=d8: e.activation(
                out=dst[:, d8 * D:(d8 + 1) * D], in_=stg, func=AF.Copy),
                ins=[stg], outs=[dst[:, d8 * D:(d8 + 1) * D]])
    em.dma("sp", bqkv[:], dd["bqkv"])
    em.dma("sp", bvrep[:], dd["bvrep"])
    em.dma("sp", borep[:], dd["borep"])

    scale = 1.0 / 8.0
    for t4 in range(4):
        tl = t4 * 512
        for loc in range(4):
            tt = t4 * 4 + loc
            stg = xwork[tt % 2]
            em.dma("sp", stg, dd["x"][tt * 128:(tt + 1) * 128, :], tag=f"xa{tt}")
            emit_ln(em, stg, x_bf[:], lns, si=tt)
            dst3 = xn_fm[:, :].rearrange("p (c n) -> p c n", c=8)[:, :, loc * 128:(loc + 1) * 128]
            emit_tp8(em, x_bf[:], ps_tp[tt % 2], dst3, ident)
        for c in range(8):
            for which, wsb, dstcol in [(0, wq_sb, q_col), (1, wk_sb, k_fm)]:
                ps = ps_pr[c % 2]
                emit_mm(em, ps[:],
                        [(wsb[:, d * D + c * 128:d * D + (c + 1) * 128],
                          xn_fm[:, d * 512:d * 512 + 512]) for d in range(8)],
                        tag=f"prj{which}.{t4}.{c}")
                if which == 0:
                    out = dstcol[:, c * 512:(c + 1) * 512]
                else:
                    out = dstcol[:, c * T + tl:c * T + tl + 512]
                em.op("dve", lambda e, ps=ps, out=out, c=c, which=which:
                      e.tensor_scalar(out=out, in0=ps[:],
                                      scalar1=bqkv[:, 8 * which + c:8 * which + c + 1],
                                      scalar2=None, op0=ALU.add),
                      ins=[ps[:], bqkv[:]], outs=[out])
        for ts4 in range(4):
            tt = t4 * 4 + ts4
            for half in range(2):
                ps = ps_pr[half]
                emit_mm(em, ps[:],
                        [(xn_fm[:, d * 512 + ts4 * 128:d * 512 + (ts4 + 1) * 128],
                          wv_sb[:, d * D + half * 512:d * D + half * 512 + 512])
                         for d in range(8)], tag=f"v{tt}.{half}")
                out = v_tm[:, tt * 1024 + half * 512:tt * 1024 + half * 512 + 512]
                em.op("dve", lambda e, ps=ps, out=out, half=half:
                      e.tensor_tensor(out=out, in0=ps[:],
                                      in1=bvrep[:, half * 512:half * 512 + 512],
                                      op=ALU.add),
                      ins=[ps[:], bvrep[:]], outs=[out])
        apipe = Pipe(depth=1)
        for qq in range(4):
            qt = t4 * 4 + qq
            kpre = qt * 128
            for h in range(H):
                def mk(h=h, qt=qt, qq=qq, kpre=kpre):
                    c, poff = h // 2, 64 * (h % 2)
                    par = h % 2
                    par3 = h % 2
                    q_h = q_col[poff:poff + 64,
                                c * 512 + qq * 128:c * 512 + (qq + 1) * 128]
                    kbase = c * T
                    k_h = lambda lo, w: k_fm[poff:poff + 64, kbase + lo:kbase + lo + w]
                    sdmh = sdm[:, par3 * 128:(par3 + 1) * 128]
                    mr = lambda j: mrow[:, par3 * 12 + j:par3 * 12 + j + 1]
                    m_A, md, m_f, mb, z_A, z_d, zr, alz = (mr(j) for j in range(8))
                    sA = (ps_s0, ps_s1)[par] if qt < 8 else ps_s0
                    sB = ps_s1
                    pb0 = par3 * 2048
                    a_end = min(kpre, 1024)
                    b_len = kpre - a_end
                    diag_in_A = qt < 8
                    st = {}

                    def front():
                        for lo in range(0, a_end, 512):
                            w = min(512, a_end - lo)
                            emit_mm(em, sA[:, lo:lo + w], [(q_h, k_h(lo, w))],
                                    tag=f"qkA{qt}.{h}")
                        if diag_in_A:
                            emit_mm(em, sA[:, kpre:kpre + 128],
                                    [(q_h, k_h(kpre, 128))], tag=f"qkdA{qt}.{h}")
                            em.op("dve", lambda e: e.tensor_tensor(
                                out=sA[:, kpre:kpre + 128],
                                in0=sA[:, kpre:kpre + 128], in1=mdiag,
                                op=ALU.add), ins=[sA[:], mdiag],
                                outs=[sA[:, kpre:kpre + 128]])
                            em.op("dve", lambda e: e.tensor_reduce(
                                out=m_A, in_=sA[:, 0:kpre + 128], axis=AX.X,
                                op=ALU.max), ins=[sA[:]], outs=[m_A])
                            em.op("dve", lambda e: e.tensor_scalar(
                                out=mb, in0=m_A, scalar1=-scale, scalar2=None,
                                op0=ALU.mult), ins=[m_A], outs=[mb])
                            em.op("act", lambda e: e.activation(
                                out=Pb[:, pb0:pb0 + kpre + 128],
                                in_=sA[:, 0:kpre + 128], func=AF.Exp, scale=scale,
                                bias=mb, accum_out=z_A),
                                ins=[sA[:], mb],
                                outs=[Pb[:, pb0:pb0 + kpre + 128], z_A])
                        else:
                            em.op("dve", lambda e: e.tensor_reduce(
                                out=m_A, in_=sA[:, 0:512], axis=AX.X, op=ALU.max),
                                ins=[sA[:]], outs=[m_A])
                            em.op("dve", lambda e: e.tensor_reduce(
                                out=md, in_=sA[:, 512:1024], axis=AX.X, op=ALU.max),
                                ins=[sA[:]], outs=[md])
                            em.op("dve", lambda e: e.tensor_scalar(
                                out=mb, in0=m_A, scalar1=md, scalar2=-scale,
                                op0=ALU.max, op1=ALU.mult),
                                ins=[m_A, md], outs=[mb])
                            em.op("dve", lambda e: e.tensor_tensor(
                                out=m_A, in0=m_A, in1=md, op=ALU.max),
                                ins=[m_A, md], outs=[m_A])
                            z_A2 = mr(8)
                            em.op("act", lambda e: e.activation(
                                out=Pb[:, pb0:pb0 + 512], in_=sA[:, 0:512],
                                func=AF.Exp, scale=scale, bias=mb, accum_out=z_A),
                                ins=[sA[:, 0:512], mb],
                                outs=[Pb[:, pb0:pb0 + 512], z_A])
                            em.op("act", lambda e: e.activation(
                                out=Pb[:, pb0 + 512:pb0 + 1024], in_=sA[:, 512:1024],
                                func=AF.Exp, scale=scale, bias=mb, accum_out=z_A2),
                                ins=[sA[:, 512:1024], mb],
                                outs=[Pb[:, pb0 + 512:pb0 + 1024], z_A2])
                            for lo in range(0, b_len, 512):
                                w = min(512, b_len - lo)
                                emit_mm(em, sB[:, lo:lo + w],
                                        [(q_h, k_h(1024 + lo, w))],
                                        tag=f"qkB{qt}.{h}")
                            emit_mm(em, sB[:, b_len:b_len + 128],
                                    [(q_h, k_h(kpre, 128))], tag=f"qkdB{qt}.{h}")
                            em.op("dve", lambda e: e.tensor_tensor(
                                out=sB[:, b_len:b_len + 128],
                                in0=sB[:, b_len:b_len + 128], in1=mdiag,
                                op=ALU.add), ins=[sB[:], mdiag],
                                outs=[sB[:, b_len:b_len + 128]])
                            em.op("dve", lambda e: e.tensor_reduce(
                                out=m_f, in_=sB[:, 0:b_len + 128], axis=AX.X,
                                op=ALU.max), ins=[sB[:]], outs=[m_f])
                            em.op("dve", lambda e: e.tensor_scalar(
                                out=mb, in0=m_f, scalar1=m_A, scalar2=-scale,
                                op0=ALU.max, op1=ALU.mult),
                                ins=[m_f, m_A], outs=[mb])
                            em.op("act", lambda e: e.activation(
                                out=Pb[:, pb0 + 1024:pb0 + 1024 + b_len + 128],
                                in_=sB[:, 0:b_len + 128], func=AF.Exp,
                                scale=scale, bias=mb, accum_out=z_d),
                                ins=[sB[:], mb],
                                outs=[Pb[:, pb0 + 1024:pb0 + 1024 + b_len + 128],
                                      z_d])

                    def tail():
                        if diag_in_A:
                            em.op("dve", lambda e: e.reciprocal(
                                out=zr, in_=z_A), ins=[z_A], outs=[zr])
                            em.op("dve", lambda e: e.tensor_scalar(
                                out=Pb[:, pb0:pb0 + kpre + 128],
                                in0=Pb[:, pb0:pb0 + kpre + 128], scalar1=zr,
                                scalar2=None, op0=ALU.mult),
                                ins=[Pb[:, pb0:pb0 + kpre + 128], zr],
                                outs=[Pb[:, pb0:pb0 + kpre + 128]])
                        else:
                            z_A2 = mr(8)
                            em.op("act", lambda e: e.activation(
                                out=md, in_=m_A, func=AF.Exp, scale=scale,
                                bias=mb), ins=[m_A, mb], outs=[md])
                            em.op("dve", lambda e, z_A2=z_A2: e.tensor_tensor(
                                out=z_A, in0=z_A, in1=z_A2, op=ALU.add),
                                ins=[z_A, z_A2], outs=[z_A])
                            em.op("dve", lambda e: e.scalar_tensor_tensor(
                                out=z_A, in0=z_A, scalar=md, in1=z_d,
                                op0=ALU.mult, op1=ALU.add),
                                ins=[z_A, md, z_d], outs=[z_A])
                            em.op("dve", lambda e: e.reciprocal(out=zr, in_=z_A),
                                  ins=[z_A], outs=[zr])
                            em.op("dve", lambda e: e.tensor_scalar(
                                out=alz, in0=md, scalar1=zr, scalar2=None,
                                op0=ALU.mult), ins=[md, zr], outs=[alz])
                            em.op("dve", lambda e: e.tensor_scalar(
                                out=Pb[:, pb0:pb0 + 1024], in0=Pb[:, pb0:pb0 + 1024],
                                scalar1=alz, scalar2=None, op0=ALU.mult),
                                ins=[Pb[:, pb0:pb0 + 1024], alz],
                                outs=[Pb[:, pb0:pb0 + 1024]])
                            em.op("dve", lambda e: e.tensor_scalar(
                                out=Pb[:, pb0 + 1024:pb0 + kpre + 128],
                                in0=Pb[:, pb0 + 1024:pb0 + kpre + 128], scalar1=zr,
                                scalar2=None, op0=ALU.mult),
                                ins=[Pb[:, pb0 + 1024:pb0 + kpre + 128], zr],
                                outs=[Pb[:, pb0 + 1024:pb0 + kpre + 128]])
                        nblk = qt + 1
                        pt0 = pb0
                        for g in range(0, nblk, 8):
                            nb = min(8, nblk - g)
                            dst3 = PT[:, pt0 + g * 128:pt0 + (g + nb) * 128].rearrange(
                                "p (c j) -> p c j", c=nb)
                            emit_tp8(em, Pb[:, pb0 + g * 128:pb0 + (g + nb) * 128],
                                     ps_tp[(g // 8 + h) % 2], dst3, ident, nblk=nb,
                                     evac="dve")
                        pvp = ps_pr[h % 2]
                        emit_mm(em, pvp[0:64, 0:128],
                                [(v_tm[:, kb * 1024 + h * HD:kb * 1024 + (h + 1) * HD],
                                  PT[:, pt0 + kb * 128:pt0 + (kb + 1) * 128])
                                 for kb in range(nblk)],
                                tag=f"pv{qt}.{h}")
                        dsta = a_col[poff:poff + 64,
                                     c * 512 + qq * 128:c * 512 + (qq + 1) * 128]
                        em.op("act", lambda e: e.activation(
                            out=dsta, in_=pvp[0:64, 0:128], func=AF.Copy),
                            ins=[pvp[:]], outs=[dsta])

                    return front, tail
                apipe.push(*mk())
        apipe.flush()
        for ts4 in range(4):
            tt = t4 * 4 + ts4
            em.dma("sp", gwork[:], dd["g0"][tt * 128:(tt + 1) * 128, :], tag=f"g0{tt}")
            em.dma("sp", xwork[tt % 2], dd["x"][tt * 128:(tt + 1) * 128, :],
                   tag=f"xb{tt}")
            for half in range(2):
                ps = ps_pr[half]
                emit_mm(em, ps[:],
                        [(a_col[:, c * 512 + ts4 * 128:c * 512 + (ts4 + 1) * 128],
                          wo_sb[:, c * D + half * 512:c * D + half * 512 + 512])
                         for c in range(8)], tag=f"o{tt}.{half}")
                em.op("dve", lambda e, ps=ps, half=half: e.tensor_tensor(
                    out=swork[:, half * 512:half * 512 + 512], in0=ps[:],
                    in1=borep[:, half * 512:half * 512 + 512], op=ALU.add),
                    ins=[ps[:], borep[:]],
                    outs=[swork[:, half * 512:half * 512 + 512]])
            em.op("dve", lambda e: e.tensor_tensor(out=gwork[:], in0=swork[:],
                                                   in1=gwork[:], op=ALU.mult),
                  ins=[swork[:], gwork[:]], outs=[gwork[:]])
            em.op("dve", lambda e, tt=tt: e.tensor_tensor(
                out=swork[:], in0=gwork[:], in1=xwork[tt % 2], op=ALU.add),
                ins=[gwork[:], xwork[tt % 2]], outs=[swork[:]])
            em.dma("sp", dd["x1"][tt * 128:(tt + 1) * 128, :], swork[:],
                   tag=f"x1{tt}")


def phase_l1f(nc, em, es, dd, C):
    sb = lambda name, shape, dt=F32: es.enter_context(
        nc.sbuf_tensor("l1f_" + name, shape, dt))
    pst = lambda name, shape, dt=F32: es.enter_context(
        nc.psum_tensor("l1f_" + name, shape, dt))
    wg = sb("wg", [P, 8 * I1], BF16)
    wu = sb("wu", [P, 8 * I1], BF16)
    wd = sb("wd", [P, 24 * D], BF16)
    xn_fm = sb("xnfm", [P, 8 * T], BF16)
    h3fm = sb("h3fm", [P, 24 * 256], BF16)
    sg = sb("sg", [P, 256])
    gtile = sb("gt", [P, D])
    xwork = [sb("xw0", [P, D])[:], sb("xw1", [P, D])[:]]
    x_bf = sb("xbf", [P, D], BF16)
    stats = sb("stats", [P, 36])
    small = sb("small", [P, 16])
    lns = {"stats": stats, "mu": small[:, 0:1], "mu2": small[:, 1:2],
           "var": small[:, 2:3], "sd": small[:, 3:4], "sd2": small[:, 4:5],
           "dump": gtile[:, 0:1024]}
    ps_g = [pst("psg%d" % i, [P, 512]) for i in range(2)]
    ps_u = [pst("psu%d" % i, [P, 512]) for i in range(2)]
    ps_dn = [pst("psdn%d" % i, [P, 512]) for i in range(2)]
    ps_tp = [pst("pstp%d" % i, [P, 1024], BF16) for i in range(2)]
    ident = C["ident"]
    Q = 1.0 / 127.0

    for src, dst, w in ((dd["wg1T"], wg, I1), (dd["wu1T"], wu, I1),
                        (dd["wd1T"], wd, D)):
        nchunk = src.shape[0] // 128
        em.dma("sp", dst[:, 0:nchunk * w].rearrange("p (c n) -> p c n", c=nchunk),
               src.rearrange("(c p) n -> p c n", p=128), tag="wload")

    for tt in range(NT):
        stg = xwork[tt % 2]
        em.dma("sp", stg, dd["x1"][tt * 128:(tt + 1) * 128, :], tag=f"xa{tt}")
        emit_ln(em, stg, x_bf[:], lns, si=tt)
        dst3 = xn_fm[:, :].rearrange("p (c n) -> p c n", c=8)[:, :, tt * 128:(tt + 1) * 128]
        emit_tp8(em, x_bf[:], ps_tp[tt % 2], dst3, ident)

    for t8 in range(8):
        tl = t8 * 256
        for hc in range(24):
            pg, pu = ps_g[hc % 2], ps_u[hc % 2]
            emit_mm(em, pg[:, 0:256],
                    [(wg[:, d * I1 + hc * 128:d * I1 + (hc + 1) * 128],
                      xn_fm[:, d * T + tl:d * T + tl + 256]) for d in range(8)],
                    tag=f"g{t8}.{hc}")
            emit_mm(em, pu[:, 0:256],
                    [(wu[:, d * I1 + hc * 128:d * I1 + (hc + 1) * 128],
                      xn_fm[:, d * T + tl:d * T + tl + 256]) for d in range(8)],
                    tag=f"u{t8}.{hc}")
            em.op("act", lambda e, pg=pg: e.activation(
                out=sg[:], in_=pg[:, 0:256], func=AF.Silu, scale=Q),
                ins=[pg[:]], outs=[sg[:]])
            em.op("dve", lambda e, pu=pu, hc=hc: e.tensor_tensor(
                out=h3fm[:, hc * 256:(hc + 1) * 256], in0=sg[:], in1=pu[:, 0:256],
                op=ALU.mult),
                ins=[sg[:], pu[:]], outs=[h3fm[:, hc * 256:(hc + 1) * 256]])
        for ts in range(2):
            tt = t8 * 2 + ts
            for half in range(2):
                pd = ps_dn[half]
                emit_mm(em, pd[:],
                        [(h3fm[:, j * 256 + ts * 128:j * 256 + (ts + 1) * 128],
                          wd[:, j * D + half * 512:j * D + half * 512 + 512])
                         for j in range(24)],
                        tag=f"dn{tt}.{half}")
                em.op("act", lambda e, pd=pd, half=half: e.activation(
                    out=gtile[:, half * 512:(half + 1) * 512],
                    in_=pd[:], func=AF.Silu, scale=1.0 / 16129.0),
                    ins=[pd[:]],
                    outs=[gtile[:, half * 512:(half + 1) * 512]])
            em.dma("sp", dd["g1"][tt * 128:(tt + 1) * 128, :], gtile[:],
                   tag=f"g1t{tt}")


def phase_l1a(nc, em, es, dd, C):
    sb = lambda name, shape, dt=F32: es.enter_context(
        nc.sbuf_tensor("l1a_" + name, shape, dt))
    pst = lambda name, shape, dt=F32: es.enter_context(
        nc.psum_tensor("l1a_" + name, shape, dt))
    wq_sb = sb("wq", [P, 8 * D], BF16)
    wk_sb = sb("wk", [P, 8 * D], BF16)
    wv_sb = sb("wv", [P, 8 * D], BF16)
    wo_sb = sb("wo", [P, 16 * D], BF16)
    wqf_bf = sb("wqf", [P, 16], BF16)
    wkf_bf = sb("wkf", [P, 16], BF16)
    nwrep = sb("nwrep", [P, D])
    h_fm = sb("hfm", [P, 8 * 512], BF16)
    q_fm = sb("qfm", [P, 8 * 512], BF16)
    k_col = sb("kcol", [P, 8 * 512], BF16)
    k_prev = sb("kprev", [P, 8 * 128], BF16)
    v_prev = sb("vprev", [P, 1024], BF16)
    lin_col = sb("lincol", [P, 8 * 512], BF16)
    win_col = q_fm
    qf_aug = sb("qfaug", [32, 512], BF16)
    kf_aug = sb("kfaug", [32, 512], BF16)
    qphi1 = sb("qphi1", [P, 4 * 512], BF16)
    qphi2 = sb("qphi2", [32, 4 * 512], BF16)
    kphi1 = sb("kphi1", [P, 4 * 512], BF16)
    kphi2 = sb("kphi2", [32, 4 * 512], BF16)
    kphi_tm = sb("kphitm", [P, 16 * 160], BF16)
    tphi = sb("tphi", [P, 1024], BF16)
    v65 = sb("v65", [P, 4 * 16 * 65], BF16)
    kv1 = sb("kv1", [P, 16 * 65])
    kv2 = sb("kv2", [32, 16 * 65])
    kv1b = sb("kv1b", [P, 16 * 65], BF16)
    kv2b = sb("kv2b", [32, 16 * 65], BF16)
    atm4 = sb("atm4", [P, 512], BF16)
    ytmp4 = sb("ytmp4", [P, 512], BF16)
    swin = sb("swin", [P, 4 * 512])
    Pw = sb("pw", [P, 512], BF16)
    PTw = sb("ptw", [P, 512], BF16)
    xwork = [sb("xw0", [P, D])[:], sb("xw1", [P, D])[:]]
    xnw = sb("xnw", [P, D])
    x_bf = sb("xbf", [P, D], BF16)
    gwork = sb("gw", [P, D])
    swork = sb("sw", [P, D])
    stats = sb("stats", [P, 36])
    small = sb("small", [P, 16])
    mrow = sb("mrow", [P, 20])
    lns = {"stats": stats, "mu": small[:, 0:1], "mu2": small[:, 1:2],
           "var": small[:, 2:3], "sd": small[:, 3:4], "sd2": small[:, 4:5],
           "dump": swork[:, 0:1024]}
    ps_m0 = pst("psm0", [P, 512])
    ps_m1 = pst("psm1", [P, 512])
    ps_a = pst("psa", [P, 512])
    ps_y = pst("psy", [P, 512])
    ps_kv = pst("pskv", [P, 512])
    ps_kv2 = pst("pskv2", [P, 512])
    ps_o = pst("pso", [P, 512])
    ps_tp = [pst("pstp0", [P, 1024], BF16)]
    ident, mtri, ones_c = C["ident"], C["mtri"], C["ones_col"]
    mwin = C["mwin"]
    c17 = C["c17"]
    sq1c = C["sq1"]
    sq2c = C["sq2"]
    SL1, SR1 = c17[0:17, 0:128], c17[0:17, 128:256]
    SL2, SR2 = c17[0:17, 256:288], c17[0:17, 288:320]
    SLa, SRa = c17[0:17, 320:480], c17[0:17, 480:640]

    for wt, (src, dst, nch) in enumerate([(dd["wq1"], wq_sb, 8),
                                          (dd["wk1"], wk_sb, 8),
                                          (dd["wv1"], wv_sb, 8),
                                          (dd["wout"], wo_sb, 16)]):
        for d8 in range(nch):
            stg = xwork[d8 % 2]
            em.dma("sp", stg, src[d8 * 128:(d8 + 1) * 128, :], tag=f"w{wt}.{d8}")
            em.op("act", lambda e, stg=stg, dst=dst, d8=d8: e.activation(
                out=dst[:, d8 * D:(d8 + 1) * D], in_=stg, func=AF.Copy),
                ins=[stg], outs=[dst[:, d8 * D:(d8 + 1) * D]])
    em.dma("sp", xwork[0][:, 0:16], dd["wqf"], tag="wqf")
    em.op("act", lambda e: e.activation(out=wqf_bf[:], in_=xwork[0][:, 0:16],
                                        func=AF.Copy),
          ins=[xwork[0]], outs=[wqf_bf[:]])
    em.dma("sp", xwork[1][:, 0:16], dd["wkf"], tag="wkf")
    em.op("act", lambda e: e.activation(out=wkf_bf[:], in_=xwork[1][:, 0:16],
                                        func=AF.Copy),
          ins=[xwork[1]], outs=[wkf_bf[:]])
    em.dma("sp", nwrep[:], dd["normw_rep"])
    for t_, val in ((kv1[:], 0.0), (kv2[:], 0.0), (kv1b[:], 0.0), (kv2b[:], 0.0),
                    (k_prev[:], 0.0), (v_prev[:], 0.0),
                    (qf_aug[:], 1.0), (kf_aug[:], 1.0)):
        em.op("pool", lambda e, t_=t_, val=val: e.memset(t_, val),
              ins=[], outs=[t_])

    for t4 in range(4):
        tl = t4 * 512
        for loc in range(4):
            tt = t4 * 4 + loc
            stg = xwork[tt % 2]
            em.dma("sp", stg, dd["x1"][tt * 128:(tt + 1) * 128, :], tag=f"xa{tt}")
            emit_ln(em, stg, xnw[:], lns, si=tt)
            ms, trs = small[:, 5:6], small[:, 6:7]
            em.op("act", lambda e: e.activation(out=swork[:], in_=xnw[:],
                                                func=AF.Square, accum_out=ms),
                  ins=[xnw[:]], outs=[swork[:], ms])
            em.op("dve", lambda e: e.tensor_scalar(out=trs, in0=ms, scalar1=1.0 / D,
                                                   scalar2=1e-6, op0=ALU.mult,
                                                   op1=ALU.add),
                  ins=[ms], outs=[trs])
            em.op("act", lambda e: e.activation(out=small[:, 7:8], in_=trs,
                                                func=AF.Sqrt),
                  ins=[trs], outs=[small[:, 7:8]])
            em.op("dve", lambda e: e.reciprocal(out=trs, in_=small[:, 7:8]),
                  ins=[small[:, 7:8]], outs=[trs])
            em.op("dve", lambda e: e.scalar_tensor_tensor(
                out=x_bf[:], in0=xnw[:], scalar=trs, in1=nwrep[:],
                op0=ALU.mult, op1=ALU.mult),
                ins=[xnw[:], trs, nwrep[:]], outs=[x_bf[:]])
            dst3 = h_fm[:, :].rearrange("p (c n) -> p c n", c=8)[:, :, loc * 128:(loc + 1) * 128]
            emit_tp8(em, x_bf[:], ps_tp[0], dst3, ident)
        spipe = Pipe()
        em.op("pool", lambda e: e.memset(v65[:], 1.0), ins=[], outs=[v65[:]])
        for c in range(8):
            for which, wsb, out in [(0, wq_sb, q_fm[:, c * 512:(c + 1) * 512]),
                                    (1, wk_sb, k_col[:, c * 512:(c + 1) * 512])]:
                ps = ps_m0 if which == 0 else ps_m1
                emit_mm(em, ps[:],
                        [(wsb[:, d * D + c * 128:d * D + (c + 1) * 128],
                          h_fm[:, d * 512:d * 512 + 512]) for d in range(8)],
                        tag=f"prj{which}.{t4}.{c}")
                eng = "act" if which == 0 else "dve"
                em.op(eng, lambda e, ps=ps, out=out, eng=eng: (
                    e.activation(out=out, in_=ps[:], func=AF.Copy) if eng == "act"
                    else e.tensor_copy(out=out, in_=ps[:])),
                    ins=[ps[:]], outs=[out])
        for ts4 in range(4):
            for half in range(2):
                ps = ps_o
                emit_mm(em, ps[:],
                        [(h_fm[:, d * 512 + ts4 * 128:d * 512 + (ts4 + 1) * 128],
                          wv_sb[:, d * D + half * 512:d * D + half * 512 + 512])
                         for d in range(8)], tag=f"v{t4}.{ts4}.{half}")
                out = v65[:, ts4 * 1040 + half * 8 * 65:
                          ts4 * 1040 + half * 8 * 65 + 8 * 65].rearrange(
                    "p (h w) -> p h w", w=65)[:, :, 0:64]
                em.op("dve", lambda e, ps=ps, out=out: e.tensor_copy(
                    out=out, in_=ps[:].rearrange("p (h w) -> p h w", w=64)),
                    ins=[ps[:]], outs=[v65[:]])
        for g in range(4):
            heads = [4 * g + j for j in range(4)]
            for j, h in enumerate(heads):
                c, poff = h // 2, 64 * (h % 2)
                for src_fm, wf, aug in ((q_fm, wqf_bf, qf_aug), (k_col, wkf_bf, kf_aug)):
                    ps = ps_m0
                    emit_mm(em, ps[0:16, :],
                            [(wf[poff:poff + 64, :],
                              src_fm[poff:poff + 64, c * 512:(c + 1) * 512])],
                            tag=f"qf{t4}.{h}")
                    em.op("act", lambda e, ps=ps, aug=aug: e.activation(
                        out=aug[0:16, :], in_=ps[0:16, :], func=AF.Copy),
                        ins=[ps[:]], outs=[aug[0:16, :]])
                pmL = (ps_m0, ps_a)[j % 2]
                pmR = (ps_m1, ps_y)[j % 2]
                for aug, pall1, pall2, sq in ((qf_aug, qphi1, qphi2, True),
                                              (kf_aug, kphi1, kphi2, False)):
                    p1 = pall1[:, j * 512:(j + 1) * 512]
                    p2 = pall2[0:32, j * 512:(j + 1) * 512]
                    for SL, SR, pp, w, sqap in ((SL1, SR1, p1, 128, sq1c),
                                                (SL2, SR2, p2, 32, sq2c)):
                        tph = tphi[0:w, (j % 2) * 512:(j % 2) * 512 + 512]
                        emit_mm(em, pmL[0:w, :], [(SL[:, 0:w], aug[0:17, :])],
                                tag="phL")
                        emit_mm(em, pmR[0:w, :], [(SR[:, 0:w], aug[0:17, :])],
                                tag="phR")
                        if sq:
                            em.op("dve", lambda e, w=w, sqap=sqap, tph=tph,
                                  pmL=pmL: e.tensor_scalar(
                                out=tph, in0=pmL[0:w, :], scalar1=sqap,
                                scalar2=None, op0=ALU.mult),
                                ins=[pmL[:]], outs=[tph])
                        else:
                            em.op("act", lambda e, w=w, tph=tph, pmL=pmL:
                                  e.activation(
                                out=tph, in_=pmL[0:w, :], func=AF.Copy),
                                ins=[pmL[:]], outs=[tph])
                        em.op("dve", lambda e, pp=pp, w=w, tph=tph, pmR=pmR:
                              e.tensor_tensor(
                            out=pp, in0=tph, in1=pmR[0:w, :], op=ALU.mult),
                            ins=[tph, pmR[:]], outs=[pp])
                for c2 in range(2):
                    for k2 in range(2):
                        cc = c2 * 2 + k2
                        emit_mm(em, pmL[:, k2 * 160:k2 * 160 + 160],
                                [(kf_aug[0:17, cc * 128:(cc + 1) * 128], SLa)],
                                tag="phLtm")
                        emit_mm(em, pmR[:, k2 * 160:k2 * 160 + 160],
                                [(kf_aug[0:17, cc * 128:(cc + 1) * 128], SRa)],
                                tag="phRtm")
                    tph = tphi[:, (j % 2) * 512:(j % 2) * 512 + 320]
                    em.op("act", lambda e, tph=tph, pmL=pmL: e.activation(
                        out=tph, in_=pmL[:, 0:320], func=AF.Copy),
                        ins=[pmL[:]], outs=[tph])
                    ktm2 = kphi_tm[:, (j * 4 + c2 * 2) * 160:
                                   (j * 4 + c2 * 2 + 2) * 160]
                    em.op("dve", lambda e, ktm2=ktm2, tph=tph, pmR=pmR:
                          e.tensor_tensor(
                        out=ktm2, in0=tph, in1=pmR[:, 0:320], op=ALU.mult),
                        ins=[tph, pmR[:]], outs=[ktm2])
            for cc in range(4):
                def mkscan(g=g, cc=cc, heads=tuple(heads)):
                    n_glob = t4 * 4 + cc
                    par = cc % 2
                    csl = lambda pall, j: pall[:, j * 512 + cc * 128:j * 512 + (cc + 1) * 128]
                    csl2 = lambda pall, j: pall[0:32, j * 512 + cc * 128:j * 512 + (cc + 1) * 128]
                    v65s = lambda j: v65[:, cc * 1040 + (4 * g + j) * 65:
                                         cc * 1040 + (4 * g + j) * 65 + 65]
                    ytp = ytmp4[:, par * 256:(par + 1) * 256]

                    def front():
                        for j in range(4):
                            emit_mm(em, ps_a[:, j * 128:(j + 1) * 128],
                                    [(csl(kphi1, j), csl(qphi1, j)),
                                     (csl2(kphi2, j), csl2(qphi2, j))],
                                    tag=f"at{n_glob}.{g}.{j}")
                        em.op("dve", lambda e: e.tensor_tensor(
                            out=atm4[:].rearrange("p (h w) -> p h w", w=128),
                            in0=ps_a[:].rearrange("p (h w) -> p h w", w=128),
                            in1=mtri.broadcast_to([P, 128, 4]).transpose([0, 2, 1]),
                            op=ALU.mult),
                            ins=[ps_a[:], mtri], outs=[atm4[:]])
                        for j, h in enumerate(heads):
                            emit_mm(em, ps_y[:, j * 65:(j + 1) * 65],
                                    [(atm4[:, j * 128:(j + 1) * 128], v65s(j)),
                                     (csl(qphi1, j), kv1b[:, h * 65:h * 65 + 65]),
                                     (csl2(qphi2, j), kv2b[0:32, h * 65:h * 65 + 65])],
                                    tag=f"y{n_glob}.{g}.{j}")
                        ze4 = mrow[:, 0:4]
                        zr4 = mrow[:, 4:8]
                        yv = ps_y[:, 0:260].rearrange("p (h w) -> p h w", w=65)
                        em.op("dve", lambda e: e.tensor_scalar(
                            out=ze4, in0=yv[:, :, 64:65], scalar1=1e-6,
                            scalar2=None, op0=ALU.add), ins=[ps_y[:]], outs=[ze4])
                        em.op("dve", lambda e: e.reciprocal(
                            out=zr4, in_=ze4), ins=[ze4], outs=[zr4])
                        em.op("dve", lambda e: e.tensor_tensor(
                            out=ytp.rearrange("p (h w) -> p h w", w=64),
                            in0=yv[:, :, 0:64],
                            in1=zr4.broadcast_to([P, 4, 64]), op=ALU.mult),
                            ins=[ps_y[:], zr4], outs=[ytp])
                        for j, h in enumerate(heads):
                            ktm = kphi_tm[:, (j * 4 + cc) * 160:(j * 4 + cc) * 160 + 160]
                            emit_mm(em, ps_kv[:, j * 65:(j + 1) * 65],
                                    [(ktm[:, 0:128], v65s(j))],
                                    tag=f"kv1u{n_glob}.{g}.{j}")
                            emit_mm(em, ps_kv2[0:25, j * 65:(j + 1) * 65],
                                    [(ktm[:, 128:153], v65s(j))],
                                    tag=f"kv2u{n_glob}.{g}.{j}")
                        gsl = slice(4 * g * 65, (4 * g + 4) * 65)
                        for dst_f, dst_b, src_ap in (
                                (kv1[:, gsl], kv1b[:, gsl], ps_kv[:, 0:260]),
                                (kv2[0:25, gsl], kv2b[0:25, gsl],
                                 ps_kv2[0:25, 0:260])):
                            em.op("dve", lambda e, dst_f=dst_f, src_ap=src_ap:
                                  e.tensor_tensor(out=dst_f, in0=dst_f,
                                                  in1=src_ap, op=ALU.add),
                                  ins=[dst_f, src_ap], outs=[dst_f])
                            em.op("pool", lambda e, dst_f=dst_f, dst_b=dst_b:
                                  e.tensor_copy(out=dst_b, in_=dst_f),
                                  ins=[dst_f], outs=[dst_b])
                        for pr in range(2):
                            h0 = heads[2 * pr]
                            cpr = h0 // 2
                            sbank = (ps_o, ps_m0)[pr]
                            for j2 in range(2):
                                h = h0 + j2
                                poff = 64 * (h % 2)
                                q_h = q_fm[poff:poff + 64,
                                           cpr * 512 + cc * 128:cpr * 512 + (cc + 1) * 128]
                                kcur = k_col[poff:poff + 64,
                                             cpr * 512 + cc * 128:cpr * 512 + (cc + 1) * 128]
                                so = j2 * 256
                                if n_glob == 0:
                                    em.op("dve", lambda e, sbank=sbank, so=so:
                                          e.memset(sbank[:, so:so + 128], -1e30),
                                          ins=[], outs=[sbank[:]])
                                else:
                                    kprev_ap = (k_prev[poff:poff + 64,
                                                       cpr * 128:(cpr + 1) * 128]
                                                if cc == 0 else
                                                k_col[poff:poff + 64,
                                                      cpr * 512 + (cc - 1) * 128:
                                                      cpr * 512 + cc * 128])
                                    emit_mm(em, sbank[:, so:so + 128],
                                            [(q_h, kprev_ap)],
                                            tag=f"wS0.{n_glob}.{h}")
                                emit_mm(em, sbank[:, so + 128:so + 256],
                                        [(q_h, kcur)], tag=f"wS1.{n_glob}.{h}")
                            em.op("dve", lambda e, sbank=sbank: e.tensor_tensor(
                                out=sbank[:, 0:512].rearrange("p (h w) -> p h w", w=256),
                                in0=sbank[:, 0:512].rearrange("p (h w) -> p h w", w=256),
                                in1=mwin.broadcast_to([P, 256, 2]).transpose([0, 2, 1]),
                                op=ALU.add),
                                ins=[sbank[:], mwin], outs=[sbank[:]])
                            swp = swin[:, (par * 2 + pr) * 512:(par * 2 + pr + 1) * 512]
                            em.op("dve", lambda e, sbank=sbank, swp=swp:
                                  e.tensor_copy(out=swp, in_=sbank[:, 0:512]),
                                  ins=[sbank[:]], outs=[swp])

                    def tail():
                        def tpy(e):
                            last = None
                            for j in range(4):
                                last = e.transpose(
                                    ps_tp[0][64 * (j % 2):64 * (j % 2) + 64,
                                             (j // 2) * 128:(j // 2) * 128 + 128],
                                    in_=ytp[:, j * 64:(j + 1) * 64], identity=ident)
                            return last
                        em.op("pe", tpy, ins=[ytp, ident], outs=[ps_tp[0][:]])
                        for pr in range(2):
                            cpr = heads[2 * pr] // 2
                            dst = lin_col[:, cpr * 512 + cc * 128:cpr * 512 + (cc + 1) * 128]
                            em.op("act", lambda e, pr=pr, dst=dst: e.activation(
                                out=dst, in_=ps_tp[0][:, pr * 128:(pr + 1) * 128],
                                func=AF.Copy),
                                ins=[ps_tp[0][:]], outs=[dst])
                        for pr in range(2):
                            h0 = heads[2 * pr]
                            cpr = h0 // 2
                            swp = swin[:, (par * 2 + pr) * 512:(par * 2 + pr + 1) * 512]
                            wm2 = mrow[:, 8 + pr * 2:8 + pr * 2 + 2]
                            em.op("dve", lambda e, swp=swp, wm2=wm2: e.tensor_reduce(
                                out=wm2, in_=swp.rearrange("p (h w) -> p h w", w=256),
                                axis=AX.X, op=ALU.max), ins=[swp], outs=[wm2])
                            wmb2 = mrow[:, 12 + pr * 2:12 + pr * 2 + 2]
                            em.op("dve", lambda e, wm2=wm2, wmb2=wmb2:
                                  e.tensor_scalar(
                                out=wmb2, in0=wm2, scalar1=-0.125, scalar2=None,
                                op0=ALU.mult), ins=[wm2], outs=[wmb2])
                            Pwp = Pw[:]
                            zw2 = mrow[:, 16 + pr * 2:16 + pr * 2 + 2]
                            for j3 in range(2):
                                em.op("act", lambda e, swp=swp, Pwp=Pwp, j3=j3,
                                      wmb2=wmb2, zw2=zw2: e.activation(
                                    out=Pwp[:, j3 * 256:(j3 + 1) * 256],
                                    in_=swp[:, j3 * 256:(j3 + 1) * 256],
                                    func=AF.Exp, scale=0.125,
                                    bias=wmb2[:, j3:j3 + 1],
                                    accum_out=zw2[:, j3:j3 + 1]),
                                    ins=[swp[:, j3 * 256:(j3 + 1) * 256],
                                         wmb2[:, j3:j3 + 1]],
                                    outs=[Pwp[:, j3 * 256:(j3 + 1) * 256],
                                          zw2[:, j3:j3 + 1]])
                            em.op("dve", lambda e, zw2=zw2: e.reciprocal(
                                out=zw2, in_=zw2), ins=[zw2], outs=[zw2])
                            em.op("pool", lambda e, Pwp=Pwp, zw2=zw2: e.tensor_tensor(
                                out=Pwp.rearrange("p (h w) -> p h w", w=256),
                                in0=Pwp.rearrange("p (h w) -> p h w", w=256),
                                in1=zw2.broadcast_to([P, 2, 256]), op=ALU.mult),
                                ins=[Pwp, zw2], outs=[Pwp])
                            def tpw(e, Pwp=Pwp):
                                last = None
                                for b2 in range(4):
                                    last = e.transpose(
                                        ps_tp[0][:, 256 + b2 * 128:256 + (b2 + 1) * 128],
                                        in_=Pwp[:, b2 * 128:(b2 + 1) * 128],
                                        identity=ident)
                                return last
                            em.op("pe", tpw, ins=[Pwp, ident], outs=[ps_tp[0][:]])
                            PTwp = PTw[:]
                            em.op("act", lambda e, PTwp=PTwp: e.activation(
                                out=PTwp, in_=ps_tp[0][:, 256:768], func=AF.Copy),
                                ins=[ps_tp[0][:]], outs=[PTwp])
                            for j2 in range(2):
                                h = h0 + j2
                                vprev_ap = (v_prev[:, h * HD:(h + 1) * HD]
                                            if cc == 0 else
                                            v65[:, (cc - 1) * 1040 + h * 65:
                                                (cc - 1) * 1040 + h * 65 + HD])
                                vcur_ap = v65[:, cc * 1040 + h * 65:
                                              cc * 1040 + h * 65 + HD]
                                emit_mm(em, ps_m1[64 * j2:64 * j2 + 64,
                                                  pr * 128:(pr + 1) * 128],
                                        [(vprev_ap, PTwp[:, j2 * 256:j2 * 256 + 128]),
                                         (vcur_ap,
                                          PTwp[:, j2 * 256 + 128:j2 * 256 + 256])],
                                        tag=f"wpv{n_glob}.{h}")
                            dstw = win_col[:, cpr * 512 + cc * 128:cpr * 512 + (cc + 1) * 128]
                            em.op("act", lambda e, pr=pr, dstw=dstw: e.activation(
                                out=dstw, in_=ps_m1[:, pr * 128:(pr + 1) * 128],
                                func=AF.Copy),
                                ins=[ps_m1[:]], outs=[dstw])

                    return front, tail
                spipe.push(*mkscan())
        spipe.flush()
        em.op("dve", lambda e: e.tensor_copy(
            out=k_prev[:, :].rearrange("p (c j) -> p c j", c=8),
            in_=k_col[:, :].rearrange("p (c j) -> p c j", j=512)[:, :, 384:512]),
            ins=[k_col[:]], outs=[k_prev[:]])
        em.op("dve", lambda e: e.tensor_copy(
            out=v_prev[:].rearrange("p (h w) -> p h w", w=64),
            in_=v65[:, 3 * 1040:].rearrange("p (h w) -> p h w", w=65)[:, :, 0:64]),
            ins=[v65[:]], outs=[v_prev[:]])
        for ts4 in range(4):
            tt = t4 * 4 + ts4
            em.dma("sp", gwork[:], dd["g1"][tt * 128:(tt + 1) * 128, :],
                   tag=f"g1{tt}")
            em.dma("sp", xwork[ts4 % 2], dd["x1"][tt * 128:(tt + 1) * 128, :],
                   tag=f"xb{tt}")
            em.op("dve", lambda e, ts4=ts4, tt=tt: e.tensor_scalar(
                out=xnw[:], in0=xwork[ts4 % 2], scalar1=stats[:, 2 * tt:2 * tt + 1],
                scalar2=stats[:, 2 * tt + 1:2 * tt + 2], op0=ALU.mult, op1=ALU.add),
                ins=[xwork[ts4 % 2], stats[:]], outs=[xnw[:]])
            for half in range(2):
                emit_mm(em, ps_o[:],
                        [(lin_col[:, c2 * 512 + ts4 * 128:c2 * 512 + (ts4 + 1) * 128],
                          wo_sb[:, c2 * D + half * 512:c2 * D + half * 512 + 512])
                         for c2 in range(8)] +
                        [(win_col[:, c2 * 512 + ts4 * 128:c2 * 512 + (ts4 + 1) * 128],
                          wo_sb[:, (8 + c2) * D + half * 512:(8 + c2) * D + half * 512 + 512])
                         for c2 in range(8)], tag=f"wo{tt}.{half}")
                em.op("dve", lambda e, half=half: e.tensor_tensor(
                    out=swork[:, half * 512:half * 512 + 512], in0=ps_o[:],
                    in1=xnw[:, half * 512:half * 512 + 512], op=ALU.add),
                    ins=[ps_o[:], xnw[:]],
                    outs=[swork[:, half * 512:half * 512 + 512]])
            em.op("dve", lambda e: e.tensor_tensor(out=gwork[:], in0=swork[:],
                                                   in1=gwork[:], op=ALU.mult),
                  ins=[swork[:], gwork[:]], outs=[gwork[:]])
            em.op("dve", lambda e, ts4=ts4: e.tensor_tensor(
                out=swork[:], in0=gwork[:], in1=xwork[ts4 % 2], op=ALU.add),
                ins=[gwork[:], xwork[ts4 % 2]], outs=[swork[:]])
            em.dma("sp", dd["y"][tt * 128:(tt + 1) * 128, :], swork[:],
                   tag=f"y{tt}")


import math

NCORES = 8


def _consts():
    iu, ju = np.triu_indices(16)
    a_map = np.array([16] + list(range(16)) + list(iu))
    b_map = np.array([16] + [16] * 16 + list(ju))
    scale = np.array([1.0] + [1.0] * 16 +
                     [0.5 if i == j else 1.0 / math.sqrt(2.0)
                      for i, j in zip(iu, ju)])
    SL = np.zeros((17, 153), np.float32)
    SR = np.zeros((17, 153), np.float32)
    for f in range(153):
        SL[a_map[f], f] = 1.0
        SR[b_map[f], f] = 1.0
    c17 = np.zeros((32, 640), np.float32)
    c17[0:17, 0:128] = SL[:, 0:128]
    c17[0:17, 128:256] = SR[:, 0:128]
    c17[0:17, 256:281] = SL[:, 128:153]
    c17[0:17, 288:313] = SR[:, 128:153]
    c17[0:17, 320:473] = SL
    c17[0:17, 480:633] = SR
    cbf = np.zeros((128, 384), np.float32)
    cbf[:, 0:128] = np.eye(128)
    cbf[:, 128:256] = (np.arange(128)[:, None] <= np.arange(128)[None, :])
    cbf[:, 256] = 1.0
    ii = np.arange(128)[:, None]
    jj = np.arange(256)[None, :]
    mwin = np.where((jj - ii >= 64) & (jj - ii <= 128), 0.0,
                    -1e30).astype(np.float32)
    mdiag = np.where(np.arange(128)[None, :] <= np.arange(128)[:, None],
                     0.0, -1e30).astype(np.float32)
    sq = (scale * scale).astype(np.float32)
    sq1 = np.zeros((128, 1), np.float32)
    sq1[0:128, 0] = sq[0:128]
    sq2 = np.zeros((128, 1), np.float32)
    sq2[0:25, 0] = sq[128:153]
    qb = np.tile(np.array([[-2.5, -3.5, -5.0]], np.float32), (128, 1))
    return {"c17": c17.astype(BF), "cbf": cbf.astype(BF), "mwin": mwin,
            "mdiag": mdiag, "sq1": sq1, "sq2": sq2, "qbias": qb}


def _build(phase):
    nc = bass.Bass("TRN2", target_bir_lowering=False)
    dm = {}

    def din(name, shape, dt=F32):
        dm[name] = nc.dram_tensor(name, shape, dt, kind="ExternalInput").ap()

    def dout(name, shape, dt=F32):
        dm[name] = nc.dram_tensor(name, shape, dt, kind="ExternalOutput").ap()

    if phase == "l0f":
        din("x", [T, D]); din("wg", [I0, D]); din("wu", [I0, D])
        din("wd", [D, I0]); dout("g0", [T, D])
        din("cst_bf", [128, 384], BF16); din("cst_f32", [128, 3])
    elif phase == "l0a":
        din("x", [T, D])
        for n in ("wqT", "wkT", "wvT", "woT"):
            din(n, [D, D])
        din("bqkv", [128, 24]); din("bvrep", [128, D]); din("borep", [128, D])
        din("g0", [T, D]); dout("x1", [T, D])
        din("cst_bf", [128, 384], BF16); din("cst_f32", [128, 256])
    elif phase == "l1f":
        din("x1", [T, D])
        din("wg1T", [D, I1], BF16); din("wu1T", [D, I1], BF16)
        din("wd1T", [I1, D], BF16)
        dout("g1", [T, D]); din("cst_bf", [128, 384], BF16)
    elif phase == "l1a":
        din("x1", [T, D])
        for n in ("wq1", "wk1", "wv1"):
            din(n, [D, D])
        din("wout", [2 * D, D]); din("wqf", [128, 16]); din("wkf", [128, 16])
        din("normw_rep", [128, D]); din("g1", [T, D]); dout("y", [T, D])
        din("cst_bf", [128, 384], BF16); din("cst17", [32, 640], BF16)
        din("cst_f32", [128, 258])

    with ExitStack() as es:
        em = Em(nc)
        sems = {d: es.enter_context(nc.semaphore(d)) for d in em.domains}
        cb = es.enter_context(nc.sbuf_tensor("cstb_sb", [128, 384], BF16))
        em.dma("sp", cb[:], dm["cst_bf"])
        C = {"ident": cb[:, 0:128], "mtri": cb[:, 128:256],
             "ones_col": cb[:, 256:257]}
        if "cst_f32" in dm:
            w_cf = int(dm["cst_f32"].shape[1])
            cf = es.enter_context(nc.sbuf_tensor("cstf_sb", [128, w_cf], F32))
            em.dma("sp", cf[:], dm["cst_f32"])
            if w_cf >= 128:
                C["mdiag"] = cf[:, 0:128]
                C["mwin"] = cf[:, 0:256]
            if w_cf >= 258:
                C["sq1"] = cf[:, 256:257]
                C["sq2"] = cf[0:32, 257:258]
            if w_cf == 3:
                C["qb"] = cf
        if "cst17" in dm:
            c17sb = es.enter_context(nc.sbuf_tensor("c17_sb", [32, 640], BF16))
            em.dma("sp", c17sb[:], dm["cst17"])
            C["c17"] = c17sb[:]
        {"l0f": phase_l0f, "l0a": phase_l0a,
         "l1f": phase_l1f, "l1a": phase_l1a}[phase](nc, em, es, dm, C)
        em.barrier()
        block = es.enter_context(nc.Block())
        em.finalize(block, sems)
    return nc


def _build_fused():
    nc = bass.Bass("TRN2", target_bir_lowering=False)
    dm = {}

    def din(name, shape, dt=F32):
        dm[name] = nc.dram_tensor(name, shape, dt, kind="ExternalInput").ap()

    def dint(name, shape, dt=F32):
        dm[name] = nc.dram_tensor(name, shape, dt, kind="Internal").ap()

    din("x", [T, D])
    din("wg", [I0, D]); din("wu", [I0, D]); din("wd", [D, I0])
    for n in ("wqT", "wkT", "wvT", "woT"):
        din(n, [D, D])
    din("bqkv", [128, 24]); din("bvrep", [128, D]); din("borep", [128, D])
    din("wg1T", [D, I1], BF16); din("wu1T", [D, I1], BF16)
    din("wd1T", [I1, D], BF16)
    for n in ("wq1", "wk1", "wv1"):
        din(n, [D, D])
    din("wout", [2 * D, D]); din("wqf", [128, 16]); din("wkf", [128, 16])
    din("normw_rep", [128, D])
    din("cst_bf", [128, 384], BF16); din("cst17", [32, 640], BF16)
    din("cst_f32", [128, 389])
    dint("g0", [T, D]); dint("x1", [T, D]); dint("g1", [T, D])
    dm["y"] = nc.dram_tensor("y", [T, D], F32, kind="ExternalOutput").ap()

    with ExitStack() as es:
        em = Em(nc)
        sems = {d: es.enter_context(nc.semaphore(d)) for d in em.domains}
        cb = es.enter_context(nc.sbuf_tensor("cstb_sb", [128, 384], BF16))
        cf = es.enter_context(nc.sbuf_tensor("cstf_sb", [128, 389], F32))
        c17sb = es.enter_context(nc.sbuf_tensor("c17_sb", [32, 640], BF16))
        em.dma("sp", cb[:], dm["cst_bf"])
        em.dma("sp", cf[:], dm["cst_f32"])
        em.dma("sp", c17sb[:], dm["cst17"])
        C = {"ident": cb[:, 0:128], "mtri": cb[:, 128:256],
             "ones_col": cb[:, 256:257], "mdiag": cf[:, 0:128],
             "mwin": cf[:, 128:384], "sq1": cf[:, 384:385],
             "sq2": cf[0:32, 385:386], "qb": cf[:, 386:389], "c17": c17sb[:]}
        d_l1 = dm
        with ExitStack() as es1:
            phase_l0f(nc, em, es1, dm, C)
        em.barrier()
        with ExitStack() as es2:
            phase_l0a(nc, em, es2, dm, C)
        em.barrier()
        with ExitStack() as es3:
            phase_l1f(nc, em, es3, d_l1, C)
        em.barrier()
        with ExitStack() as es4:
            phase_l1a(nc, em, es4, d_l1, C)
        em.barrier()
        block = es.enter_context(nc.Block())
        em.finalize(block, sems)
    return nc


_NC_CACHE = {}


def _get_nc(phase):
    if phase not in _NC_CACHE:
        _NC_CACHE[phase] = _build_fused() if phase == "fused" else _build(phase)
    return _NC_CACHE[phase]


def kernel(L0_attn_x, L0_attn_Wq, L0_attn_bq, L0_attn_Wk, L0_attn_bk,
           L0_attn_Wv, L0_attn_bv, L0_attn_Wo, L0_attn_bo,
           L0_ffn_w_gate, L0_ffn_w_up, L0_ffn_w_down,
           L1_attn_norm_w, L1_attn_Wq, L1_attn_Wk, L1_attn_Wv,
           L1_attn_Wqf, L1_attn_Wkf, L1_attn_Wout,
           L1_ffn_w_gate, L1_ffn_w_up, L1_ffn_w_down):
    f = lambda a: np.ascontiguousarray(np.asarray(a), dtype=np.float32)
    cs = _consts()
    x = f(L0_attn_x)
    com_l0f = {"wg": f(L0_ffn_w_gate), "wu": f(L0_ffn_w_up),
               "wd": f(L0_ffn_w_down), "cst_bf": cs["cbf"]}
    bqkv = np.zeros((128, 24), np.float32)
    for i, b in enumerate((L0_attn_bq, L0_attn_bk, L0_attn_bv)):
        bqkv[:, 8 * i:8 * (i + 1)] = f(b).reshape(8, 128).T
    cf32 = np.zeros((128, 256), np.float32)
    cf32[:, 0:128] = cs["mdiag"]
    com_l0a = {"wqT": f(np.asarray(L0_attn_Wq).T), "wkT": f(np.asarray(L0_attn_Wk).T),
               "wvT": f(np.asarray(L0_attn_Wv).T), "woT": f(np.asarray(L0_attn_Wo).T),
               "bqkv": bqkv, "bvrep": np.tile(f(L0_attn_bv)[None, :], (128, 1)),
               "borep": np.tile(f(L0_attn_bo)[None, :], (128, 1)),
               "cst_bf": cs["cbf"], "cst_f32": cf32}
    com_l1f = {"wg1T": np.ascontiguousarray(np.asarray(L1_ffn_w_gate).T).astype(BF),
               "wu1T": np.ascontiguousarray(np.asarray(L1_ffn_w_up).T).astype(BF),
               "wd1T": np.ascontiguousarray(np.asarray(L1_ffn_w_down).T).astype(BF),
               "cst_bf": cs["cbf"]}
    wqf = f(L1_attn_Wqf)
    wkf = f(L1_attn_Wkf)
    com_l1a = {"wq1": f(L1_attn_Wq), "wk1": f(L1_attn_Wk), "wv1": f(L1_attn_Wv),
               "wout": f(L1_attn_Wout),
               "wqf": np.concatenate([wqf, wqf], 0),
               "wkf": np.concatenate([wkf, wkf], 0),
               "normw_rep": np.tile(f(L1_attn_norm_w)[None, :], (128, 1)),
               "cst_bf": cs["cbf"], "cst17": cs["c17"],
               "cst_f32": np.concatenate(
                   [cs["mwin"], cs["sq1"], cs["sq2"]], 1)}
    cores = list(range(NCORES))
    nc = _get_nc("fused")
    cstf_fused = np.concatenate([cs["mdiag"], cs["mwin"], cs["sq1"], cs["sq2"],
                                 cs["qbias"]], 1)
    com = {}
    for d in (com_l0f, com_l0a, com_l1f, com_l1a):
        com.update(d)
    com["cst_f32"] = cstf_fused
    per_core = [dict(com, x=x[i]) for i in range(NCORES)]
    res = run_bass_kernel_spmd(nc, per_core, core_ids=cores)
    y = np.stack([res.results[i]["y"] for i in range(NCORES)], 0)
    return y.astype(np.float32)


# version 25
# speedup vs baseline: 1.0364x; 1.0364x over previous
import numpy as np
import ml_dtypes
from contextlib import ExitStack

import concourse.bass as bass
import concourse.mybir as mybir
from concourse.bass_utils import run_bass_kernel_spmd

F32 = mybir.dt.float32
BF16 = mybir.dt.bfloat16
AF = mybir.ActivationFunctionType
ALU = mybir.AluOpType
AX = mybir.AxisListType
BF = ml_dtypes.bfloat16

T = 2048
D = 1024
NT = T // 128
I0 = 2816
I1 = 3072
H = 16
HD = 64
FEAT = 16
NF = 153
CH = 128


STREAMS = ("pe", "dve", "act", "pool", "sp")
ENG_DOMS = ("pe", "dve", "act", "pool")
DMA_RING = {"sp": 6, "act": 2, "pool": 2}


class Em:
    def __init__(self, nc):
        self.nc = nc
        self.items = {s: [] for s in STREAMS}
        self.domains = list(ENG_DOMS)
        self.rings = {}
        for s, n in DMA_RING.items():
            self.rings[s] = [f"{s}q{i}" for i in range(n)]
            self.domains += self.rings[s]
        self.dma_idx = {s: 0 for s in DMA_RING}
        self.count = {d: 0 for d in self.domains}
        self.hwm = {(s, d): 0 for s in STREAMS for d in self.domains}
        self.tens = {}

    @staticmethod
    def _ivl(ap):
        lo = int(ap.offset)
        span = 0
        for step, cnt in list(ap.ap):
            span += abs(int(step)) * (int(cnt) - 1)
        hi = lo + span + 1
        if type(ap.tensor).__name__ == "PSumTensorHandle":
            ge = 2048 // mybir.dt.size(ap.dtype)
            return (ap.tensor.name, (lo // ge) * ge,
                    ((hi + ge - 1) // ge) * ge)
        return (ap.tensor.name, lo, hi)

    def _dep_read(self, key, need):
        name, lo, hi = key
        t = self.tens.get(name)
        if not t:
            return
        for (wlo, whi, dom, cnt) in t["w"]:
            if wlo < hi and lo < whi:
                if cnt > need.get(dom, 0):
                    need[dom] = cnt

    def _dep_write(self, key, need):
        name, lo, hi = key
        t = self.tens.get(name)
        if not t:
            return
        for rec in t["w"] + t["r"]:
            (xlo, xhi, dom, cnt) = rec
            if xlo < hi and lo < xhi:
                if cnt > need.get(dom, 0):
                    need[dom] = cnt

    def _record(self, kind, key, dom, cnt):
        name, lo, hi = key
        t = self.tens.setdefault(name, {"w": [], "r": []})
        if kind == "w":
            t["w"] = [r for r in t["w"] if not (lo <= r[0] and r[1] <= hi)]
            t["r"] = [r for r in t["r"] if not (lo <= r[0] and r[1] <= hi)]
            t["w"].append((lo, hi, dom, cnt))
        else:
            t["r"].append((lo, hi, dom, cnt))
            if len(t["r"]) > 400:
                by = {}
                for (xlo, xhi, d2, c2) in t["r"]:
                    cur = by.get(d2)
                    by[d2] = (min(xlo, cur[0]), max(xhi, cur[1]),
                              max(c2, cur[2])) if cur else (xlo, xhi, c2)
                t["r"] = [(l2, h2, d2, c2) for d2, (l2, h2, c2) in by.items()]

    def _wait(self, stream, d, val):
        if val > self.hwm[(stream, d)]:
            self.items[stream].append(("w", d, val))
            self.hwm[(stream, d)] = val

    def op(self, stream, fn, ins=(), outs=(), dom=None, inc=1, tag=""):
        if dom is None:
            dom = stream
        need = {}
        rkeys = [self._ivl(ap) for ap in ins]
        wkeys = [self._ivl(ap) for ap in outs]
        for k in rkeys:
            self._dep_read(k, need)
        for k in wkeys:
            self._dep_write(k, need)
        for d, val in sorted(need.items()):
            self._wait(stream, d, val)
        self.count[dom] += inc
        cnt = self.count[dom]
        self.items[stream].append(("op", fn, dom, inc, tag))
        for k in rkeys:
            self._record("r", k, dom, cnt)
        for k in wkeys:
            self._record("w", k, dom, cnt)

    def dma(self, stream, out, in_, tag=""):
        ring = self.rings[stream]
        dom = ring[self.dma_idx[stream] % len(ring)]
        self.dma_idx[stream] += 1
        self._wait(stream, dom, self.count[dom])
        self.op(stream, lambda e: e.dma_start(out=out, in_=in_),
                ins=[in_], outs=[out], dom=dom, inc=16, tag=tag)

    def barrier(self):
        for s in STREAMS:
            for d in self.domains:
                self._wait(s, d, self.count[d])
        self.tens = {}

    def finalize(self, block, sems):
        eng_attr = {"pe": "tensor", "dve": "vector", "act": "scalar",
                    "pool": "gpsimd", "sp": "sync"}

        def mk(stream):
            def body(e):
                for it in self.items[stream]:
                    if it[0] == "w":
                        e.wait_ge(sems[it[1]], it[2])
                    else:
                        _, fn, dom, inc, tag = it
                        instr = fn(e)
                        instr.then_inc(sems[dom], inc)
            return body

        for stream in STREAMS:
            getattr(block, eng_attr[stream])(mk(stream))


class Pipe:

    def __init__(self, depth=1):
        self.q = []
        self.depth = depth

    def push(self, front, tail):
        front()
        self.q.append(tail)
        if len(self.q) > self.depth:
            self.q.pop(0)()

    def flush(self):
        for t in self.q:
            t()
        self.q = []



MID = [0.25, 0.75, 1.25, 1.75, 2.5, 3.5, 5.0]
WJ = [0.5, 0.5, 0.5, 0.5, 1.0, 1.0, 2.0]
P = 128


def emit_ln(em, x_ap, xn_ap, s, si=0, dump=None):
    st = s["stats"]
    ssq = st[:, 32:33]
    sm = st[:, 33:34]
    rstd = st[:, 2 * si:2 * si + 1]
    mneg = st[:, 2 * si + 1:2 * si + 2]
    mu, mu2, var, sd = s["mu"], s["mu2"], s["var"], s["sd"]
    if dump is None:
        dump = s["dump"]
    em.op("act", lambda e: e.activation(out=dump, in_=x_ap, func=AF.Square,
                                        accum_out=ssq),
          ins=[x_ap], outs=[dump, ssq])
    em.op("dve", lambda e: e.tensor_reduce(out=sm, in_=x_ap, axis=AX.X, op=ALU.add),
          ins=[x_ap], outs=[sm])
    em.op("dve", lambda e: e.tensor_scalar(out=mu, in0=sm, scalar1=1.0 / D,
                                           scalar2=None, op0=ALU.mult),
          ins=[sm], outs=[mu])
    em.op("dve", lambda e: e.tensor_scalar(out=mu2, in0=mu, scalar1=mu,
                                           scalar2=None, op0=ALU.mult),
          ins=[mu], outs=[mu2])
    em.op("dve", lambda e: e.scalar_tensor_tensor(out=var, in0=ssq, scalar=1.0 / D,
                                                  in1=mu2, op0=ALU.mult,
                                                  op1=ALU.subtract),
          ins=[ssq, mu2], outs=[var])
    em.op("dve", lambda e: e.tensor_scalar(out=sd, in0=var, scalar1=1e-5,
                                           scalar2=None, op0=ALU.add),
          ins=[var], outs=[sd])
    em.op("act", lambda e: e.activation(out=s["sd2"], in_=sd, func=AF.Sqrt),
          ins=[sd], outs=[s["sd2"]])
    em.op("dve", lambda e: e.reciprocal(out=rstd, in_=s["sd2"]),
          ins=[s["sd2"]], outs=[rstd])
    em.op("dve", lambda e: e.tensor_scalar(out=mneg, in0=mu, scalar1=rstd,
                                           scalar2=-1.0, op0=ALU.mult, op1=ALU.mult),
          ins=[mu, rstd], outs=[mneg])
    em.op("dve", lambda e: e.tensor_scalar(out=xn_ap, in0=x_ap, scalar1=rstd,
                                           scalar2=mneg, op0=ALU.mult, op1=ALU.add),
          ins=[x_ap, rstd, mneg], outs=[xn_ap])


QPAR = [0]


def emit_quant(em, src, dst, W, q):
    off = 0
    while off < W:
        w = min(512, W - off)
        _quant_slice(em, src[:, off:off + w], dst[:, off:off + w], w, q,
                     QPAR[0] % 2)
        QPAR[0] += 1
        off += w


def _quant_slice(em, src, dst, W, q, par):
    nb = W // 16
    o = par * 512
    ob = par * 32
    amax = q["amax"][:, ob:ob + nb]
    scaleb = q["scaleb"][:, ob:ob + nb]
    r = q["r"][:, ob:ob + nb]
    z = q["z"][:, o:o + W]
    y = q["y"][:, o:o + W]
    b0, b1, b2, b3, b4 = (q[k][:, o:o + W] for k in ("b0", "b1", "b2", "b3", "b4"))
    src3 = src.rearrange("p (f b) -> p f b", b=16)
    z3 = z.rearrange("p (f b) -> p f b", b=16)
    em.op("dve", lambda e: e.tensor_reduce(out=amax, in_=src3, axis=AX.X,
                                           op=ALU.max, apply_absolute_value=True),
          ins=[src], outs=[amax])
    em.op("dve", lambda e: e.tensor_scalar(out=scaleb, in0=amax, scalar1=6e-12,
                                           scalar2=None, op0=ALU.max),
          ins=[amax], outs=[scaleb])
    em.op("dve", lambda e: e.reciprocal(out=r, in_=scaleb), ins=[scaleb], outs=[r])
    em.op("dve", lambda e: e.tensor_scalar(out=r, in0=r, scalar1=6.0,
                                           scalar2=None, op0=ALU.mult),
          ins=[r], outs=[r])
    em.op("dve", lambda e: e.reciprocal(out=scaleb, in_=r), ins=[r], outs=[scaleb])
    em.op("dve", lambda e: e.tensor_tensor(out=z3, in0=src3,
                                           in1=r.broadcast_to([P, nb, 16]),
                                           op=ALU.mult),
          ins=[src, r], outs=[z])
    em.op("act", lambda e: e.activation(out=y, in_=z, func=AF.Abs),
          ins=[z], outs=[y])
    em.op("act", lambda e: e.activation(out=b4, in_=z, func=AF.Sign),
          ins=[z], outs=[b4])

    def cmp(dstb, j, eng="dve"):
        em.op(eng, lambda e: e.tensor_scalar(out=dstb, in0=y, scalar1=MID[j],
                                             scalar2=WJ[j], op0=ALU.is_gt,
                                             op1=ALU.mult),
              ins=[y], outs=[dstb])

    def add(dstb, a, bb, eng="dve"):
        em.op(eng, lambda e: e.tensor_tensor(out=dstb, in0=a, in1=bb, op=ALU.add),
              ins=[a, bb], outs=[dstb])

    cmp(b0, 0)
    cmp(b1, 1)
    add(b2, b0, b1)
    cmp(b0, 2)
    cmp(b1, 3)
    add(b3, b0, b1)
    cmp(b0, 4)
    cmp(b1, 5)
    add(b2, b2, b3)
    add(b3, b0, b1)
    cmp(b0, 6)
    add(b1, b3, b0)
    add(b0, b2, b1)
    em.op("dve", lambda e: e.tensor_tensor(
        out=b1.rearrange("p (f b) -> p f b", b=16),
        in0=b0.rearrange("p (f b) -> p f b", b=16),
        in1=scaleb.broadcast_to([P, nb, 16]), op=ALU.mult),
        ins=[b0, scaleb], outs=[b1])
    em.op("dve", lambda e: e.tensor_tensor(out=dst, in0=b1, in1=b4, op=ALU.mult),
          ins=[b1, b4], outs=[dst])


def emit_tp8(em, src_bf, ps_tp, dst3, ident, nblk=8, evac="act"):
    def tp(e):
        last = None
        for j in range(nblk):
            last = e.transpose(ps_tp[:, j * 128:(j + 1) * 128],
                               in_=src_bf[:, j * 128:(j + 1) * 128],
                               identity=ident)
        return last
    em.op("pe", tp, ins=[src_bf, ident], outs=[ps_tp[:]])
    src3 = ps_tp[:, 0:nblk * 128].rearrange("p (c j) -> p c j", c=nblk)
    em.op(evac, lambda e: e.tensor_copy(out=dst3, in_=src3)
          if evac != "act" else e.activation(out=dst3, in_=src3, func=AF.Copy),
          ins=[ps_tp[:]], outs=[dst3])


def emit_mm(em, ps_ap, pairs, start=True, stop=True, tag=""):
    n = len(pairs)
    def fn(e):
        last = None
        for i, (l, r) in enumerate(pairs):
            last = e.matmul(ps_ap, lhsT=l, rhs=r,
                            start=(start and i == 0),
                            stop=(stop and i == n - 1))
        return last
    ins = [ap for pr in pairs for ap in pr]
    em.op("pe", fn, ins=ins, outs=[ps_ap], tag=tag)


def phase_l0f(nc, em, es, dd, C):
    sb = lambda name, shape, dt=F32: es.enter_context(
        nc.sbuf_tensor("l0f_" + name, shape, dt))
    pst = lambda name, shape, dt=F32: es.enter_context(
        nc.psum_tensor("l0f_" + name, shape, dt))
    wgq = sb("wgq", [P, 8 * I0], BF16)
    wuq = sb("wuq", [P, 8 * I0], BF16)
    wdq = sb("wdq", [P, 22 * D], BF16)
    xq_fm = sb("xqfm", [P, 8 * T], BF16)
    scrf = sb("scrf", [P, 3072])
    scrb = sb("scrb", [P, 2048], BF16)
    qs = {"z": sb("qz", [P, 1024]), "y": sb("qy", [P, 1024]),
          "amax": sb("qam", [P, 64]), "scaleb": sb("qsc", [P, 64]),
          "r": sb("qr", [P, 64])}
    for k in ("b0", "b1", "b2", "b3", "b4"):
        qs[k] = sb("q" + k, [P, 1024], BF16)
    stats = sb("stats", [P, 36])
    small = sb("small", [P, 8])
    lns = {"stats": stats, "mu": small[:, 0:1], "mu2": small[:, 1:2],
           "var": small[:, 2:3], "sd": small[:, 3:4], "sd2": small[:, 4:5]}
    ps_ab = [pst("psab%d" % i, [P, 512]) for i in range(4)]
    ps_dn = [pst("psdn%d" % i, [P, 512]) for i in range(2)]
    ps_tps = [pst("pstp%d" % i, [P, 1024], BF16) for i in range(2)]
    tppar = [0]
    def ps_tp_next():
        tppar[0] += 1
        return ps_tps[tppar[0] % 2]
    ident = C["ident"]

    s01 = [scrf[:, 0:1024], scrf[:, 1024:2048]]
    wq_tms = [scrb[:, 0:1024], scrb[:, 1024:2048]]

    def wtiles(wsrc, nrow):
        for k in range(nrow):
            yield k, wsrc[k * 128:(k + 1) * 128, :]

    for wi, (wsrc, dst, dcol) in enumerate([(dd["wg"], wgq, I0), (dd["wu"], wuq, I0)]):
        for k in range(22):
            stg = s01[k % 2]
            wq_tm = wq_tms[k % 2]
            em.dma("sp", stg, wsrc[k * 128:(k + 1) * 128, :], tag=f"w{wi}t{k}")
            emit_quant(em, stg, wq_tm, 1024, qs)
            dst3 = dst[:, :].rearrange("p (c n) -> p c n", c=8)[:, :, k * 128:(k + 1) * 128]
            emit_tp8(em, wq_tm, ps_tp_next(), dst3, ident)
    for r in range(8):
        for s, W in ((0, 1024), (1, 1024), (2, 768)):
            stg = s01[(r * 3 + s) % 2]
            wq_tm = wq_tms[(r * 3 + s) % 2]
            em.dma("sp", stg[:, 0:W],
                   dd["wd"][r * 128:(r + 1) * 128, s * 1024:s * 1024 + W],
                   tag=f"wdt{r}.{s}")
            emit_quant(em, stg[:, 0:W], wq_tm[:, 0:W], W, qs)
            nb = W // 128
            dst3 = wdq[:, :].rearrange("p (k n) -> p k n", k=22)[
                :, s * 8:s * 8 + nb, r * 128:(r + 1) * 128]
            emit_tp8(em, wq_tm, ps_tp_next(), dst3, ident, nblk=nb)

    xn = scrf[:, 2048:3072]
    for tt in range(NT):
        stg = s01[tt % 2]
        xq_tm = wq_tms[tt % 2]
        em.dma("sp", stg, dd["x"][tt * 128:(tt + 1) * 128, :], tag=f"xt{tt}")
        emit_ln(em, stg, xn, lns, si=tt, dump=xn)
        emit_quant(em, xn, xq_tm, 1024, qs)
        dst3 = xq_fm[:, :].rearrange("p (c n) -> p c n", c=8)[:, :, tt * 128:(tt + 1) * 128]
        emit_tp8(em, xq_tm, ps_tp_next(), dst3, ident)

    h3fs = [scrf[:, 0:512], scrf[:, 2048:2560]]
    sgs = [scrf[:, 512:1024], scrf[:, 2560:3072]]
    gtile = scrf[:, 1024:2048]
    h3qs = [scrb[:, 0:512], scrb[:, 1024:1536]]
    h3qfms = [scrb[:, 512:1024], scrb[:, 1536:2048]]
    HGS = [(hg, 512 if hg < 5 else 256) for hg in range(6)]
    for tt in range(NT):
        xcol = lambda c: xq_fm[:, c * T + tt * 128: c * T + (tt + 1) * 128]
        for hg, W in HGS:
            pa = ps_ab[(hg % 2) * 2]
            pb = ps_ab[(hg % 2) * 2 + 1]
            sg = sgs[hg % 2]
            h3f = h3fs[hg % 2]
            h3q = h3qs[hg % 2]
            h3qfm = h3qfms[hg % 2]
            emit_mm(em, pa[:, 0:W],
                    [(xcol(c), wgq[:, c * I0 + hg * 512: c * I0 + hg * 512 + W])
                     for c in range(8)], tag=f"gate{tt}.{hg}")
            emit_mm(em, pb[:, 0:W],
                    [(xcol(c), wuq[:, c * I0 + hg * 512: c * I0 + hg * 512 + W])
                     for c in range(8)], tag=f"up{tt}.{hg}")
            em.op("act", lambda e, pa=pa, W=W, sg=sg: e.activation(
                out=sg[:, 0:W], in_=pa[:, 0:W], func=AF.Silu),
                ins=[pa[:]], outs=[sg[:, 0:W]])
            em.op("dve", lambda e, pb=pb, W=W, sg=sg, h3f=h3f: e.tensor_tensor(
                out=h3f[:, 0:W], in0=sg[:, 0:W], in1=pb[:, 0:W], op=ALU.mult),
                ins=[sg[:, 0:W], pb[:]], outs=[h3f[:, 0:W]])
            emit_quant(em, h3f[:, 0:W], h3q[:, 0:W], W, qs)
            nb = W // 128
            dst3 = h3qfm[:, 0:W].rearrange("p (c j) -> p c j", c=nb)
            emit_tp8(em, h3q, ps_tp_next(), dst3, ident, nblk=nb)
            for half in range(2):
                emit_mm(em, ps_dn[half][:],
                        [(h3qfm[:, j * 128:(j + 1) * 128],
                          wdq[:, (hg * 4 + j) * D + half * 512:
                              (hg * 4 + j) * D + half * 512 + 512])
                         for j in range(nb)],
                        start=(hg == 0), stop=(hg == 5), tag=f"dn{tt}.{hg}.{half}")
        for half in range(2):
            em.op("act", lambda e, half=half: e.activation(
                out=gtile[:, half * 512:(half + 1) * 512], in_=ps_dn[half][:],
                func=AF.Silu),
                ins=[ps_dn[half][:]], outs=[gtile[:, half * 512:(half + 1) * 512]])
        em.dma("sp", dd["g0"][tt * 128:(tt + 1) * 128, :], gtile, tag=f"g0t{tt}")


def phase_l0a(nc, em, es, dd, C):
    sb = lambda name, shape, dt=F32: es.enter_context(
        nc.sbuf_tensor("l0a_" + name, shape, dt))
    pst = lambda name, shape, dt=F32: es.enter_context(
        nc.psum_tensor("l0a_" + name, shape, dt))
    wq_sb = sb("wq", [P, 8 * D], BF16)
    wk_sb = sb("wk", [P, 8 * D], BF16)
    wv_sb = sb("wv", [P, 8 * D], BF16)
    wo_sb = sb("wo", [P, 8 * D], BF16)
    bqkv = sb("bqkv", [P, 24])
    bvrep = sb("bvrep", [P, D])
    borep = sb("borep", [P, D])
    xn_fm = sb("xnfm", [P, 8 * 512], BF16)
    k_fm = sb("kfm", [P, 8 * T], BF16)
    v_tm = sb("vtm", [P, 16 * 1024], BF16)
    q_col = sb("qcol", [P, 8 * 512], BF16)
    a_col = q_col
    Pb = sb("p", [P, 6144], BF16)
    PT = sb("pt", [P, 6144], BF16)
    sdm = sb("sdm", [P, 384])
    xwork = [sb("xw0", [P, D])[:], sb("xw1", [P, D])[:]]
    x_bf = sb("xbf", [P, D], BF16)
    gwork = sb("gw", [P, D])
    swork = sb("sw", [P, D])
    stats = sb("stats", [P, 36])
    small = sb("small", [P, 8])
    lns = {"stats": stats, "mu": small[:, 0:1], "mu2": small[:, 1:2],
           "var": small[:, 2:3], "sd": small[:, 3:4], "sd2": small[:, 4:5],
           "dump": swork[:, 0:1024]}
    mrow = sb("mrow", [P, 36])
    ps_s0 = pst("pss0", [P, 1024])
    ps_s1 = pst("pss1", [P, 1024])
    ps_pr = [pst("pspr%d" % i, [P, 512]) for i in range(2)]
    ps_tp = [pst("pstp%d" % i, [P, 1024], BF16) for i in range(2)]
    ident = C["ident"]
    mdiag = C["mdiag"]

    for wt, (src, dst) in enumerate([(dd["wqT"], wq_sb), (dd["wkT"], wk_sb),
                                     (dd["wvT"], wv_sb), (dd["woT"], wo_sb)]):
        for d8 in range(8):
            stg = xwork[d8 % 2]
            em.dma("sp", stg, src[d8 * 128:(d8 + 1) * 128, :], tag=f"w{wt}.{d8}")
            em.op("act", lambda e, stg=stg, dst=dst, d8=d8: e.activation(
                out=dst[:, d8 * D:(d8 + 1) * D], in_=stg, func=AF.Copy),
                ins=[stg], outs=[dst[:, d8 * D:(d8 + 1) * D]])
    em.dma("sp", bqkv[:], dd["bqkv"])
    em.dma("sp", bvrep[:], dd["bvrep"])
    em.dma("sp", borep[:], dd["borep"])

    scale = 1.0 / 8.0
    for t4 in range(4):
        tl = t4 * 512
        for loc in range(4):
            tt = t4 * 4 + loc
            stg = xwork[tt % 2]
            em.dma("sp", stg, dd["x"][tt * 128:(tt + 1) * 128, :], tag=f"xa{tt}")
            emit_ln(em, stg, x_bf[:], lns, si=tt)
            dst3 = xn_fm[:, :].rearrange("p (c n) -> p c n", c=8)[:, :, loc * 128:(loc + 1) * 128]
            emit_tp8(em, x_bf[:], ps_tp[tt % 2], dst3, ident)
        for c in range(8):
            for which, wsb, dstcol in [(0, wq_sb, q_col), (1, wk_sb, k_fm)]:
                ps = ps_pr[c % 2]
                emit_mm(em, ps[:],
                        [(wsb[:, d * D + c * 128:d * D + (c + 1) * 128],
                          xn_fm[:, d * 512:d * 512 + 512]) for d in range(8)],
                        tag=f"prj{which}.{t4}.{c}")
                if which == 0:
                    out = dstcol[:, c * 512:(c + 1) * 512]
                else:
                    out = dstcol[:, c * T + tl:c * T + tl + 512]
                em.op("dve", lambda e, ps=ps, out=out, c=c, which=which:
                      e.tensor_scalar(out=out, in0=ps[:],
                                      scalar1=bqkv[:, 8 * which + c:8 * which + c + 1],
                                      scalar2=None, op0=ALU.add),
                      ins=[ps[:], bqkv[:]], outs=[out])
        for ts4 in range(4):
            tt = t4 * 4 + ts4
            for half in range(2):
                ps = ps_pr[half]
                emit_mm(em, ps[:],
                        [(xn_fm[:, d * 512 + ts4 * 128:d * 512 + (ts4 + 1) * 128],
                          wv_sb[:, d * D + half * 512:d * D + half * 512 + 512])
                         for d in range(8)], tag=f"v{tt}.{half}")
                out = v_tm[:, tt * 1024 + half * 512:tt * 1024 + half * 512 + 512]
                em.op("dve", lambda e, ps=ps, out=out, half=half:
                      e.tensor_tensor(out=out, in0=ps[:],
                                      in1=bvrep[:, half * 512:half * 512 + 512],
                                      op=ALU.add),
                      ins=[ps[:], bvrep[:]], outs=[out])
        apipe = [None]
        for qq in range(4):
            qt = t4 * 4 + qq
            kpre = qt * 128
            for h in range(H):
                def mk(h=h, qt=qt, qq=qq, kpre=kpre):
                    c, poff = h // 2, 64 * (h % 2)
                    par = h % 2
                    par3 = h % 2
                    q_h = q_col[poff:poff + 64,
                                c * 512 + qq * 128:c * 512 + (qq + 1) * 128]
                    kbase = c * T
                    k_h = lambda lo, w: k_fm[poff:poff + 64, kbase + lo:kbase + lo + w]
                    sdmh = sdm[:, par3 * 128:(par3 + 1) * 128]
                    mr = lambda j: mrow[:, par3 * 12 + j:par3 * 12 + j + 1]
                    m_A, md, m_f, mb, z_A, z_d, zr, alz = (mr(j) for j in range(8))
                    sA = (ps_s0, ps_s1)[par] if qt < 8 else ps_s0
                    sB = ps_s1
                    pb0 = par3 * 2048
                    a_end = min(kpre, 1024)
                    b_len = kpre - a_end
                    diag_in_A = qt < 8
                    st = {}

                    def front():
                        for lo in range(0, a_end, 512):
                            w = min(512, a_end - lo)
                            emit_mm(em, sA[:, lo:lo + w], [(q_h, k_h(lo, w))],
                                    tag=f"qkA{qt}.{h}")
                        if diag_in_A:
                            emit_mm(em, sA[:, kpre:kpre + 128],
                                    [(q_h, k_h(kpre, 128))], tag=f"qkdA{qt}.{h}")
                            em.op("dve", lambda e: e.tensor_tensor(
                                out=sA[:, kpre:kpre + 128],
                                in0=sA[:, kpre:kpre + 128], in1=mdiag,
                                op=ALU.add), ins=[sA[:], mdiag],
                                outs=[sA[:, kpre:kpre + 128]])
                            em.op("dve", lambda e: e.tensor_reduce(
                                out=m_A, in_=sA[:, 0:kpre + 128], axis=AX.X,
                                op=ALU.max), ins=[sA[:]], outs=[m_A])
                            em.op("dve", lambda e: e.tensor_scalar(
                                out=mb, in0=m_A, scalar1=-scale, scalar2=None,
                                op0=ALU.mult), ins=[m_A], outs=[mb])
                            em.op("act", lambda e: e.activation(
                                out=Pb[:, pb0:pb0 + kpre + 128],
                                in_=sA[:, 0:kpre + 128], func=AF.Exp, scale=scale,
                                bias=mb, accum_out=z_A),
                                ins=[sA[:], mb],
                                outs=[Pb[:, pb0:pb0 + kpre + 128], z_A])
                        else:
                            em.op("dve", lambda e: e.tensor_reduce(
                                out=m_A, in_=sA[:, 0:512], axis=AX.X, op=ALU.max),
                                ins=[sA[:]], outs=[m_A])
                            em.op("dve", lambda e: e.tensor_reduce(
                                out=md, in_=sA[:, 512:1024], axis=AX.X, op=ALU.max),
                                ins=[sA[:]], outs=[md])
                            em.op("dve", lambda e: e.tensor_scalar(
                                out=mb, in0=m_A, scalar1=md, scalar2=-scale,
                                op0=ALU.max, op1=ALU.mult),
                                ins=[m_A, md], outs=[mb])
                            em.op("dve", lambda e: e.tensor_tensor(
                                out=m_A, in0=m_A, in1=md, op=ALU.max),
                                ins=[m_A, md], outs=[m_A])
                            z_A2 = mr(8)
                            em.op("act", lambda e: e.activation(
                                out=Pb[:, pb0:pb0 + 512], in_=sA[:, 0:512],
                                func=AF.Exp, scale=scale, bias=mb, accum_out=z_A),
                                ins=[sA[:, 0:512], mb],
                                outs=[Pb[:, pb0:pb0 + 512], z_A])
                            em.op("act", lambda e: e.activation(
                                out=Pb[:, pb0 + 512:pb0 + 1024], in_=sA[:, 512:1024],
                                func=AF.Exp, scale=scale, bias=mb, accum_out=z_A2),
                                ins=[sA[:, 512:1024], mb],
                                outs=[Pb[:, pb0 + 512:pb0 + 1024], z_A2])
                            for lo in range(0, b_len, 512):
                                w = min(512, b_len - lo)
                                emit_mm(em, sB[:, lo:lo + w],
                                        [(q_h, k_h(1024 + lo, w))],
                                        tag=f"qkB{qt}.{h}")
                            emit_mm(em, sB[:, b_len:b_len + 128],
                                    [(q_h, k_h(kpre, 128))], tag=f"qkdB{qt}.{h}")
                            em.op("dve", lambda e: e.tensor_tensor(
                                out=sB[:, b_len:b_len + 128],
                                in0=sB[:, b_len:b_len + 128], in1=mdiag,
                                op=ALU.add), ins=[sB[:], mdiag],
                                outs=[sB[:, b_len:b_len + 128]])
                            em.op("dve", lambda e: e.tensor_reduce(
                                out=m_f, in_=sB[:, 0:b_len + 128], axis=AX.X,
                                op=ALU.max), ins=[sB[:]], outs=[m_f])
                            em.op("dve", lambda e: e.tensor_scalar(
                                out=mb, in0=m_f, scalar1=m_A, scalar2=-scale,
                                op0=ALU.max, op1=ALU.mult),
                                ins=[m_f, m_A], outs=[mb])
                            em.op("act", lambda e: e.activation(
                                out=Pb[:, pb0 + 1024:pb0 + 1024 + b_len + 128],
                                in_=sB[:, 0:b_len + 128], func=AF.Exp,
                                scale=scale, bias=mb, accum_out=z_d),
                                ins=[sB[:], mb],
                                outs=[Pb[:, pb0 + 1024:pb0 + 1024 + b_len + 128],
                                      z_d])

                    def tail_dve():
                        if diag_in_A:
                            em.op("dve", lambda e: e.reciprocal(
                                out=zr, in_=z_A), ins=[z_A], outs=[zr])
                            em.op("dve", lambda e: e.tensor_scalar(
                                out=Pb[:, pb0:pb0 + kpre + 128],
                                in0=Pb[:, pb0:pb0 + kpre + 128], scalar1=zr,
                                scalar2=None, op0=ALU.mult),
                                ins=[Pb[:, pb0:pb0 + kpre + 128], zr],
                                outs=[Pb[:, pb0:pb0 + kpre + 128]])
                        else:
                            z_A2 = mr(8)
                            em.op("act", lambda e: e.activation(
                                out=md, in_=m_A, func=AF.Exp, scale=scale,
                                bias=mb), ins=[m_A, mb], outs=[md])
                            em.op("dve", lambda e, z_A2=z_A2: e.tensor_tensor(
                                out=z_A, in0=z_A, in1=z_A2, op=ALU.add),
                                ins=[z_A, z_A2], outs=[z_A])
                            em.op("dve", lambda e: e.scalar_tensor_tensor(
                                out=z_A, in0=z_A, scalar=md, in1=z_d,
                                op0=ALU.mult, op1=ALU.add),
                                ins=[z_A, md, z_d], outs=[z_A])
                            em.op("dve", lambda e: e.reciprocal(out=zr, in_=z_A),
                                  ins=[z_A], outs=[zr])
                            em.op("dve", lambda e: e.tensor_scalar(
                                out=alz, in0=md, scalar1=zr, scalar2=None,
                                op0=ALU.mult), ins=[md, zr], outs=[alz])
                            em.op("dve", lambda e: e.tensor_scalar(
                                out=Pb[:, pb0:pb0 + 1024], in0=Pb[:, pb0:pb0 + 1024],
                                scalar1=alz, scalar2=None, op0=ALU.mult),
                                ins=[Pb[:, pb0:pb0 + 1024], alz],
                                outs=[Pb[:, pb0:pb0 + 1024]])
                            em.op("dve", lambda e: e.tensor_scalar(
                                out=Pb[:, pb0 + 1024:pb0 + kpre + 128],
                                in0=Pb[:, pb0 + 1024:pb0 + kpre + 128], scalar1=zr,
                                scalar2=None, op0=ALU.mult),
                                ins=[Pb[:, pb0 + 1024:pb0 + kpre + 128], zr],
                                outs=[Pb[:, pb0 + 1024:pb0 + kpre + 128]])

                    def tail_pe():
                        nblk = qt + 1
                        pt0 = pb0
                        for g in range(0, nblk, 8):
                            nb = min(8, nblk - g)
                            dst3 = PT[:, pt0 + g * 128:pt0 + (g + nb) * 128].rearrange(
                                "p (c j) -> p c j", c=nb)
                            emit_tp8(em, Pb[:, pb0 + g * 128:pb0 + (g + nb) * 128],
                                     ps_tp[(g // 8 + h) % 2], dst3, ident, nblk=nb,
                                     evac="dve")
                        pvp = ps_pr[h % 2]
                        emit_mm(em, pvp[0:64, 0:128],
                                [(v_tm[:, kb * 1024 + h * HD:kb * 1024 + (h + 1) * HD],
                                  PT[:, pt0 + kb * 128:pt0 + (kb + 1) * 128])
                                 for kb in range(nblk)],
                                tag=f"pv{qt}.{h}")
                        dsta = a_col[poff:poff + 64,
                                     c * 512 + qq * 128:c * 512 + (qq + 1) * 128]
                        em.op("act", lambda e: e.activation(
                            out=dsta, in_=pvp[0:64, 0:128], func=AF.Copy),
                            ins=[pvp[:]], outs=[dsta])

                    return front, tail_dve, tail_pe
                f_, td_, tp_ = mk()
                if apipe[0] is None:
                    f_()
                else:
                    ptd, ptp = apipe[0]
                    ptd()
                    f_()
                    ptp()
                apipe[0] = (td_, tp_)
        if apipe[0] is not None:
            apipe[0][0]()
            apipe[0][1]()
        for ts4 in range(4):
            tt = t4 * 4 + ts4
            em.dma("sp", gwork[:], dd["g0"][tt * 128:(tt + 1) * 128, :], tag=f"g0{tt}")
            em.dma("sp", xwork[tt % 2], dd["x"][tt * 128:(tt + 1) * 128, :],
                   tag=f"xb{tt}")
            for half in range(2):
                ps = ps_pr[half]
                emit_mm(em, ps[:],
                        [(a_col[:, c * 512 + ts4 * 128:c * 512 + (ts4 + 1) * 128],
                          wo_sb[:, c * D + half * 512:c * D + half * 512 + 512])
                         for c in range(8)], tag=f"o{tt}.{half}")
                em.op("dve", lambda e, ps=ps, half=half: e.tensor_tensor(
                    out=swork[:, half * 512:half * 512 + 512], in0=ps[:],
                    in1=borep[:, half * 512:half * 512 + 512], op=ALU.add),
                    ins=[ps[:], borep[:]],
                    outs=[swork[:, half * 512:half * 512 + 512]])
            em.op("dve", lambda e: e.tensor_tensor(out=gwork[:], in0=swork[:],
                                                   in1=gwork[:], op=ALU.mult),
                  ins=[swork[:], gwork[:]], outs=[gwork[:]])
            em.op("dve", lambda e, tt=tt: e.tensor_tensor(
                out=swork[:], in0=gwork[:], in1=xwork[tt % 2], op=ALU.add),
                ins=[gwork[:], xwork[tt % 2]], outs=[swork[:]])
            em.dma("sp", dd["x1"][tt * 128:(tt + 1) * 128, :], swork[:],
                   tag=f"x1{tt}")


def phase_l1f(nc, em, es, dd, C):
    sb = lambda name, shape, dt=F32: es.enter_context(
        nc.sbuf_tensor("l1f_" + name, shape, dt))
    pst = lambda name, shape, dt=F32: es.enter_context(
        nc.psum_tensor("l1f_" + name, shape, dt))
    wg = sb("wg", [P, 8 * I1], BF16)
    wu = sb("wu", [P, 8 * I1], BF16)
    wd = sb("wd", [P, 24 * D], BF16)
    xn_fm = sb("xnfm", [P, 8 * T], BF16)
    h3fm = sb("h3fm", [P, 24 * 256], BF16)
    sg = sb("sg", [P, 256])
    gtile = sb("gt", [P, D])
    xwork = [sb("xw0", [P, D])[:], sb("xw1", [P, D])[:]]
    x_bf = sb("xbf", [P, D], BF16)
    stats = sb("stats", [P, 36])
    small = sb("small", [P, 16])
    lns = {"stats": stats, "mu": small[:, 0:1], "mu2": small[:, 1:2],
           "var": small[:, 2:3], "sd": small[:, 3:4], "sd2": small[:, 4:5],
           "dump": gtile[:, 0:1024]}
    ps_g = [pst("psg%d" % i, [P, 512]) for i in range(2)]
    ps_u = [pst("psu%d" % i, [P, 512]) for i in range(2)]
    ps_dn = [pst("psdn%d" % i, [P, 512]) for i in range(2)]
    ps_tp = [pst("pstp%d" % i, [P, 1024], BF16) for i in range(2)]
    ident = C["ident"]
    Q = 1.0 / 127.0

    for src, dst, w in ((dd["wg1T"], wg, I1), (dd["wu1T"], wu, I1),
                        (dd["wd1T"], wd, D)):
        nchunk = src.shape[0] // 128
        em.dma("sp", dst[:, 0:nchunk * w].rearrange("p (c n) -> p c n", c=nchunk),
               src.rearrange("(c p) n -> p c n", p=128), tag="wload")

    for tt in range(NT):
        stg = xwork[tt % 2]
        em.dma("sp", stg, dd["x1"][tt * 128:(tt + 1) * 128, :], tag=f"xa{tt}")
        emit_ln(em, stg, x_bf[:], lns, si=tt)
        dst3 = xn_fm[:, :].rearrange("p (c n) -> p c n", c=8)[:, :, tt * 128:(tt + 1) * 128]
        emit_tp8(em, x_bf[:], ps_tp[tt % 2], dst3, ident)

    for t8 in range(8):
        tl = t8 * 256
        for hc in range(24):
            pg, pu = ps_g[hc % 2], ps_u[hc % 2]
            emit_mm(em, pg[:, 0:256],
                    [(wg[:, d * I1 + hc * 128:d * I1 + (hc + 1) * 128],
                      xn_fm[:, d * T + tl:d * T + tl + 256]) for d in range(8)],
                    tag=f"g{t8}.{hc}")
            emit_mm(em, pu[:, 0:256],
                    [(wu[:, d * I1 + hc * 128:d * I1 + (hc + 1) * 128],
                      xn_fm[:, d * T + tl:d * T + tl + 256]) for d in range(8)],
                    tag=f"u{t8}.{hc}")
            em.op("act", lambda e, pg=pg: e.activation(
                out=sg[:], in_=pg[:, 0:256], func=AF.Silu, scale=Q),
                ins=[pg[:]], outs=[sg[:]])
            em.op("dve", lambda e, pu=pu, hc=hc: e.tensor_tensor(
                out=h3fm[:, hc * 256:(hc + 1) * 256], in0=sg[:], in1=pu[:, 0:256],
                op=ALU.mult),
                ins=[sg[:], pu[:]], outs=[h3fm[:, hc * 256:(hc + 1) * 256]])
        for ts in range(2):
            tt = t8 * 2 + ts
            for half in range(2):
                pd = ps_dn[half]
                emit_mm(em, pd[:],
                        [(h3fm[:, j * 256 + ts * 128:j * 256 + (ts + 1) * 128],
                          wd[:, j * D + half * 512:j * D + half * 512 + 512])
                         for j in range(24)],
                        tag=f"dn{tt}.{half}")
                em.op("act", lambda e, pd=pd, half=half: e.activation(
                    out=gtile[:, half * 512:(half + 1) * 512],
                    in_=pd[:], func=AF.Silu, scale=1.0 / 16129.0),
                    ins=[pd[:]],
                    outs=[gtile[:, half * 512:(half + 1) * 512]])
            em.dma("sp", dd["g1"][tt * 128:(tt + 1) * 128, :], gtile[:],
                   tag=f"g1t{tt}")


def phase_l1a(nc, em, es, dd, C):
    sb = lambda name, shape, dt=F32: es.enter_context(
        nc.sbuf_tensor("l1a_" + name, shape, dt))
    pst = lambda name, shape, dt=F32: es.enter_context(
        nc.psum_tensor("l1a_" + name, shape, dt))
    wq_sb = sb("wq", [P, 8 * D], BF16)
    wk_sb = sb("wk", [P, 8 * D], BF16)
    wv_sb = sb("wv", [P, 8 * D], BF16)
    wo_sb = sb("wo", [P, 16 * D], BF16)
    wqf_bf = sb("wqf", [P, 16], BF16)
    wkf_bf = sb("wkf", [P, 16], BF16)
    nwrep = sb("nwrep", [P, D])
    h_fm = sb("hfm", [P, 8 * 512], BF16)
    q_fm = sb("qfm", [P, 8 * 512], BF16)
    k_col = sb("kcol", [P, 8 * 512], BF16)
    k_prev = sb("kprev", [P, 8 * 128], BF16)
    v_prev = sb("vprev", [P, 1024], BF16)
    lin_col = sb("lincol", [P, 8 * 512], BF16)
    win_col = q_fm
    qf_aug = sb("qfaug", [32, 512], BF16)
    kf_aug = sb("kfaug", [32, 512], BF16)
    qphi1 = sb("qphi1", [P, 4 * 512], BF16)
    qphi2 = sb("qphi2", [32, 4 * 512], BF16)
    kphi1 = sb("kphi1", [P, 4 * 512], BF16)
    kphi2 = sb("kphi2", [32, 4 * 512], BF16)
    kphi_tm = sb("kphitm", [P, 16 * 160], BF16)
    tphi = sb("tphi", [P, 1024], BF16)
    v65 = sb("v65", [P, 4 * 16 * 65], BF16)
    kv1 = sb("kv1", [P, 16 * 65])
    kv2 = sb("kv2", [32, 16 * 65])
    kv1b = sb("kv1b", [P, 16 * 65], BF16)
    kv2b = sb("kv2b", [32, 16 * 65], BF16)
    atm4 = sb("atm4", [P, 512], BF16)
    ytmp4 = sb("ytmp4", [P, 512], BF16)
    swin = sb("swin", [P, 4 * 512])
    Pw = sb("pw", [P, 512], BF16)
    PTw = sb("ptw", [P, 512], BF16)
    xwork = [sb("xw0", [P, D])[:], sb("xw1", [P, D])[:]]
    xnw = sb("xnw", [P, D])
    x_bf = sb("xbf", [P, D], BF16)
    gwork = sb("gw", [P, D])
    swork = sb("sw", [P, D])
    stats = sb("stats", [P, 36])
    small = sb("small", [P, 16])
    mrow = sb("mrow", [P, 20])
    lns = {"stats": stats, "mu": small[:, 0:1], "mu2": small[:, 1:2],
           "var": small[:, 2:3], "sd": small[:, 3:4], "sd2": small[:, 4:5],
           "dump": swork[:, 0:1024]}
    ps_m0 = pst("psm0", [P, 512])
    ps_m1 = pst("psm1", [P, 512])
    ps_a = pst("psa", [P, 512])
    ps_y = pst("psy", [P, 512])
    ps_kv = pst("pskv", [P, 512])
    ps_kv2 = pst("pskv2", [P, 512])
    ps_o = pst("pso", [P, 512])
    ps_tp = [pst("pstp0", [P, 1024], BF16)]
    ident, mtri, ones_c = C["ident"], C["mtri"], C["ones_col"]
    mwin = C["mwin"]
    c17 = C["c17"]
    sq1c = C["sq1"]
    sq2c = C["sq2"]
    SL1, SR1 = c17[0:17, 0:128], c17[0:17, 128:256]
    SL2, SR2 = c17[0:17, 256:288], c17[0:17, 288:320]
    SLa, SRa = c17[0:17, 320:480], c17[0:17, 480:640]

    for wt, (src, dst, nch) in enumerate([(dd["wq1"], wq_sb, 8),
                                          (dd["wk1"], wk_sb, 8),
                                          (dd["wv1"], wv_sb, 8),
                                          (dd["wout"], wo_sb, 16)]):
        for d8 in range(nch):
            stg = xwork[d8 % 2]
            em.dma("sp", stg, src[d8 * 128:(d8 + 1) * 128, :], tag=f"w{wt}.{d8}")
            em.op("act", lambda e, stg=stg, dst=dst, d8=d8: e.activation(
                out=dst[:, d8 * D:(d8 + 1) * D], in_=stg, func=AF.Copy),
                ins=[stg], outs=[dst[:, d8 * D:(d8 + 1) * D]])
    em.dma("sp", xwork[0][:, 0:16], dd["wqf"], tag="wqf")
    em.op("act", lambda e: e.activation(out=wqf_bf[:], in_=xwork[0][:, 0:16],
                                        func=AF.Copy),
          ins=[xwork[0]], outs=[wqf_bf[:]])
    em.dma("sp", xwork[1][:, 0:16], dd["wkf"], tag="wkf")
    em.op("act", lambda e: e.activation(out=wkf_bf[:], in_=xwork[1][:, 0:16],
                                        func=AF.Copy),
          ins=[xwork[1]], outs=[wkf_bf[:]])
    em.dma("sp", nwrep[:], dd["normw_rep"])
    for t_, val in ((kv1[:], 0.0), (kv2[:], 0.0), (kv1b[:], 0.0), (kv2b[:], 0.0),
                    (k_prev[:], 0.0), (v_prev[:], 0.0),
                    (qf_aug[:], 1.0), (kf_aug[:], 1.0)):
        em.op("pool", lambda e, t_=t_, val=val: e.memset(t_, val),
              ins=[], outs=[t_])

    for t4 in range(4):
        tl = t4 * 512
        for loc in range(4):
            tt = t4 * 4 + loc
            stg = xwork[tt % 2]
            em.dma("sp", stg, dd["x1"][tt * 128:(tt + 1) * 128, :], tag=f"xa{tt}")
            emit_ln(em, stg, xnw[:], lns, si=tt)
            ms, trs = small[:, 5:6], small[:, 6:7]
            em.op("act", lambda e: e.activation(out=swork[:], in_=xnw[:],
                                                func=AF.Square, accum_out=ms),
                  ins=[xnw[:]], outs=[swork[:], ms])
            em.op("dve", lambda e: e.tensor_scalar(out=trs, in0=ms, scalar1=1.0 / D,
                                                   scalar2=1e-6, op0=ALU.mult,
                                                   op1=ALU.add),
                  ins=[ms], outs=[trs])
            em.op("act", lambda e: e.activation(out=small[:, 7:8], in_=trs,
                                                func=AF.Sqrt),
                  ins=[trs], outs=[small[:, 7:8]])
            em.op("dve", lambda e: e.reciprocal(out=trs, in_=small[:, 7:8]),
                  ins=[small[:, 7:8]], outs=[trs])
            em.op("dve", lambda e: e.scalar_tensor_tensor(
                out=x_bf[:], in0=xnw[:], scalar=trs, in1=nwrep[:],
                op0=ALU.mult, op1=ALU.mult),
                ins=[xnw[:], trs, nwrep[:]], outs=[x_bf[:]])
            dst3 = h_fm[:, :].rearrange("p (c n) -> p c n", c=8)[:, :, loc * 128:(loc + 1) * 128]
            emit_tp8(em, x_bf[:], ps_tp[0], dst3, ident)
        spipe = Pipe()
        em.op("pool", lambda e: e.memset(v65[:], 1.0), ins=[], outs=[v65[:]])
        for c in range(8):
            for which, wsb, out in [(0, wq_sb, q_fm[:, c * 512:(c + 1) * 512]),
                                    (1, wk_sb, k_col[:, c * 512:(c + 1) * 512])]:
                ps = ps_m0 if which == 0 else ps_m1
                emit_mm(em, ps[:],
                        [(wsb[:, d * D + c * 128:d * D + (c + 1) * 128],
                          h_fm[:, d * 512:d * 512 + 512]) for d in range(8)],
                        tag=f"prj{which}.{t4}.{c}")
                eng = "act" if which == 0 else "dve"
                em.op(eng, lambda e, ps=ps, out=out, eng=eng: (
                    e.activation(out=out, in_=ps[:], func=AF.Copy) if eng == "act"
                    else e.tensor_copy(out=out, in_=ps[:])),
                    ins=[ps[:]], outs=[out])
        for ts4 in range(4):
            for half in range(2):
                ps = ps_o
                emit_mm(em, ps[:],
                        [(h_fm[:, d * 512 + ts4 * 128:d * 512 + (ts4 + 1) * 128],
                          wv_sb[:, d * D + half * 512:d * D + half * 512 + 512])
                         for d in range(8)], tag=f"v{t4}.{ts4}.{half}")
                out = v65[:, ts4 * 1040 + half * 8 * 65:
                          ts4 * 1040 + half * 8 * 65 + 8 * 65].rearrange(
                    "p (h w) -> p h w", w=65)[:, :, 0:64]
                em.op("dve", lambda e, ps=ps, out=out: e.tensor_copy(
                    out=out, in_=ps[:].rearrange("p (h w) -> p h w", w=64)),
                    ins=[ps[:]], outs=[v65[:]])
        for g in range(4):
            heads = [4 * g + j for j in range(4)]
            for j, h in enumerate(heads):
                c, poff = h // 2, 64 * (h % 2)
                for src_fm, wf, aug in ((q_fm, wqf_bf, qf_aug), (k_col, wkf_bf, kf_aug)):
                    ps = ps_m0
                    emit_mm(em, ps[0:16, :],
                            [(wf[poff:poff + 64, :],
                              src_fm[poff:poff + 64, c * 512:(c + 1) * 512])],
                            tag=f"qf{t4}.{h}")
                    em.op("act", lambda e, ps=ps, aug=aug: e.activation(
                        out=aug[0:16, :], in_=ps[0:16, :], func=AF.Copy),
                        ins=[ps[:]], outs=[aug[0:16, :]])
                pmL = (ps_m0, ps_a)[j % 2]
                pmR = (ps_m1, ps_y)[j % 2]
                for aug, pall1, pall2, sq in ((qf_aug, qphi1, qphi2, True),
                                              (kf_aug, kphi1, kphi2, False)):
                    p1 = pall1[:, j * 512:(j + 1) * 512]
                    p2 = pall2[0:32, j * 512:(j + 1) * 512]
                    for SL, SR, pp, w, sqap in ((SL1, SR1, p1, 128, sq1c),
                                                (SL2, SR2, p2, 32, sq2c)):
                        tph = tphi[0:w, (j % 2) * 512:(j % 2) * 512 + 512]
                        emit_mm(em, pmL[0:w, :], [(SL[:, 0:w], aug[0:17, :])],
                                tag="phL")
                        emit_mm(em, pmR[0:w, :], [(SR[:, 0:w], aug[0:17, :])],
                                tag="phR")
                        if sq:
                            em.op("dve", lambda e, w=w, sqap=sqap, tph=tph,
                                  pmL=pmL: e.tensor_scalar(
                                out=tph, in0=pmL[0:w, :], scalar1=sqap,
                                scalar2=None, op0=ALU.mult),
                                ins=[pmL[:]], outs=[tph])
                        else:
                            em.op("act", lambda e, w=w, tph=tph, pmL=pmL:
                                  e.activation(
                                out=tph, in_=pmL[0:w, :], func=AF.Copy),
                                ins=[pmL[:]], outs=[tph])
                        em.op("dve", lambda e, pp=pp, w=w, tph=tph, pmR=pmR:
                              e.tensor_tensor(
                            out=pp, in0=tph, in1=pmR[0:w, :], op=ALU.mult),
                            ins=[tph, pmR[:]], outs=[pp])
                for c2 in range(2):
                    for k2 in range(2):
                        cc = c2 * 2 + k2
                        emit_mm(em, pmL[:, k2 * 160:k2 * 160 + 160],
                                [(kf_aug[0:17, cc * 128:(cc + 1) * 128], SLa)],
                                tag="phLtm")
                        emit_mm(em, pmR[:, k2 * 160:k2 * 160 + 160],
                                [(kf_aug[0:17, cc * 128:(cc + 1) * 128], SRa)],
                                tag="phRtm")
                    tph = tphi[:, (j % 2) * 512:(j % 2) * 512 + 320]
                    em.op("act", lambda e, tph=tph, pmL=pmL: e.activation(
                        out=tph, in_=pmL[:, 0:320], func=AF.Copy),
                        ins=[pmL[:]], outs=[tph])
                    ktm2 = kphi_tm[:, (j * 4 + c2 * 2) * 160:
                                   (j * 4 + c2 * 2 + 2) * 160]
                    em.op("dve", lambda e, ktm2=ktm2, tph=tph, pmR=pmR:
                          e.tensor_tensor(
                        out=ktm2, in0=tph, in1=pmR[:, 0:320], op=ALU.mult),
                        ins=[tph, pmR[:]], outs=[ktm2])
            for cc in range(4):
                def mkscan(g=g, cc=cc, heads=tuple(heads)):
                    n_glob = t4 * 4 + cc
                    par = cc % 2
                    csl = lambda pall, j: pall[:, j * 512 + cc * 128:j * 512 + (cc + 1) * 128]
                    csl2 = lambda pall, j: pall[0:32, j * 512 + cc * 128:j * 512 + (cc + 1) * 128]
                    v65s = lambda j: v65[:, cc * 1040 + (4 * g + j) * 65:
                                         cc * 1040 + (4 * g + j) * 65 + 65]
                    ytp = ytmp4[:, par * 256:(par + 1) * 256]

                    def front():
                        for j in range(4):
                            emit_mm(em, ps_a[:, j * 128:(j + 1) * 128],
                                    [(csl(kphi1, j), csl(qphi1, j)),
                                     (csl2(kphi2, j), csl2(qphi2, j))],
                                    tag=f"at{n_glob}.{g}.{j}")
                        em.op("dve", lambda e: e.tensor_tensor(
                            out=atm4[:].rearrange("p (h w) -> p h w", w=128),
                            in0=ps_a[:].rearrange("p (h w) -> p h w", w=128),
                            in1=mtri.broadcast_to([P, 128, 4]).transpose([0, 2, 1]),
                            op=ALU.mult),
                            ins=[ps_a[:], mtri], outs=[atm4[:]])
                        for j, h in enumerate(heads):
                            emit_mm(em, ps_y[:, j * 65:(j + 1) * 65],
                                    [(atm4[:, j * 128:(j + 1) * 128], v65s(j)),
                                     (csl(qphi1, j), kv1b[:, h * 65:h * 65 + 65]),
                                     (csl2(qphi2, j), kv2b[0:32, h * 65:h * 65 + 65])],
                                    tag=f"y{n_glob}.{g}.{j}")
                        ze4 = mrow[:, 0:4]
                        zr4 = mrow[:, 4:8]
                        yv = ps_y[:, 0:260].rearrange("p (h w) -> p h w", w=65)
                        em.op("dve", lambda e: e.tensor_scalar(
                            out=ze4, in0=yv[:, :, 64:65], scalar1=1e-6,
                            scalar2=None, op0=ALU.add), ins=[ps_y[:]], outs=[ze4])
                        em.op("dve", lambda e: e.reciprocal(
                            out=zr4, in_=ze4), ins=[ze4], outs=[zr4])
                        em.op("dve", lambda e: e.tensor_tensor(
                            out=ytp.rearrange("p (h w) -> p h w", w=64),
                            in0=yv[:, :, 0:64],
                            in1=zr4.broadcast_to([P, 4, 64]), op=ALU.mult),
                            ins=[ps_y[:], zr4], outs=[ytp])
                        for j, h in enumerate(heads):
                            ktm = kphi_tm[:, (j * 4 + cc) * 160:(j * 4 + cc) * 160 + 160]
                            emit_mm(em, ps_kv[:, j * 65:(j + 1) * 65],
                                    [(ktm[:, 0:128], v65s(j))],
                                    tag=f"kv1u{n_glob}.{g}.{j}")
                            emit_mm(em, ps_kv2[0:25, j * 65:(j + 1) * 65],
                                    [(ktm[:, 128:153], v65s(j))],
                                    tag=f"kv2u{n_glob}.{g}.{j}")
                        gsl = slice(4 * g * 65, (4 * g + 4) * 65)
                        for dst_f, dst_b, src_ap in (
                                (kv1[:, gsl], kv1b[:, gsl], ps_kv[:, 0:260]),
                                (kv2[0:25, gsl], kv2b[0:25, gsl],
                                 ps_kv2[0:25, 0:260])):
                            em.op("dve", lambda e, dst_f=dst_f, src_ap=src_ap:
                                  e.tensor_tensor(out=dst_f, in0=dst_f,
                                                  in1=src_ap, op=ALU.add),
                                  ins=[dst_f, src_ap], outs=[dst_f])
                            em.op("pool", lambda e, dst_f=dst_f, dst_b=dst_b:
                                  e.tensor_copy(out=dst_b, in_=dst_f),
                                  ins=[dst_f], outs=[dst_b])
                        for pr in range(2):
                            h0 = heads[2 * pr]
                            cpr = h0 // 2
                            sbank = (ps_o, ps_m0)[pr]
                            for j2 in range(2):
                                h = h0 + j2
                                poff = 64 * (h % 2)
                                q_h = q_fm[poff:poff + 64,
                                           cpr * 512 + cc * 128:cpr * 512 + (cc + 1) * 128]
                                kcur = k_col[poff:poff + 64,
                                             cpr * 512 + cc * 128:cpr * 512 + (cc + 1) * 128]
                                so = j2 * 256
                                if n_glob == 0:
                                    em.op("dve", lambda e, sbank=sbank, so=so:
                                          e.memset(sbank[:, so:so + 128], -1e30),
                                          ins=[], outs=[sbank[:]])
                                else:
                                    kprev_ap = (k_prev[poff:poff + 64,
                                                       cpr * 128:(cpr + 1) * 128]
                                                if cc == 0 else
                                                k_col[poff:poff + 64,
                                                      cpr * 512 + (cc - 1) * 128:
                                                      cpr * 512 + cc * 128])
                                    emit_mm(em, sbank[:, so:so + 128],
                                            [(q_h, kprev_ap)],
                                            tag=f"wS0.{n_glob}.{h}")
                                emit_mm(em, sbank[:, so + 128:so + 256],
                                        [(q_h, kcur)], tag=f"wS1.{n_glob}.{h}")
                            em.op("dve", lambda e, sbank=sbank: e.tensor_tensor(
                                out=sbank[:, 0:512].rearrange("p (h w) -> p h w", w=256),
                                in0=sbank[:, 0:512].rearrange("p (h w) -> p h w", w=256),
                                in1=mwin.broadcast_to([P, 256, 2]).transpose([0, 2, 1]),
                                op=ALU.add),
                                ins=[sbank[:], mwin], outs=[sbank[:]])
                            swp = swin[:, (par * 2 + pr) * 512:(par * 2 + pr + 1) * 512]
                            em.op("dve", lambda e, sbank=sbank, swp=swp:
                                  e.tensor_copy(out=swp, in_=sbank[:, 0:512]),
                                  ins=[sbank[:]], outs=[swp])

                    def tail():
                        def tpy(e):
                            last = None
                            for j in range(4):
                                last = e.transpose(
                                    ps_tp[0][64 * (j % 2):64 * (j % 2) + 64,
                                             (j // 2) * 128:(j // 2) * 128 + 128],
                                    in_=ytp[:, j * 64:(j + 1) * 64], identity=ident)
                            return last
                        em.op("pe", tpy, ins=[ytp, ident], outs=[ps_tp[0][:]])
                        for pr in range(2):
                            cpr = heads[2 * pr] // 2
                            dst = lin_col[:, cpr * 512 + cc * 128:cpr * 512 + (cc + 1) * 128]
                            em.op("act", lambda e, pr=pr, dst=dst: e.activation(
                                out=dst, in_=ps_tp[0][:, pr * 128:(pr + 1) * 128],
                                func=AF.Copy),
                                ins=[ps_tp[0][:]], outs=[dst])
                        for pr in range(2):
                            h0 = heads[2 * pr]
                            cpr = h0 // 2
                            swp = swin[:, (par * 2 + pr) * 512:(par * 2 + pr + 1) * 512]
                            wm2 = mrow[:, 8 + pr * 2:8 + pr * 2 + 2]
                            em.op("dve", lambda e, swp=swp, wm2=wm2: e.tensor_reduce(
                                out=wm2, in_=swp.rearrange("p (h w) -> p h w", w=256),
                                axis=AX.X, op=ALU.max), ins=[swp], outs=[wm2])
                            wmb2 = mrow[:, 12 + pr * 2:12 + pr * 2 + 2]
                            em.op("dve", lambda e, wm2=wm2, wmb2=wmb2:
                                  e.tensor_scalar(
                                out=wmb2, in0=wm2, scalar1=-0.125, scalar2=None,
                                op0=ALU.mult), ins=[wm2], outs=[wmb2])
                            Pwp = Pw[:]
                            zw2 = mrow[:, 16 + pr * 2:16 + pr * 2 + 2]
                            for j3 in range(2):
                                em.op("act", lambda e, swp=swp, Pwp=Pwp, j3=j3,
                                      wmb2=wmb2, zw2=zw2: e.activation(
                                    out=Pwp[:, j3 * 256:(j3 + 1) * 256],
                                    in_=swp[:, j3 * 256:(j3 + 1) * 256],
                                    func=AF.Exp, scale=0.125,
                                    bias=wmb2[:, j3:j3 + 1],
                                    accum_out=zw2[:, j3:j3 + 1]),
                                    ins=[swp[:, j3 * 256:(j3 + 1) * 256],
                                         wmb2[:, j3:j3 + 1]],
                                    outs=[Pwp[:, j3 * 256:(j3 + 1) * 256],
                                          zw2[:, j3:j3 + 1]])
                            em.op("dve", lambda e, zw2=zw2: e.reciprocal(
                                out=zw2, in_=zw2), ins=[zw2], outs=[zw2])
                            em.op("pool", lambda e, Pwp=Pwp, zw2=zw2: e.tensor_tensor(
                                out=Pwp.rearrange("p (h w) -> p h w", w=256),
                                in0=Pwp.rearrange("p (h w) -> p h w", w=256),
                                in1=zw2.broadcast_to([P, 2, 256]), op=ALU.mult),
                                ins=[Pwp, zw2], outs=[Pwp])
                            def tpw(e, Pwp=Pwp):
                                last = None
                                for b2 in range(4):
                                    last = e.transpose(
                                        ps_tp[0][:, 256 + b2 * 128:256 + (b2 + 1) * 128],
                                        in_=Pwp[:, b2 * 128:(b2 + 1) * 128],
                                        identity=ident)
                                return last
                            em.op("pe", tpw, ins=[Pwp, ident], outs=[ps_tp[0][:]])
                            PTwp = PTw[:]
                            em.op("act", lambda e, PTwp=PTwp: e.activation(
                                out=PTwp, in_=ps_tp[0][:, 256:768], func=AF.Copy),
                                ins=[ps_tp[0][:]], outs=[PTwp])
                            for j2 in range(2):
                                h = h0 + j2
                                vprev_ap = (v_prev[:, h * HD:(h + 1) * HD]
                                            if cc == 0 else
                                            v65[:, (cc - 1) * 1040 + h * 65:
                                                (cc - 1) * 1040 + h * 65 + HD])
                                vcur_ap = v65[:, cc * 1040 + h * 65:
                                              cc * 1040 + h * 65 + HD]
                                emit_mm(em, ps_m1[64 * j2:64 * j2 + 64,
                                                  pr * 128:(pr + 1) * 128],
                                        [(vprev_ap, PTwp[:, j2 * 256:j2 * 256 + 128]),
                                         (vcur_ap,
                                          PTwp[:, j2 * 256 + 128:j2 * 256 + 256])],
                                        tag=f"wpv{n_glob}.{h}")
                            dstw = win_col[:, cpr * 512 + cc * 128:cpr * 512 + (cc + 1) * 128]
                            em.op("act", lambda e, pr=pr, dstw=dstw: e.activation(
                                out=dstw, in_=ps_m1[:, pr * 128:(pr + 1) * 128],
                                func=AF.Copy),
                                ins=[ps_m1[:]], outs=[dstw])

                    return front, tail
                spipe.push(*mkscan())
        spipe.flush()
        em.op("dve", lambda e: e.tensor_copy(
            out=k_prev[:, :].rearrange("p (c j) -> p c j", c=8),
            in_=k_col[:, :].rearrange("p (c j) -> p c j", j=512)[:, :, 384:512]),
            ins=[k_col[:]], outs=[k_prev[:]])
        em.op("dve", lambda e: e.tensor_copy(
            out=v_prev[:].rearrange("p (h w) -> p h w", w=64),
            in_=v65[:, 3 * 1040:].rearrange("p (h w) -> p h w", w=65)[:, :, 0:64]),
            ins=[v65[:]], outs=[v_prev[:]])
        for ts4 in range(4):
            tt = t4 * 4 + ts4
            em.dma("sp", gwork[:], dd["g1"][tt * 128:(tt + 1) * 128, :],
                   tag=f"g1{tt}")
            em.dma("sp", xwork[ts4 % 2], dd["x1"][tt * 128:(tt + 1) * 128, :],
                   tag=f"xb{tt}")
            em.op("dve", lambda e, ts4=ts4, tt=tt: e.tensor_scalar(
                out=xnw[:], in0=xwork[ts4 % 2], scalar1=stats[:, 2 * tt:2 * tt + 1],
                scalar2=stats[:, 2 * tt + 1:2 * tt + 2], op0=ALU.mult, op1=ALU.add),
                ins=[xwork[ts4 % 2], stats[:]], outs=[xnw[:]])
            for half in range(2):
                emit_mm(em, ps_o[:],
                        [(lin_col[:, c2 * 512 + ts4 * 128:c2 * 512 + (ts4 + 1) * 128],
                          wo_sb[:, c2 * D + half * 512:c2 * D + half * 512 + 512])
                         for c2 in range(8)] +
                        [(win_col[:, c2 * 512 + ts4 * 128:c2 * 512 + (ts4 + 1) * 128],
                          wo_sb[:, (8 + c2) * D + half * 512:(8 + c2) * D + half * 512 + 512])
                         for c2 in range(8)], tag=f"wo{tt}.{half}")
                em.op("dve", lambda e, half=half: e.tensor_tensor(
                    out=swork[:, half * 512:half * 512 + 512], in0=ps_o[:],
                    in1=xnw[:, half * 512:half * 512 + 512], op=ALU.add),
                    ins=[ps_o[:], xnw[:]],
                    outs=[swork[:, half * 512:half * 512 + 512]])
            em.op("dve", lambda e: e.tensor_tensor(out=gwork[:], in0=swork[:],
                                                   in1=gwork[:], op=ALU.mult),
                  ins=[swork[:], gwork[:]], outs=[gwork[:]])
            em.op("dve", lambda e, ts4=ts4: e.tensor_tensor(
                out=swork[:], in0=gwork[:], in1=xwork[ts4 % 2], op=ALU.add),
                ins=[gwork[:], xwork[ts4 % 2]], outs=[swork[:]])
            em.dma("sp", dd["y"][tt * 128:(tt + 1) * 128, :], swork[:],
                   tag=f"y{tt}")


import math

NCORES = 8


def _consts():
    iu, ju = np.triu_indices(16)
    a_map = np.array([16] + list(range(16)) + list(iu))
    b_map = np.array([16] + [16] * 16 + list(ju))
    scale = np.array([1.0] + [1.0] * 16 +
                     [0.5 if i == j else 1.0 / math.sqrt(2.0)
                      for i, j in zip(iu, ju)])
    SL = np.zeros((17, 153), np.float32)
    SR = np.zeros((17, 153), np.float32)
    for f in range(153):
        SL[a_map[f], f] = 1.0
        SR[b_map[f], f] = 1.0
    c17 = np.zeros((32, 640), np.float32)
    c17[0:17, 0:128] = SL[:, 0:128]
    c17[0:17, 128:256] = SR[:, 0:128]
    c17[0:17, 256:281] = SL[:, 128:153]
    c17[0:17, 288:313] = SR[:, 128:153]
    c17[0:17, 320:473] = SL
    c17[0:17, 480:633] = SR
    cbf = np.zeros((128, 384), np.float32)
    cbf[:, 0:128] = np.eye(128)
    cbf[:, 128:256] = (np.arange(128)[:, None] <= np.arange(128)[None, :])
    cbf[:, 256] = 1.0
    ii = np.arange(128)[:, None]
    jj = np.arange(256)[None, :]
    mwin = np.where((jj - ii >= 64) & (jj - ii <= 128), 0.0,
                    -1e30).astype(np.float32)
    mdiag = np.where(np.arange(128)[None, :] <= np.arange(128)[:, None],
                     0.0, -1e30).astype(np.float32)
    sq = (scale * scale).astype(np.float32)
    sq1 = np.zeros((128, 1), np.float32)
    sq1[0:128, 0] = sq[0:128]
    sq2 = np.zeros((128, 1), np.float32)
    sq2[0:25, 0] = sq[128:153]
    return {"c17": c17.astype(BF), "cbf": cbf.astype(BF), "mwin": mwin,
            "mdiag": mdiag, "sq1": sq1, "sq2": sq2}


def _build(phase):
    nc = bass.Bass("TRN2", target_bir_lowering=False)
    dm = {}

    def din(name, shape, dt=F32):
        dm[name] = nc.dram_tensor(name, shape, dt, kind="ExternalInput").ap()

    def dout(name, shape, dt=F32):
        dm[name] = nc.dram_tensor(name, shape, dt, kind="ExternalOutput").ap()

    if phase == "l0f":
        din("x", [T, D]); din("wg", [I0, D]); din("wu", [I0, D])
        din("wd", [D, I0]); dout("g0", [T, D])
        din("cst_bf", [128, 384], BF16)
    elif phase == "l0a":
        din("x", [T, D])
        for n in ("wqT", "wkT", "wvT", "woT"):
            din(n, [D, D])
        din("bqkv", [128, 24]); din("bvrep", [128, D]); din("borep", [128, D])
        din("g0", [T, D]); dout("x1", [T, D])
        din("cst_bf", [128, 384], BF16); din("cst_f32", [128, 256])
    elif phase == "l1f":
        din("x1", [T, D])
        din("wg1T", [D, I1], BF16); din("wu1T", [D, I1], BF16)
        din("wd1T", [I1, D], BF16)
        dout("g1", [T, D]); din("cst_bf", [128, 384], BF16)
    elif phase == "l1a":
        din("x1", [T, D])
        for n in ("wq1", "wk1", "wv1"):
            din(n, [D, D])
        din("wout", [2 * D, D]); din("wqf", [128, 16]); din("wkf", [128, 16])
        din("normw_rep", [128, D]); din("g1", [T, D]); dout("y", [T, D])
        din("cst_bf", [128, 384], BF16); din("cst17", [32, 640], BF16)
        din("cst_f32", [128, 258])

    with ExitStack() as es:
        em = Em(nc)
        sems = {d: es.enter_context(nc.semaphore(d)) for d in em.domains}
        cb = es.enter_context(nc.sbuf_tensor("cstb_sb", [128, 384], BF16))
        em.dma("sp", cb[:], dm["cst_bf"])
        C = {"ident": cb[:, 0:128], "mtri": cb[:, 128:256],
             "ones_col": cb[:, 256:257]}
        if "cst_f32" in dm:
            w_cf = int(dm["cst_f32"].shape[1])
            cf = es.enter_context(nc.sbuf_tensor("cstf_sb", [128, w_cf], F32))
            em.dma("sp", cf[:], dm["cst_f32"])
            C["mdiag"] = cf[:, 0:128]
            C["mwin"] = cf[:, 0:256]
            if w_cf >= 258:
                C["sq1"] = cf[:, 256:257]
                C["sq2"] = cf[0:32, 257:258]
        if "cst17" in dm:
            c17sb = es.enter_context(nc.sbuf_tensor("c17_sb", [32, 640], BF16))
            em.dma("sp", c17sb[:], dm["cst17"])
            C["c17"] = c17sb[:]
        {"l0f": phase_l0f, "l0a": phase_l0a,
         "l1f": phase_l1f, "l1a": phase_l1a}[phase](nc, em, es, dm, C)
        em.barrier()
        block = es.enter_context(nc.Block())
        em.finalize(block, sems)
    return nc


def _build_fused():
    nc = bass.Bass("TRN2", target_bir_lowering=False)
    dm = {}

    def din(name, shape, dt=F32):
        dm[name] = nc.dram_tensor(name, shape, dt, kind="ExternalInput").ap()

    def dint(name, shape, dt=F32):
        dm[name] = nc.dram_tensor(name, shape, dt, kind="Internal").ap()

    din("x", [T, D])
    din("wg", [I0, D]); din("wu", [I0, D]); din("wd", [D, I0])
    for n in ("wqT", "wkT", "wvT", "woT"):
        din(n, [D, D])
    din("bqkv", [128, 24]); din("bvrep", [128, D]); din("borep", [128, D])
    din("wg1T", [D, I1], BF16); din("wu1T", [D, I1], BF16)
    din("wd1T", [I1, D], BF16)
    for n in ("wq1", "wk1", "wv1"):
        din(n, [D, D])
    din("wout", [2 * D, D]); din("wqf", [128, 16]); din("wkf", [128, 16])
    din("normw_rep", [128, D])
    din("cst_bf", [128, 384], BF16); din("cst17", [32, 640], BF16)
    din("cst_f32", [128, 386])
    dint("g0", [T, D]); dint("x1", [T, D]); dint("g1", [T, D])
    dm["y"] = nc.dram_tensor("y", [T, D], F32, kind="ExternalOutput").ap()

    with ExitStack() as es:
        em = Em(nc)
        sems = {d: es.enter_context(nc.semaphore(d)) for d in em.domains}
        cb = es.enter_context(nc.sbuf_tensor("cstb_sb", [128, 384], BF16))
        cf = es.enter_context(nc.sbuf_tensor("cstf_sb", [128, 386], F32))
        c17sb = es.enter_context(nc.sbuf_tensor("c17_sb", [32, 640], BF16))
        em.dma("sp", cb[:], dm["cst_bf"])
        em.dma("sp", cf[:], dm["cst_f32"])
        em.dma("sp", c17sb[:], dm["cst17"])
        C = {"ident": cb[:, 0:128], "mtri": cb[:, 128:256],
             "ones_col": cb[:, 256:257], "mdiag": cf[:, 0:128],
             "mwin": cf[:, 128:384], "sq1": cf[:, 384:385],
             "sq2": cf[0:32, 385:386], "c17": c17sb[:]}
        d_l1 = dm
        with ExitStack() as es1:
            phase_l0f(nc, em, es1, dm, C)
        em.barrier()
        with ExitStack() as es2:
            phase_l0a(nc, em, es2, dm, C)
        em.barrier()
        with ExitStack() as es3:
            phase_l1f(nc, em, es3, d_l1, C)
        em.barrier()
        with ExitStack() as es4:
            phase_l1a(nc, em, es4, d_l1, C)
        em.barrier()
        block = es.enter_context(nc.Block())
        em.finalize(block, sems)
    return nc


_NC_CACHE = {}


def _get_nc(phase):
    if phase not in _NC_CACHE:
        _NC_CACHE[phase] = _build_fused() if phase == "fused" else _build(phase)
    return _NC_CACHE[phase]


def kernel(L0_attn_x, L0_attn_Wq, L0_attn_bq, L0_attn_Wk, L0_attn_bk,
           L0_attn_Wv, L0_attn_bv, L0_attn_Wo, L0_attn_bo,
           L0_ffn_w_gate, L0_ffn_w_up, L0_ffn_w_down,
           L1_attn_norm_w, L1_attn_Wq, L1_attn_Wk, L1_attn_Wv,
           L1_attn_Wqf, L1_attn_Wkf, L1_attn_Wout,
           L1_ffn_w_gate, L1_ffn_w_up, L1_ffn_w_down):
    f = lambda a: np.ascontiguousarray(np.asarray(a), dtype=np.float32)
    cs = _consts()
    x = f(L0_attn_x)
    com_l0f = {"wg": f(L0_ffn_w_gate), "wu": f(L0_ffn_w_up),
               "wd": f(L0_ffn_w_down), "cst_bf": cs["cbf"]}
    bqkv = np.zeros((128, 24), np.float32)
    for i, b in enumerate((L0_attn_bq, L0_attn_bk, L0_attn_bv)):
        bqkv[:, 8 * i:8 * (i + 1)] = f(b).reshape(8, 128).T
    cf32 = np.zeros((128, 256), np.float32)
    cf32[:, 0:128] = cs["mdiag"]
    com_l0a = {"wqT": f(np.asarray(L0_attn_Wq).T), "wkT": f(np.asarray(L0_attn_Wk).T),
               "wvT": f(np.asarray(L0_attn_Wv).T), "woT": f(np.asarray(L0_attn_Wo).T),
               "bqkv": bqkv, "bvrep": np.tile(f(L0_attn_bv)[None, :], (128, 1)),
               "borep": np.tile(f(L0_attn_bo)[None, :], (128, 1)),
               "cst_bf": cs["cbf"], "cst_f32": cf32}
    com_l1f = {"wg1T": np.ascontiguousarray(np.asarray(L1_ffn_w_gate).T).astype(BF),
               "wu1T": np.ascontiguousarray(np.asarray(L1_ffn_w_up).T).astype(BF),
               "wd1T": np.ascontiguousarray(np.asarray(L1_ffn_w_down).T).astype(BF),
               "cst_bf": cs["cbf"]}
    wqf = f(L1_attn_Wqf)
    wkf = f(L1_attn_Wkf)
    com_l1a = {"wq1": f(L1_attn_Wq), "wk1": f(L1_attn_Wk), "wv1": f(L1_attn_Wv),
               "wout": f(L1_attn_Wout),
               "wqf": np.concatenate([wqf, wqf], 0),
               "wkf": np.concatenate([wkf, wkf], 0),
               "normw_rep": np.tile(f(L1_attn_norm_w)[None, :], (128, 1)),
               "cst_bf": cs["cbf"], "cst17": cs["c17"],
               "cst_f32": np.concatenate(
                   [cs["mwin"], cs["sq1"], cs["sq2"]], 1)}
    cores = list(range(NCORES))
    nc = _get_nc("fused")
    cstf_fused = np.concatenate([cs["mdiag"], cs["mwin"], cs["sq1"], cs["sq2"]], 1)
    com = {}
    for d in (com_l0f, com_l0a, com_l1f, com_l1a):
        com.update(d)
    com["cst_f32"] = cstf_fused
    per_core = [dict(com, x=x[i]) for i in range(NCORES)]
    res = run_bass_kernel_spmd(nc, per_core, core_ids=cores)
    y = np.stack([res.results[i]["y"] for i in range(NCORES)], 0)
    return y.astype(np.float32)


# version 26 (final)
# speedup vs baseline: 1.1587x; 1.1179x over previous
import numpy as np
import ml_dtypes
from contextlib import ExitStack

import concourse.bass as bass
import concourse.mybir as mybir
from concourse.bass_utils import run_bass_kernel_spmd

F32 = mybir.dt.float32
BF16 = mybir.dt.bfloat16
AF = mybir.ActivationFunctionType
ALU = mybir.AluOpType
AX = mybir.AxisListType
BF = ml_dtypes.bfloat16

T = 2048
D = 1024
NT = T // 128
I0 = 2816
I1 = 3072
H = 16
HD = 64
FEAT = 16
NF = 153
CH = 128


STREAMS = ("pe", "dve", "act", "pool", "sp")
ENG_DOMS = ("pe", "dve", "act", "pool")
DMA_RING = {"sp": 6, "act": 2, "pool": 2}


class Em:
    def __init__(self, nc):
        self.nc = nc
        self.items = {s: [] for s in STREAMS}
        self.domains = list(ENG_DOMS)
        self.rings = {}
        for s, n in DMA_RING.items():
            self.rings[s] = [f"{s}q{i}" for i in range(n)]
            self.domains += self.rings[s]
        self.dma_idx = {s: 0 for s in DMA_RING}
        self.count = {d: 0 for d in self.domains}
        self.hwm = {(s, d): 0 for s in STREAMS for d in self.domains}
        self.tens = {}

    @staticmethod
    def _ivl(ap):
        pairs = list(ap.ap)
        lo = int(ap.offset)
        if pairs:
            pitch = int(pairs[0][0])
            npart = int(pairs[0][1])
        else:
            pitch, npart = 0, 1
        span = 0
        for step, cnt in pairs[1:]:
            span += abs(int(step)) * (int(cnt) - 1)
        if pitch > 0:
            pb = lo // pitch
            clo = lo - pb * pitch
        else:
            pb, clo = 0, lo
        chi = clo + span + 1
        if type(ap.tensor).__name__ == "PSumTensorHandle":
            ge = 2048 // mybir.dt.size(ap.dtype)
            clo = (clo // ge) * ge
            chi = ((chi + ge - 1) // ge) * ge
        return (ap.tensor.name, pb, pb + npart, clo, chi)

    def _dep_read(self, key, need):
        name, plo, phi, clo, chi = key
        t = self.tens.get(name)
        if not t:
            return
        for (wpl, wph, wcl, wch, dom, cnt) in t["w"]:
            if wpl < phi and plo < wph and wcl < chi and clo < wch:
                if cnt > need.get(dom, 0):
                    need[dom] = cnt

    def _dep_write(self, key, need):
        name, plo, phi, clo, chi = key
        t = self.tens.get(name)
        if not t:
            return
        for (xpl, xph, xcl, xch, dom, cnt) in t["w"] + t["r"]:
            if xpl < phi and plo < xph and xcl < chi and clo < xch:
                if cnt > need.get(dom, 0):
                    need[dom] = cnt

    def _record(self, kind, key, dom, cnt):
        name, plo, phi, clo, chi = key
        t = self.tens.setdefault(name, {"w": [], "r": []})
        if kind == "w":
            cov = lambda r: (plo <= r[0] and r[1] <= phi and
                             clo <= r[2] and r[3] <= chi)
            t["w"] = [r for r in t["w"] if not cov(r)]
            t["r"] = [r for r in t["r"] if not cov(r)]
            t["w"].append((plo, phi, clo, chi, dom, cnt))
        else:
            t["r"].append((plo, phi, clo, chi, dom, cnt))
            if len(t["r"]) > 400:
                by = {}
                for (xpl, xph, xcl, xch, d2, c2) in t["r"]:
                    cur = by.get(d2)
                    by[d2] = ((min(xpl, cur[0]), max(xph, cur[1]),
                               min(xcl, cur[2]), max(xch, cur[3]),
                               max(c2, cur[4])) if cur
                              else (xpl, xph, xcl, xch, c2))
                t["r"] = [(a, b, c, d, d2, e) for d2, (a, b, c, d, e)
                          in by.items()]

    def _wait(self, stream, d, val):
        if val > self.hwm[(stream, d)]:
            self.items[stream].append(("w", d, val))
            self.hwm[(stream, d)] = val

    def op(self, stream, fn, ins=(), outs=(), dom=None, inc=1, tag=""):
        if dom is None:
            dom = stream
        need = {}
        rkeys = [self._ivl(ap) for ap in ins]
        wkeys = [self._ivl(ap) for ap in outs]
        for k in rkeys:
            self._dep_read(k, need)
        for k in wkeys:
            self._dep_write(k, need)
        for d, val in sorted(need.items()):
            self._wait(stream, d, val)
        self.count[dom] += inc
        cnt = self.count[dom]
        self.items[stream].append(("op", fn, dom, inc, tag))
        for k in rkeys:
            self._record("r", k, dom, cnt)
        for k in wkeys:
            self._record("w", k, dom, cnt)

    def dma(self, stream, out, in_, tag=""):
        ring = self.rings[stream]
        dom = ring[self.dma_idx[stream] % len(ring)]
        self.dma_idx[stream] += 1
        self._wait(stream, dom, self.count[dom])
        self.op(stream, lambda e: e.dma_start(out=out, in_=in_),
                ins=[in_], outs=[out], dom=dom, inc=16, tag=tag)

    def barrier(self):
        for s in STREAMS:
            for d in self.domains:
                self._wait(s, d, self.count[d])
        self.tens = {}

    def finalize(self, block, sems):
        eng_attr = {"pe": "tensor", "dve": "vector", "act": "scalar",
                    "pool": "gpsimd", "sp": "sync"}

        def mk(stream):
            def body(e):
                for it in self.items[stream]:
                    if it[0] == "w":
                        e.wait_ge(sems[it[1]], it[2])
                    else:
                        _, fn, dom, inc, tag = it
                        instr = fn(e)
                        instr.then_inc(sems[dom], inc)
            return body

        for stream in STREAMS:
            getattr(block, eng_attr[stream])(mk(stream))


class Pipe:

    def __init__(self, depth=1):
        self.q = []
        self.depth = depth

    def push(self, front, tail):
        front()
        self.q.append(tail)
        if len(self.q) > self.depth:
            self.q.pop(0)()

    def flush(self):
        for t in self.q:
            t()
        self.q = []



MID = [0.25, 0.75, 1.25, 1.75, 2.5, 3.5, 5.0]
WJ = [0.5, 0.5, 0.5, 0.5, 1.0, 1.0, 2.0]
P = 128


def emit_ln(em, x_ap, xn_ap, s, si=0, dump=None):
    st = s["stats"]
    ssq = st[:, 32:33]
    sm = st[:, 33:34]
    rstd = st[:, 2 * si:2 * si + 1]
    mneg = st[:, 2 * si + 1:2 * si + 2]
    mu, mu2, var, sd = s["mu"], s["mu2"], s["var"], s["sd"]
    if dump is None:
        dump = s["dump"]
    em.op("act", lambda e: e.activation(out=dump, in_=x_ap, func=AF.Square,
                                        accum_out=ssq),
          ins=[x_ap], outs=[dump, ssq])
    em.op("dve", lambda e: e.tensor_reduce(out=sm, in_=x_ap, axis=AX.X, op=ALU.add),
          ins=[x_ap], outs=[sm])
    em.op("dve", lambda e: e.tensor_scalar(out=mu, in0=sm, scalar1=1.0 / D,
                                           scalar2=None, op0=ALU.mult),
          ins=[sm], outs=[mu])
    em.op("dve", lambda e: e.tensor_scalar(out=mu2, in0=mu, scalar1=mu,
                                           scalar2=None, op0=ALU.mult),
          ins=[mu], outs=[mu2])
    em.op("dve", lambda e: e.scalar_tensor_tensor(out=var, in0=ssq, scalar=1.0 / D,
                                                  in1=mu2, op0=ALU.mult,
                                                  op1=ALU.subtract),
          ins=[ssq, mu2], outs=[var])
    em.op("dve", lambda e: e.tensor_scalar(out=sd, in0=var, scalar1=1e-5,
                                           scalar2=None, op0=ALU.add),
          ins=[var], outs=[sd])
    em.op("act", lambda e: e.activation(out=s["sd2"], in_=sd, func=AF.Sqrt),
          ins=[sd], outs=[s["sd2"]])
    em.op("dve", lambda e: e.reciprocal(out=rstd, in_=s["sd2"]),
          ins=[s["sd2"]], outs=[rstd])
    em.op("dve", lambda e: e.tensor_scalar(out=mneg, in0=mu, scalar1=rstd,
                                           scalar2=-1.0, op0=ALU.mult, op1=ALU.mult),
          ins=[mu, rstd], outs=[mneg])
    em.op("dve", lambda e: e.tensor_scalar(out=xn_ap, in0=x_ap, scalar1=rstd,
                                           scalar2=mneg, op0=ALU.mult, op1=ALU.add),
          ins=[x_ap, rstd, mneg], outs=[xn_ap])


QPAR = [0]


def emit_quant(em, src, dst, W, q):
    off = 0
    while off < W:
        w = min(512, W - off)
        _quant_slice(em, src[:, off:off + w], dst[:, off:off + w], w, q,
                     QPAR[0] % 2)
        QPAR[0] += 1
        off += w


def _quant_slice(em, src, dst, W, q, par):
    nb = W // 16
    o = par * 512
    ob = par * 32
    amax = q["amax"][:, ob:ob + nb]
    scaleb = q["scaleb"][:, ob:ob + nb]
    r = q["r"][:, ob:ob + nb]
    z = q["z"][:, o:o + W]
    y = q["y"][:, o:o + W]
    b0, b1, b2, b3, b4 = (q[k][:, o:o + W] for k in ("b0", "b1", "b2", "b3", "b4"))
    src3 = src.rearrange("p (f b) -> p f b", b=16)
    z3 = z.rearrange("p (f b) -> p f b", b=16)
    em.op("dve", lambda e: e.tensor_reduce(out=amax, in_=src3, axis=AX.X,
                                           op=ALU.max, apply_absolute_value=True),
          ins=[src], outs=[amax])
    em.op("dve", lambda e: e.tensor_scalar(out=scaleb, in0=amax, scalar1=6e-12,
                                           scalar2=None, op0=ALU.max),
          ins=[amax], outs=[scaleb])
    em.op("dve", lambda e: e.reciprocal(out=r, in_=scaleb), ins=[scaleb], outs=[r])
    em.op("dve", lambda e: e.tensor_scalar(out=r, in0=r, scalar1=6.0,
                                           scalar2=None, op0=ALU.mult),
          ins=[r], outs=[r])
    em.op("dve", lambda e: e.reciprocal(out=scaleb, in_=r), ins=[r], outs=[scaleb])
    em.op("dve", lambda e: e.tensor_tensor(out=z3, in0=src3,
                                           in1=r.broadcast_to([P, nb, 16]),
                                           op=ALU.mult),
          ins=[src, r], outs=[z])
    em.op("act", lambda e: e.activation(out=y, in_=z, func=AF.Abs),
          ins=[z], outs=[y])
    em.op("act", lambda e: e.activation(out=b4, in_=z, func=AF.Sign),
          ins=[z], outs=[b4])

    def cmp(dstb, j, eng="dve"):
        em.op(eng, lambda e: e.tensor_scalar(out=dstb, in0=y, scalar1=MID[j],
                                             scalar2=WJ[j], op0=ALU.is_gt,
                                             op1=ALU.mult),
              ins=[y], outs=[dstb])

    def add(dstb, a, bb, eng="dve"):
        em.op(eng, lambda e: e.tensor_tensor(out=dstb, in0=a, in1=bb, op=ALU.add),
              ins=[a, bb], outs=[dstb])

    cmp(b0, 0)
    cmp(b1, 1)
    add(b2, b0, b1)
    cmp(b0, 2)
    cmp(b1, 3)
    add(b3, b0, b1)
    cmp(b0, 4)
    cmp(b1, 5)
    add(b2, b2, b3)
    add(b3, b0, b1)
    cmp(b0, 6)
    add(b1, b3, b0)
    add(b0, b2, b1)
    em.op("dve", lambda e: e.tensor_tensor(
        out=b1.rearrange("p (f b) -> p f b", b=16),
        in0=b0.rearrange("p (f b) -> p f b", b=16),
        in1=scaleb.broadcast_to([P, nb, 16]), op=ALU.mult),
        ins=[b0, scaleb], outs=[b1])
    em.op("dve", lambda e: e.tensor_tensor(out=dst, in0=b1, in1=b4, op=ALU.mult),
          ins=[b1, b4], outs=[dst])


def emit_tp8(em, src_bf, ps_tp, dst3, ident, nblk=8, evac="act"):
    def tp(e):
        last = None
        for j in range(nblk):
            last = e.transpose(ps_tp[:, j * 128:(j + 1) * 128],
                               in_=src_bf[:, j * 128:(j + 1) * 128],
                               identity=ident)
        return last
    em.op("pe", tp, ins=[src_bf, ident], outs=[ps_tp[:]])
    src3 = ps_tp[:, 0:nblk * 128].rearrange("p (c j) -> p c j", c=nblk)
    em.op(evac, lambda e: e.tensor_copy(out=dst3, in_=src3)
          if evac != "act" else e.activation(out=dst3, in_=src3, func=AF.Copy),
          ins=[ps_tp[:]], outs=[dst3])


def emit_mm(em, ps_ap, pairs, start=True, stop=True, tag=""):
    n = len(pairs)
    def fn(e):
        last = None
        for i, (l, r) in enumerate(pairs):
            last = e.matmul(ps_ap, lhsT=l, rhs=r,
                            start=(start and i == 0),
                            stop=(stop and i == n - 1))
        return last
    ins = [ap for pr in pairs for ap in pr]
    em.op("pe", fn, ins=ins, outs=[ps_ap], tag=tag)


def phase_l0f(nc, em, es, dd, C):
    sb = lambda name, shape, dt=F32: es.enter_context(
        nc.sbuf_tensor("l0f_" + name, shape, dt))
    pst = lambda name, shape, dt=F32: es.enter_context(
        nc.psum_tensor("l0f_" + name, shape, dt))
    wgq = sb("wgq", [P, 8 * I0], BF16)
    wuq = sb("wuq", [P, 8 * I0], BF16)
    wdq = sb("wdq", [P, 22 * D], BF16)
    xq_fm = sb("xqfm", [P, 8 * T], BF16)
    scrf = sb("scrf", [P, 3072])
    scrb = sb("scrb", [P, 2048], BF16)
    qs = {"z": sb("qz", [P, 1024]), "y": sb("qy", [P, 1024]),
          "amax": sb("qam", [P, 64]), "scaleb": sb("qsc", [P, 64]),
          "r": sb("qr", [P, 64])}
    for k in ("b0", "b1", "b2", "b3", "b4"):
        qs[k] = sb("q" + k, [P, 1024], BF16)
    stats = sb("stats", [P, 36])
    small = sb("small", [P, 8])
    lns = {"stats": stats, "mu": small[:, 0:1], "mu2": small[:, 1:2],
           "var": small[:, 2:3], "sd": small[:, 3:4], "sd2": small[:, 4:5]}
    ps_ab = [pst("psab%d" % i, [P, 512]) for i in range(4)]
    ps_dn = [pst("psdn%d" % i, [P, 512]) for i in range(2)]
    ps_tps = [pst("pstp%d" % i, [P, 1024], BF16) for i in range(2)]
    tppar = [0]
    def ps_tp_next():
        tppar[0] += 1
        return ps_tps[tppar[0] % 2]
    ident = C["ident"]

    s01 = [scrf[:, 0:1024], scrf[:, 1024:2048]]
    wq_tms = [scrb[:, 0:1024], scrb[:, 1024:2048]]

    def wtiles(wsrc, nrow):
        for k in range(nrow):
            yield k, wsrc[k * 128:(k + 1) * 128, :]

    for wi, (wsrc, dst, dcol) in enumerate([(dd["wg"], wgq, I0), (dd["wu"], wuq, I0)]):
        for k in range(22):
            stg = s01[k % 2]
            wq_tm = wq_tms[k % 2]
            em.dma("sp", stg, wsrc[k * 128:(k + 1) * 128, :], tag=f"w{wi}t{k}")
            emit_quant(em, stg, wq_tm, 1024, qs)
            dst3 = dst[:, :].rearrange("p (c n) -> p c n", c=8)[:, :, k * 128:(k + 1) * 128]
            emit_tp8(em, wq_tm, ps_tp_next(), dst3, ident)
    for r in range(8):
        for s, W in ((0, 1024), (1, 1024), (2, 768)):
            stg = s01[(r * 3 + s) % 2]
            wq_tm = wq_tms[(r * 3 + s) % 2]
            em.dma("sp", stg[:, 0:W],
                   dd["wd"][r * 128:(r + 1) * 128, s * 1024:s * 1024 + W],
                   tag=f"wdt{r}.{s}")
            emit_quant(em, stg[:, 0:W], wq_tm[:, 0:W], W, qs)
            nb = W // 128
            dst3 = wdq[:, :].rearrange("p (k n) -> p k n", k=22)[
                :, s * 8:s * 8 + nb, r * 128:(r + 1) * 128]
            emit_tp8(em, wq_tm, ps_tp_next(), dst3, ident, nblk=nb)

    xn = scrf[:, 2048:3072]
    for tt in range(NT):
        stg = s01[tt % 2]
        xq_tm = wq_tms[tt % 2]
        em.dma("sp", stg, dd["x"][tt * 128:(tt + 1) * 128, :], tag=f"xt{tt}")
        emit_ln(em, stg, xn, lns, si=tt, dump=xn)
        emit_quant(em, xn, xq_tm, 1024, qs)
        dst3 = xq_fm[:, :].rearrange("p (c n) -> p c n", c=8)[:, :, tt * 128:(tt + 1) * 128]
        emit_tp8(em, xq_tm, ps_tp_next(), dst3, ident)

    h3fs = [scrf[:, 0:512], scrf[:, 2048:2560]]
    sgs = [scrf[:, 512:1024], scrf[:, 2560:3072]]
    gtile = scrf[:, 1024:2048]
    h3qs = [scrb[:, 0:512], scrb[:, 1024:1536]]
    h3qfms = [scrb[:, 512:1024], scrb[:, 1536:2048]]
    HGS = [(hg, 512 if hg < 5 else 256) for hg in range(6)]
    for tt in range(NT):
        xcol = lambda c: xq_fm[:, c * T + tt * 128: c * T + (tt + 1) * 128]
        for hg, W in HGS:
            pa = ps_ab[(hg % 2) * 2]
            pb = ps_ab[(hg % 2) * 2 + 1]
            sg = sgs[hg % 2]
            h3f = h3fs[hg % 2]
            h3q = h3qs[hg % 2]
            h3qfm = h3qfms[hg % 2]
            emit_mm(em, pa[:, 0:W],
                    [(xcol(c), wgq[:, c * I0 + hg * 512: c * I0 + hg * 512 + W])
                     for c in range(8)], tag=f"gate{tt}.{hg}")
            emit_mm(em, pb[:, 0:W],
                    [(xcol(c), wuq[:, c * I0 + hg * 512: c * I0 + hg * 512 + W])
                     for c in range(8)], tag=f"up{tt}.{hg}")
            em.op("act", lambda e, pa=pa, W=W, sg=sg: e.activation(
                out=sg[:, 0:W], in_=pa[:, 0:W], func=AF.Silu),
                ins=[pa[:]], outs=[sg[:, 0:W]])
            em.op("dve", lambda e, pb=pb, W=W, sg=sg, h3f=h3f: e.tensor_tensor(
                out=h3f[:, 0:W], in0=sg[:, 0:W], in1=pb[:, 0:W], op=ALU.mult),
                ins=[sg[:, 0:W], pb[:]], outs=[h3f[:, 0:W]])
            emit_quant(em, h3f[:, 0:W], h3q[:, 0:W], W, qs)
            nb = W // 128
            dst3 = h3qfm[:, 0:W].rearrange("p (c j) -> p c j", c=nb)
            emit_tp8(em, h3q, ps_tp_next(), dst3, ident, nblk=nb)
            for half in range(2):
                emit_mm(em, ps_dn[half][:],
                        [(h3qfm[:, j * 128:(j + 1) * 128],
                          wdq[:, (hg * 4 + j) * D + half * 512:
                              (hg * 4 + j) * D + half * 512 + 512])
                         for j in range(nb)],
                        start=(hg == 0), stop=(hg == 5), tag=f"dn{tt}.{hg}.{half}")
        for half in range(2):
            em.op("act", lambda e, half=half: e.activation(
                out=gtile[:, half * 512:(half + 1) * 512], in_=ps_dn[half][:],
                func=AF.Silu),
                ins=[ps_dn[half][:]], outs=[gtile[:, half * 512:(half + 1) * 512]])
        em.dma("sp", dd["g0"][tt * 128:(tt + 1) * 128, :], gtile, tag=f"g0t{tt}")


def phase_l0a(nc, em, es, dd, C):
    sb = lambda name, shape, dt=F32: es.enter_context(
        nc.sbuf_tensor("l0a_" + name, shape, dt))
    pst = lambda name, shape, dt=F32: es.enter_context(
        nc.psum_tensor("l0a_" + name, shape, dt))
    wq_sb = sb("wq", [P, 8 * D], BF16)
    wk_sb = sb("wk", [P, 8 * D], BF16)
    wv_sb = sb("wv", [P, 8 * D], BF16)
    wo_sb = sb("wo", [P, 8 * D], BF16)
    bqkv = sb("bqkv", [P, 24])
    bvrep = sb("bvrep", [P, D])
    borep = sb("borep", [P, D])
    xn_fm = sb("xnfm", [P, 8 * 512], BF16)
    k_fm = sb("kfm", [P, 8 * T], BF16)
    v_tm = sb("vtm", [P, 16 * 1024], BF16)
    q_col = sb("qcol", [P, 8 * 512], BF16)
    a_col = q_col
    Pb = sb("p", [P, 6144], BF16)
    PT = sb("pt", [P, 6144], BF16)
    sdm = sb("sdm", [P, 384])
    xwork = [sb("xw0", [P, D])[:], sb("xw1", [P, D])[:]]
    x_bf = sb("xbf", [P, D], BF16)
    gwork = sb("gw", [P, D])
    swork = sb("sw", [P, D])
    stats = sb("stats", [P, 36])
    small = sb("small", [P, 8])
    lns = {"stats": stats, "mu": small[:, 0:1], "mu2": small[:, 1:2],
           "var": small[:, 2:3], "sd": small[:, 3:4], "sd2": small[:, 4:5],
           "dump": swork[:, 0:1024]}
    mrow = sb("mrow", [P, 36])
    ps_s0 = pst("pss0", [P, 1024])
    ps_s1 = pst("pss1", [P, 1024])
    ps_pr = [pst("pspr%d" % i, [P, 512]) for i in range(2)]
    ps_tp = [pst("pstp%d" % i, [P, 1024], BF16) for i in range(2)]
    ident = C["ident"]
    mdiag = C["mdiag"]

    for wt, (src, dst) in enumerate([(dd["wqT"], wq_sb), (dd["wkT"], wk_sb),
                                     (dd["wvT"], wv_sb), (dd["woT"], wo_sb)]):
        for d8 in range(8):
            stg = xwork[d8 % 2]
            em.dma("sp", stg, src[d8 * 128:(d8 + 1) * 128, :], tag=f"w{wt}.{d8}")
            em.op("act", lambda e, stg=stg, dst=dst, d8=d8: e.activation(
                out=dst[:, d8 * D:(d8 + 1) * D], in_=stg, func=AF.Copy),
                ins=[stg], outs=[dst[:, d8 * D:(d8 + 1) * D]])
    em.dma("sp", bqkv[:], dd["bqkv"])
    em.dma("sp", bvrep[:], dd["bvrep"])
    em.dma("sp", borep[:], dd["borep"])

    scale = 1.0 / 8.0
    for t4 in range(4):
        tl = t4 * 512
        for loc in range(4):
            tt = t4 * 4 + loc
            stg = xwork[tt % 2]
            em.dma("sp", stg, dd["x"][tt * 128:(tt + 1) * 128, :], tag=f"xa{tt}")
            emit_ln(em, stg, x_bf[:], lns, si=tt)
            dst3 = xn_fm[:, :].rearrange("p (c n) -> p c n", c=8)[:, :, loc * 128:(loc + 1) * 128]
            emit_tp8(em, x_bf[:], ps_tp[tt % 2], dst3, ident)
        for c in range(8):
            for which, wsb, dstcol in [(0, wq_sb, q_col), (1, wk_sb, k_fm)]:
                ps = ps_pr[c % 2]
                emit_mm(em, ps[:],
                        [(wsb[:, d * D + c * 128:d * D + (c + 1) * 128],
                          xn_fm[:, d * 512:d * 512 + 512]) for d in range(8)],
                        tag=f"prj{which}.{t4}.{c}")
                if which == 0:
                    out = dstcol[:, c * 512:(c + 1) * 512]
                else:
                    out = dstcol[:, c * T + tl:c * T + tl + 512]
                em.op("dve", lambda e, ps=ps, out=out, c=c, which=which:
                      e.tensor_scalar(out=out, in0=ps[:],
                                      scalar1=bqkv[:, 8 * which + c:8 * which + c + 1],
                                      scalar2=None, op0=ALU.add),
                      ins=[ps[:], bqkv[:]], outs=[out])
        for ts4 in range(4):
            tt = t4 * 4 + ts4
            for half in range(2):
                ps = ps_pr[half]
                emit_mm(em, ps[:],
                        [(xn_fm[:, d * 512 + ts4 * 128:d * 512 + (ts4 + 1) * 128],
                          wv_sb[:, d * D + half * 512:d * D + half * 512 + 512])
                         for d in range(8)], tag=f"v{tt}.{half}")
                out = v_tm[:, tt * 1024 + half * 512:tt * 1024 + half * 512 + 512]
                em.op("dve", lambda e, ps=ps, out=out, half=half:
                      e.tensor_tensor(out=out, in0=ps[:],
                                      in1=bvrep[:, half * 512:half * 512 + 512],
                                      op=ALU.add),
                      ins=[ps[:], bvrep[:]], outs=[out])
        apipe = [None]
        for qq in range(4):
            qt = t4 * 4 + qq
            kpre = qt * 128
            for h in range(H):
                def mk(h=h, qt=qt, qq=qq, kpre=kpre):
                    c, poff = h // 2, 64 * (h % 2)
                    par = h % 2
                    par3 = h % 2
                    q_h = q_col[poff:poff + 64,
                                c * 512 + qq * 128:c * 512 + (qq + 1) * 128]
                    kbase = c * T
                    k_h = lambda lo, w: k_fm[poff:poff + 64, kbase + lo:kbase + lo + w]
                    sdmh = sdm[:, par3 * 128:(par3 + 1) * 128]
                    mr = lambda j: mrow[:, par3 * 12 + j:par3 * 12 + j + 1]
                    m_A, md, m_f, mb, z_A, z_d, zr, alz = (mr(j) for j in range(8))
                    sA = (ps_s0, ps_s1)[par] if qt < 8 else ps_s0
                    sB = ps_s1
                    pb0 = par3 * 2048
                    a_end = min(kpre, 1024)
                    b_len = kpre - a_end
                    diag_in_A = qt < 8
                    st = {}

                    def front():
                        for lo in range(0, a_end, 512):
                            w = min(512, a_end - lo)
                            emit_mm(em, sA[:, lo:lo + w], [(q_h, k_h(lo, w))],
                                    tag=f"qkA{qt}.{h}")
                        if diag_in_A:
                            emit_mm(em, sA[:, kpre:kpre + 128],
                                    [(q_h, k_h(kpre, 128))], tag=f"qkdA{qt}.{h}")
                            em.op("dve", lambda e: e.tensor_tensor(
                                out=sA[:, kpre:kpre + 128],
                                in0=sA[:, kpre:kpre + 128], in1=mdiag,
                                op=ALU.add), ins=[sA[:], mdiag],
                                outs=[sA[:, kpre:kpre + 128]])
                            em.op("dve", lambda e: e.tensor_reduce(
                                out=m_A, in_=sA[:, 0:kpre + 128], axis=AX.X,
                                op=ALU.max), ins=[sA[:]], outs=[m_A])
                            em.op("dve", lambda e: e.tensor_scalar(
                                out=mb, in0=m_A, scalar1=-scale, scalar2=None,
                                op0=ALU.mult), ins=[m_A], outs=[mb])
                            em.op("act", lambda e: e.activation(
                                out=Pb[:, pb0:pb0 + kpre + 128],
                                in_=sA[:, 0:kpre + 128], func=AF.Exp, scale=scale,
                                bias=mb, accum_out=z_A),
                                ins=[sA[:], mb],
                                outs=[Pb[:, pb0:pb0 + kpre + 128], z_A])
                        else:
                            em.op("dve", lambda e: e.tensor_reduce(
                                out=m_A, in_=sA[:, 0:512], axis=AX.X, op=ALU.max),
                                ins=[sA[:]], outs=[m_A])
                            em.op("dve", lambda e: e.tensor_reduce(
                                out=md, in_=sA[:, 512:1024], axis=AX.X, op=ALU.max),
                                ins=[sA[:]], outs=[md])
                            em.op("dve", lambda e: e.tensor_scalar(
                                out=mb, in0=m_A, scalar1=md, scalar2=-scale,
                                op0=ALU.max, op1=ALU.mult),
                                ins=[m_A, md], outs=[mb])
                            em.op("dve", lambda e: e.tensor_tensor(
                                out=m_A, in0=m_A, in1=md, op=ALU.max),
                                ins=[m_A, md], outs=[m_A])
                            z_A2 = mr(8)
                            em.op("act", lambda e: e.activation(
                                out=Pb[:, pb0:pb0 + 512], in_=sA[:, 0:512],
                                func=AF.Exp, scale=scale, bias=mb, accum_out=z_A),
                                ins=[sA[:, 0:512], mb],
                                outs=[Pb[:, pb0:pb0 + 512], z_A])
                            em.op("act", lambda e: e.activation(
                                out=Pb[:, pb0 + 512:pb0 + 1024], in_=sA[:, 512:1024],
                                func=AF.Exp, scale=scale, bias=mb, accum_out=z_A2),
                                ins=[sA[:, 512:1024], mb],
                                outs=[Pb[:, pb0 + 512:pb0 + 1024], z_A2])
                            for lo in range(0, b_len, 512):
                                w = min(512, b_len - lo)
                                emit_mm(em, sB[:, lo:lo + w],
                                        [(q_h, k_h(1024 + lo, w))],
                                        tag=f"qkB{qt}.{h}")
                            emit_mm(em, sB[:, b_len:b_len + 128],
                                    [(q_h, k_h(kpre, 128))], tag=f"qkdB{qt}.{h}")
                            em.op("dve", lambda e: e.tensor_tensor(
                                out=sB[:, b_len:b_len + 128],
                                in0=sB[:, b_len:b_len + 128], in1=mdiag,
                                op=ALU.add), ins=[sB[:], mdiag],
                                outs=[sB[:, b_len:b_len + 128]])
                            em.op("dve", lambda e: e.tensor_reduce(
                                out=m_f, in_=sB[:, 0:b_len + 128], axis=AX.X,
                                op=ALU.max), ins=[sB[:]], outs=[m_f])
                            em.op("dve", lambda e: e.tensor_scalar(
                                out=mb, in0=m_f, scalar1=m_A, scalar2=-scale,
                                op0=ALU.max, op1=ALU.mult),
                                ins=[m_f, m_A], outs=[mb])
                            em.op("act", lambda e: e.activation(
                                out=Pb[:, pb0 + 1024:pb0 + 1024 + b_len + 128],
                                in_=sB[:, 0:b_len + 128], func=AF.Exp,
                                scale=scale, bias=mb, accum_out=z_d),
                                ins=[sB[:], mb],
                                outs=[Pb[:, pb0 + 1024:pb0 + 1024 + b_len + 128],
                                      z_d])

                    def tail_dve():
                        if diag_in_A:
                            em.op("dve", lambda e: e.reciprocal(
                                out=zr, in_=z_A), ins=[z_A], outs=[zr])
                            em.op("dve", lambda e: e.tensor_scalar(
                                out=Pb[:, pb0:pb0 + kpre + 128],
                                in0=Pb[:, pb0:pb0 + kpre + 128], scalar1=zr,
                                scalar2=None, op0=ALU.mult),
                                ins=[Pb[:, pb0:pb0 + kpre + 128], zr],
                                outs=[Pb[:, pb0:pb0 + kpre + 128]])
                        else:
                            z_A2 = mr(8)
                            em.op("act", lambda e: e.activation(
                                out=md, in_=m_A, func=AF.Exp, scale=scale,
                                bias=mb), ins=[m_A, mb], outs=[md])
                            em.op("dve", lambda e, z_A2=z_A2: e.tensor_tensor(
                                out=z_A, in0=z_A, in1=z_A2, op=ALU.add),
                                ins=[z_A, z_A2], outs=[z_A])
                            em.op("dve", lambda e: e.scalar_tensor_tensor(
                                out=z_A, in0=z_A, scalar=md, in1=z_d,
                                op0=ALU.mult, op1=ALU.add),
                                ins=[z_A, md, z_d], outs=[z_A])
                            em.op("dve", lambda e: e.reciprocal(out=zr, in_=z_A),
                                  ins=[z_A], outs=[zr])
                            em.op("dve", lambda e: e.tensor_scalar(
                                out=alz, in0=md, scalar1=zr, scalar2=None,
                                op0=ALU.mult), ins=[md, zr], outs=[alz])
                            em.op("dve", lambda e: e.tensor_scalar(
                                out=Pb[:, pb0:pb0 + 1024], in0=Pb[:, pb0:pb0 + 1024],
                                scalar1=alz, scalar2=None, op0=ALU.mult),
                                ins=[Pb[:, pb0:pb0 + 1024], alz],
                                outs=[Pb[:, pb0:pb0 + 1024]])
                            em.op("dve", lambda e: e.tensor_scalar(
                                out=Pb[:, pb0 + 1024:pb0 + kpre + 128],
                                in0=Pb[:, pb0 + 1024:pb0 + kpre + 128], scalar1=zr,
                                scalar2=None, op0=ALU.mult),
                                ins=[Pb[:, pb0 + 1024:pb0 + kpre + 128], zr],
                                outs=[Pb[:, pb0 + 1024:pb0 + kpre + 128]])

                    def tail_pe():
                        nblk = qt + 1
                        pt0 = pb0
                        for g in range(0, nblk, 8):
                            nb = min(8, nblk - g)
                            dst3 = PT[:, pt0 + g * 128:pt0 + (g + nb) * 128].rearrange(
                                "p (c j) -> p c j", c=nb)
                            emit_tp8(em, Pb[:, pb0 + g * 128:pb0 + (g + nb) * 128],
                                     ps_tp[(g // 8 + h) % 2], dst3, ident, nblk=nb,
                                     evac="dve")
                        pvp = ps_pr[h % 2]
                        emit_mm(em, pvp[0:64, 0:128],
                                [(v_tm[:, kb * 1024 + h * HD:kb * 1024 + (h + 1) * HD],
                                  PT[:, pt0 + kb * 128:pt0 + (kb + 1) * 128])
                                 for kb in range(nblk)],
                                tag=f"pv{qt}.{h}")
                        dsta = a_col[poff:poff + 64,
                                     c * 512 + qq * 128:c * 512 + (qq + 1) * 128]
                        em.op("act", lambda e: e.activation(
                            out=dsta, in_=pvp[0:64, 0:128], func=AF.Copy),
                            ins=[pvp[:]], outs=[dsta])

                    return front, tail_dve, tail_pe
                f_, td_, tp_ = mk()
                if apipe[0] is None:
                    f_()
                else:
                    ptd, ptp = apipe[0]
                    ptd()
                    f_()
                    ptp()
                apipe[0] = (td_, tp_)
        if apipe[0] is not None:
            apipe[0][0]()
            apipe[0][1]()
        for ts4 in range(4):
            tt = t4 * 4 + ts4
            em.dma("sp", gwork[:], dd["g0"][tt * 128:(tt + 1) * 128, :], tag=f"g0{tt}")
            em.dma("sp", xwork[tt % 2], dd["x"][tt * 128:(tt + 1) * 128, :],
                   tag=f"xb{tt}")
            for half in range(2):
                ps = ps_pr[half]
                emit_mm(em, ps[:],
                        [(a_col[:, c * 512 + ts4 * 128:c * 512 + (ts4 + 1) * 128],
                          wo_sb[:, c * D + half * 512:c * D + half * 512 + 512])
                         for c in range(8)], tag=f"o{tt}.{half}")
                em.op("dve", lambda e, ps=ps, half=half: e.tensor_tensor(
                    out=swork[:, half * 512:half * 512 + 512], in0=ps[:],
                    in1=borep[:, half * 512:half * 512 + 512], op=ALU.add),
                    ins=[ps[:], borep[:]],
                    outs=[swork[:, half * 512:half * 512 + 512]])
            em.op("dve", lambda e: e.tensor_tensor(out=gwork[:], in0=swork[:],
                                                   in1=gwork[:], op=ALU.mult),
                  ins=[swork[:], gwork[:]], outs=[gwork[:]])
            em.op("dve", lambda e, tt=tt: e.tensor_tensor(
                out=swork[:], in0=gwork[:], in1=xwork[tt % 2], op=ALU.add),
                ins=[gwork[:], xwork[tt % 2]], outs=[swork[:]])
            em.dma("sp", dd["x1"][tt * 128:(tt + 1) * 128, :], swork[:],
                   tag=f"x1{tt}")


def phase_l1f(nc, em, es, dd, C):
    sb = lambda name, shape, dt=F32: es.enter_context(
        nc.sbuf_tensor("l1f_" + name, shape, dt))
    pst = lambda name, shape, dt=F32: es.enter_context(
        nc.psum_tensor("l1f_" + name, shape, dt))
    wg = sb("wg", [P, 8 * I1], BF16)
    wu = sb("wu", [P, 8 * I1], BF16)
    wd = sb("wd", [P, 24 * D], BF16)
    xn_fm = sb("xnfm", [P, 8 * T], BF16)
    h3fm = sb("h3fm", [P, 24 * 256], BF16)
    sg = sb("sg", [P, 256])
    gtile = sb("gt", [P, D])
    xwork = [sb("xw0", [P, D])[:], sb("xw1", [P, D])[:]]
    x_bf = sb("xbf", [P, D], BF16)
    stats = sb("stats", [P, 36])
    small = sb("small", [P, 16])
    lns = {"stats": stats, "mu": small[:, 0:1], "mu2": small[:, 1:2],
           "var": small[:, 2:3], "sd": small[:, 3:4], "sd2": small[:, 4:5],
           "dump": gtile[:, 0:1024]}
    ps_g = [pst("psg%d" % i, [P, 512]) for i in range(2)]
    ps_u = [pst("psu%d" % i, [P, 512]) for i in range(2)]
    ps_dn = [pst("psdn%d" % i, [P, 512]) for i in range(2)]
    ps_tp = [pst("pstp%d" % i, [P, 1024], BF16) for i in range(2)]
    ident = C["ident"]
    Q = 1.0 / 127.0

    for src, dst, w in ((dd["wg1T"], wg, I1), (dd["wu1T"], wu, I1),
                        (dd["wd1T"], wd, D)):
        nchunk = src.shape[0] // 128
        em.dma("sp", dst[:, 0:nchunk * w].rearrange("p (c n) -> p c n", c=nchunk),
               src.rearrange("(c p) n -> p c n", p=128), tag="wload")

    for tt in range(NT):
        stg = xwork[tt % 2]
        em.dma("sp", stg, dd["x1"][tt * 128:(tt + 1) * 128, :], tag=f"xa{tt}")
        emit_ln(em, stg, x_bf[:], lns, si=tt)
        dst3 = xn_fm[:, :].rearrange("p (c n) -> p c n", c=8)[:, :, tt * 128:(tt + 1) * 128]
        emit_tp8(em, x_bf[:], ps_tp[tt % 2], dst3, ident)

    for t8 in range(8):
        tl = t8 * 256
        for hc in range(24):
            pg, pu = ps_g[hc % 2], ps_u[hc % 2]
            emit_mm(em, pg[:, 0:256],
                    [(wg[:, d * I1 + hc * 128:d * I1 + (hc + 1) * 128],
                      xn_fm[:, d * T + tl:d * T + tl + 256]) for d in range(8)],
                    tag=f"g{t8}.{hc}")
            emit_mm(em, pu[:, 0:256],
                    [(wu[:, d * I1 + hc * 128:d * I1 + (hc + 1) * 128],
                      xn_fm[:, d * T + tl:d * T + tl + 256]) for d in range(8)],
                    tag=f"u{t8}.{hc}")
            em.op("act", lambda e, pg=pg: e.activation(
                out=sg[:], in_=pg[:, 0:256], func=AF.Silu, scale=Q),
                ins=[pg[:]], outs=[sg[:]])
            em.op("dve", lambda e, pu=pu, hc=hc: e.tensor_tensor(
                out=h3fm[:, hc * 256:(hc + 1) * 256], in0=sg[:], in1=pu[:, 0:256],
                op=ALU.mult),
                ins=[sg[:], pu[:]], outs=[h3fm[:, hc * 256:(hc + 1) * 256]])
        for ts in range(2):
            tt = t8 * 2 + ts
            for half in range(2):
                pd = ps_dn[half]
                emit_mm(em, pd[:],
                        [(h3fm[:, j * 256 + ts * 128:j * 256 + (ts + 1) * 128],
                          wd[:, j * D + half * 512:j * D + half * 512 + 512])
                         for j in range(24)],
                        tag=f"dn{tt}.{half}")
                em.op("act", lambda e, pd=pd, half=half: e.activation(
                    out=gtile[:, half * 512:(half + 1) * 512],
                    in_=pd[:], func=AF.Silu, scale=1.0 / 16129.0),
                    ins=[pd[:]],
                    outs=[gtile[:, half * 512:(half + 1) * 512]])
            em.dma("sp", dd["g1"][tt * 128:(tt + 1) * 128, :], gtile[:],
                   tag=f"g1t{tt}")


def phase_l1a(nc, em, es, dd, C):
    sb = lambda name, shape, dt=F32: es.enter_context(
        nc.sbuf_tensor("l1a_" + name, shape, dt))
    pst = lambda name, shape, dt=F32: es.enter_context(
        nc.psum_tensor("l1a_" + name, shape, dt))
    wq_sb = sb("wq", [P, 8 * D], BF16)
    wk_sb = sb("wk", [P, 8 * D], BF16)
    wv_sb = sb("wv", [P, 8 * D], BF16)
    wo_sb = sb("wo", [P, 16 * D], BF16)
    wqf_bf = sb("wqf", [P, 16], BF16)
    wkf_bf = sb("wkf", [P, 16], BF16)
    nwrep = sb("nwrep", [P, D])
    h_fm = sb("hfm", [P, 8 * 512], BF16)
    q_fm = sb("qfm", [P, 8 * 512], BF16)
    k_col = sb("kcol", [P, 8 * 512], BF16)
    k_prev = sb("kprev", [P, 8 * 128], BF16)
    v_prev = sb("vprev", [P, 1024], BF16)
    lin_col = sb("lincol", [P, 8 * 512], BF16)
    win_col = q_fm
    qf_aug = sb("qfaug", [32, 512], BF16)
    kf_aug = sb("kfaug", [32, 512], BF16)
    qphi1 = sb("qphi1", [P, 4 * 512], BF16)
    qphi2 = sb("qphi2", [32, 4 * 512], BF16)
    kphi1 = sb("kphi1", [P, 4 * 512], BF16)
    kphi2 = sb("kphi2", [32, 4 * 512], BF16)
    kphi_tm = sb("kphitm", [P, 16 * 160], BF16)
    tphi = sb("tphi", [P, 1024], BF16)
    v65 = sb("v65", [P, 4 * 16 * 65], BF16)
    kv1 = sb("kv1", [P, 16 * 65])
    kv2 = sb("kv2", [32, 16 * 65])
    kv1b = sb("kv1b", [P, 16 * 65], BF16)
    kv2b = sb("kv2b", [32, 16 * 65], BF16)
    atm4 = sb("atm4", [P, 512], BF16)
    ytmp4 = sb("ytmp4", [P, 512], BF16)
    swin = sb("swin", [P, 4 * 512])
    Pw = sb("pw", [P, 512], BF16)
    PTw = sb("ptw", [P, 512], BF16)
    xwork = [sb("xw0", [P, D])[:], sb("xw1", [P, D])[:]]
    xnw = sb("xnw", [P, D])
    x_bf = sb("xbf", [P, D], BF16)
    gwork = sb("gw", [P, D])
    swork = sb("sw", [P, D])
    stats = sb("stats", [P, 36])
    small = sb("small", [P, 16])
    mrow = sb("mrow", [P, 20])
    lns = {"stats": stats, "mu": small[:, 0:1], "mu2": small[:, 1:2],
           "var": small[:, 2:3], "sd": small[:, 3:4], "sd2": small[:, 4:5],
           "dump": swork[:, 0:1024]}
    ps_m0 = pst("psm0", [P, 512])
    ps_m1 = pst("psm1", [P, 512])
    ps_a = pst("psa", [P, 512])
    ps_y = pst("psy", [P, 512])
    ps_kv = pst("pskv", [P, 512])
    ps_kv2 = pst("pskv2", [P, 512])
    ps_o = pst("pso", [P, 512])
    ps_tp = [pst("pstp0", [P, 1024], BF16)]
    ident, mtri, ones_c = C["ident"], C["mtri"], C["ones_col"]
    mwin = C["mwin"]
    c17 = C["c17"]
    sq1c = C["sq1"]
    sq2c = C["sq2"]
    SL1, SR1 = c17[0:17, 0:128], c17[0:17, 128:256]
    SL2, SR2 = c17[0:17, 256:288], c17[0:17, 288:320]
    SLa, SRa = c17[0:17, 320:480], c17[0:17, 480:640]

    for wt, (src, dst, nch) in enumerate([(dd["wq1"], wq_sb, 8),
                                          (dd["wk1"], wk_sb, 8),
                                          (dd["wv1"], wv_sb, 8),
                                          (dd["wout"], wo_sb, 16)]):
        for d8 in range(nch):
            stg = xwork[d8 % 2]
            em.dma("sp", stg, src[d8 * 128:(d8 + 1) * 128, :], tag=f"w{wt}.{d8}")
            em.op("act", lambda e, stg=stg, dst=dst, d8=d8: e.activation(
                out=dst[:, d8 * D:(d8 + 1) * D], in_=stg, func=AF.Copy),
                ins=[stg], outs=[dst[:, d8 * D:(d8 + 1) * D]])
    em.dma("sp", xwork[0][:, 0:16], dd["wqf"], tag="wqf")
    em.op("act", lambda e: e.activation(out=wqf_bf[:], in_=xwork[0][:, 0:16],
                                        func=AF.Copy),
          ins=[xwork[0]], outs=[wqf_bf[:]])
    em.dma("sp", xwork[1][:, 0:16], dd["wkf"], tag="wkf")
    em.op("act", lambda e: e.activation(out=wkf_bf[:], in_=xwork[1][:, 0:16],
                                        func=AF.Copy),
          ins=[xwork[1]], outs=[wkf_bf[:]])
    em.dma("sp", nwrep[:], dd["normw_rep"])
    for t_, val in ((kv1[:], 0.0), (kv2[:], 0.0), (kv1b[:], 0.0), (kv2b[:], 0.0),
                    (k_prev[:], 0.0), (v_prev[:], 0.0),
                    (qf_aug[:], 1.0), (kf_aug[:], 1.0)):
        em.op("pool", lambda e, t_=t_, val=val: e.memset(t_, val),
              ins=[], outs=[t_])

    for t4 in range(4):
        tl = t4 * 512
        for loc in range(4):
            tt = t4 * 4 + loc
            stg = xwork[tt % 2]
            em.dma("sp", stg, dd["x1"][tt * 128:(tt + 1) * 128, :], tag=f"xa{tt}")
            emit_ln(em, stg, xnw[:], lns, si=tt)
            ms, trs = small[:, 5:6], small[:, 6:7]
            em.op("act", lambda e: e.activation(out=swork[:], in_=xnw[:],
                                                func=AF.Square, accum_out=ms),
                  ins=[xnw[:]], outs=[swork[:], ms])
            em.op("dve", lambda e: e.tensor_scalar(out=trs, in0=ms, scalar1=1.0 / D,
                                                   scalar2=1e-6, op0=ALU.mult,
                                                   op1=ALU.add),
                  ins=[ms], outs=[trs])
            em.op("act", lambda e: e.activation(out=small[:, 7:8], in_=trs,
                                                func=AF.Sqrt),
                  ins=[trs], outs=[small[:, 7:8]])
            em.op("dve", lambda e: e.reciprocal(out=trs, in_=small[:, 7:8]),
                  ins=[small[:, 7:8]], outs=[trs])
            em.op("dve", lambda e: e.scalar_tensor_tensor(
                out=x_bf[:], in0=xnw[:], scalar=trs, in1=nwrep[:],
                op0=ALU.mult, op1=ALU.mult),
                ins=[xnw[:], trs, nwrep[:]], outs=[x_bf[:]])
            dst3 = h_fm[:, :].rearrange("p (c n) -> p c n", c=8)[:, :, loc * 128:(loc + 1) * 128]
            emit_tp8(em, x_bf[:], ps_tp[0], dst3, ident)
        spipe = Pipe()
        em.op("pool", lambda e: e.memset(v65[:], 1.0), ins=[], outs=[v65[:]])
        for c in range(8):
            for which, wsb, out in [(0, wq_sb, q_fm[:, c * 512:(c + 1) * 512]),
                                    (1, wk_sb, k_col[:, c * 512:(c + 1) * 512])]:
                ps = ps_m0 if which == 0 else ps_m1
                emit_mm(em, ps[:],
                        [(wsb[:, d * D + c * 128:d * D + (c + 1) * 128],
                          h_fm[:, d * 512:d * 512 + 512]) for d in range(8)],
                        tag=f"prj{which}.{t4}.{c}")
                eng = "act" if which == 0 else "dve"
                em.op(eng, lambda e, ps=ps, out=out, eng=eng: (
                    e.activation(out=out, in_=ps[:], func=AF.Copy) if eng == "act"
                    else e.tensor_copy(out=out, in_=ps[:])),
                    ins=[ps[:]], outs=[out])
        for ts4 in range(4):
            for half in range(2):
                ps = ps_o
                emit_mm(em, ps[:],
                        [(h_fm[:, d * 512 + ts4 * 128:d * 512 + (ts4 + 1) * 128],
                          wv_sb[:, d * D + half * 512:d * D + half * 512 + 512])
                         for d in range(8)], tag=f"v{t4}.{ts4}.{half}")
                out = v65[:, ts4 * 1040 + half * 8 * 65:
                          ts4 * 1040 + half * 8 * 65 + 8 * 65].rearrange(
                    "p (h w) -> p h w", w=65)[:, :, 0:64]
                em.op("dve", lambda e, ps=ps, out=out: e.tensor_copy(
                    out=out, in_=ps[:].rearrange("p (h w) -> p h w", w=64)),
                    ins=[ps[:]], outs=[v65[:]])
        for g in range(4):
            heads = [4 * g + j for j in range(4)]
            for j, h in enumerate(heads):
                c, poff = h // 2, 64 * (h % 2)
                for src_fm, wf, aug in ((q_fm, wqf_bf, qf_aug), (k_col, wkf_bf, kf_aug)):
                    ps = ps_m0
                    emit_mm(em, ps[0:16, :],
                            [(wf[poff:poff + 64, :],
                              src_fm[poff:poff + 64, c * 512:(c + 1) * 512])],
                            tag=f"qf{t4}.{h}")
                    em.op("act", lambda e, ps=ps, aug=aug: e.activation(
                        out=aug[0:16, :], in_=ps[0:16, :], func=AF.Copy),
                        ins=[ps[:]], outs=[aug[0:16, :]])
                pmL = (ps_m0, ps_a)[j % 2]
                pmR = (ps_m1, ps_y)[j % 2]
                for aug, pall1, pall2, sq in ((qf_aug, qphi1, qphi2, True),
                                              (kf_aug, kphi1, kphi2, False)):
                    p1 = pall1[:, j * 512:(j + 1) * 512]
                    p2 = pall2[0:32, j * 512:(j + 1) * 512]
                    for SL, SR, pp, w, sqap in ((SL1, SR1, p1, 128, sq1c),
                                                (SL2, SR2, p2, 32, sq2c)):
                        tph = tphi[0:w, (j % 2) * 512:(j % 2) * 512 + 512]
                        emit_mm(em, pmL[0:w, :], [(SL[:, 0:w], aug[0:17, :])],
                                tag="phL")
                        emit_mm(em, pmR[0:w, :], [(SR[:, 0:w], aug[0:17, :])],
                                tag="phR")
                        if sq:
                            em.op("dve", lambda e, w=w, sqap=sqap, tph=tph,
                                  pmL=pmL: e.tensor_scalar(
                                out=tph, in0=pmL[0:w, :], scalar1=sqap,
                                scalar2=None, op0=ALU.mult),
                                ins=[pmL[:]], outs=[tph])
                        else:
                            em.op("act", lambda e, w=w, tph=tph, pmL=pmL:
                                  e.activation(
                                out=tph, in_=pmL[0:w, :], func=AF.Copy),
                                ins=[pmL[:]], outs=[tph])
                        em.op("dve", lambda e, pp=pp, w=w, tph=tph, pmR=pmR:
                              e.tensor_tensor(
                            out=pp, in0=tph, in1=pmR[0:w, :], op=ALU.mult),
                            ins=[tph, pmR[:]], outs=[pp])
                for c2 in range(2):
                    for k2 in range(2):
                        cc = c2 * 2 + k2
                        emit_mm(em, pmL[:, k2 * 160:k2 * 160 + 160],
                                [(kf_aug[0:17, cc * 128:(cc + 1) * 128], SLa)],
                                tag="phLtm")
                        emit_mm(em, pmR[:, k2 * 160:k2 * 160 + 160],
                                [(kf_aug[0:17, cc * 128:(cc + 1) * 128], SRa)],
                                tag="phRtm")
                    tph = tphi[:, (j % 2) * 512:(j % 2) * 512 + 320]
                    em.op("act", lambda e, tph=tph, pmL=pmL: e.activation(
                        out=tph, in_=pmL[:, 0:320], func=AF.Copy),
                        ins=[pmL[:]], outs=[tph])
                    ktm2 = kphi_tm[:, (j * 4 + c2 * 2) * 160:
                                   (j * 4 + c2 * 2 + 2) * 160]
                    em.op("dve", lambda e, ktm2=ktm2, tph=tph, pmR=pmR:
                          e.tensor_tensor(
                        out=ktm2, in0=tph, in1=pmR[:, 0:320], op=ALU.mult),
                        ins=[tph, pmR[:]], outs=[ktm2])
            for cc in range(4):
                def mkscan(g=g, cc=cc, heads=tuple(heads)):
                    n_glob = t4 * 4 + cc
                    par = cc % 2
                    csl = lambda pall, j: pall[:, j * 512 + cc * 128:j * 512 + (cc + 1) * 128]
                    csl2 = lambda pall, j: pall[0:32, j * 512 + cc * 128:j * 512 + (cc + 1) * 128]
                    v65s = lambda j: v65[:, cc * 1040 + (4 * g + j) * 65:
                                         cc * 1040 + (4 * g + j) * 65 + 65]
                    ytp = ytmp4[:, par * 256:(par + 1) * 256]

                    def front():
                        for j in range(4):
                            emit_mm(em, ps_a[:, j * 128:(j + 1) * 128],
                                    [(csl(kphi1, j), csl(qphi1, j)),
                                     (csl2(kphi2, j), csl2(qphi2, j))],
                                    tag=f"at{n_glob}.{g}.{j}")
                        em.op("dve", lambda e: e.tensor_tensor(
                            out=atm4[:].rearrange("p (h w) -> p h w", w=128),
                            in0=ps_a[:].rearrange("p (h w) -> p h w", w=128),
                            in1=mtri.broadcast_to([P, 128, 4]).transpose([0, 2, 1]),
                            op=ALU.mult),
                            ins=[ps_a[:], mtri], outs=[atm4[:]])
                        for j, h in enumerate(heads):
                            emit_mm(em, ps_y[:, j * 65:(j + 1) * 65],
                                    [(atm4[:, j * 128:(j + 1) * 128], v65s(j)),
                                     (csl(qphi1, j), kv1b[:, h * 65:h * 65 + 65]),
                                     (csl2(qphi2, j), kv2b[0:32, h * 65:h * 65 + 65])],
                                    tag=f"y{n_glob}.{g}.{j}")
                        ze4 = mrow[:, 0:4]
                        zr4 = mrow[:, 4:8]
                        yv = ps_y[:, 0:260].rearrange("p (h w) -> p h w", w=65)
                        em.op("dve", lambda e: e.tensor_scalar(
                            out=ze4, in0=yv[:, :, 64:65], scalar1=1e-6,
                            scalar2=None, op0=ALU.add), ins=[ps_y[:]], outs=[ze4])
                        em.op("dve", lambda e: e.reciprocal(
                            out=zr4, in_=ze4), ins=[ze4], outs=[zr4])
                        em.op("dve", lambda e: e.tensor_tensor(
                            out=ytp.rearrange("p (h w) -> p h w", w=64),
                            in0=yv[:, :, 0:64],
                            in1=zr4.broadcast_to([P, 4, 64]), op=ALU.mult),
                            ins=[ps_y[:], zr4], outs=[ytp])
                        for j, h in enumerate(heads):
                            ktm = kphi_tm[:, (j * 4 + cc) * 160:(j * 4 + cc) * 160 + 160]
                            emit_mm(em, ps_kv[:, j * 65:(j + 1) * 65],
                                    [(ktm[:, 0:128], v65s(j))],
                                    tag=f"kv1u{n_glob}.{g}.{j}")
                            emit_mm(em, ps_kv2[0:25, j * 65:(j + 1) * 65],
                                    [(ktm[:, 128:153], v65s(j))],
                                    tag=f"kv2u{n_glob}.{g}.{j}")
                        gsl = slice(4 * g * 65, (4 * g + 4) * 65)
                        for dst_f, dst_b, src_ap in (
                                (kv1[:, gsl], kv1b[:, gsl], ps_kv[:, 0:260]),
                                (kv2[0:25, gsl], kv2b[0:25, gsl],
                                 ps_kv2[0:25, 0:260])):
                            em.op("dve", lambda e, dst_f=dst_f, src_ap=src_ap:
                                  e.tensor_tensor(out=dst_f, in0=dst_f,
                                                  in1=src_ap, op=ALU.add),
                                  ins=[dst_f, src_ap], outs=[dst_f])
                            em.op("pool", lambda e, dst_f=dst_f, dst_b=dst_b:
                                  e.tensor_copy(out=dst_b, in_=dst_f),
                                  ins=[dst_f], outs=[dst_b])
                        for pr in range(2):
                            h0 = heads[2 * pr]
                            cpr = h0 // 2
                            sbank = (ps_o, ps_m0)[pr]
                            for j2 in range(2):
                                h = h0 + j2
                                poff = 64 * (h % 2)
                                q_h = q_fm[poff:poff + 64,
                                           cpr * 512 + cc * 128:cpr * 512 + (cc + 1) * 128]
                                kcur = k_col[poff:poff + 64,
                                             cpr * 512 + cc * 128:cpr * 512 + (cc + 1) * 128]
                                so = j2 * 256
                                if n_glob == 0:
                                    em.op("dve", lambda e, sbank=sbank, so=so:
                                          e.memset(sbank[:, so:so + 128], -1e30),
                                          ins=[], outs=[sbank[:]])
                                else:
                                    kprev_ap = (k_prev[poff:poff + 64,
                                                       cpr * 128:(cpr + 1) * 128]
                                                if cc == 0 else
                                                k_col[poff:poff + 64,
                                                      cpr * 512 + (cc - 1) * 128:
                                                      cpr * 512 + cc * 128])
                                    emit_mm(em, sbank[:, so:so + 128],
                                            [(q_h, kprev_ap)],
                                            tag=f"wS0.{n_glob}.{h}")
                                emit_mm(em, sbank[:, so + 128:so + 256],
                                        [(q_h, kcur)], tag=f"wS1.{n_glob}.{h}")
                            em.op("dve", lambda e, sbank=sbank: e.tensor_tensor(
                                out=sbank[:, 0:512].rearrange("p (h w) -> p h w", w=256),
                                in0=sbank[:, 0:512].rearrange("p (h w) -> p h w", w=256),
                                in1=mwin.broadcast_to([P, 256, 2]).transpose([0, 2, 1]),
                                op=ALU.add),
                                ins=[sbank[:], mwin], outs=[sbank[:]])
                            swp = swin[:, (par * 2 + pr) * 512:(par * 2 + pr + 1) * 512]
                            em.op("dve", lambda e, sbank=sbank, swp=swp:
                                  e.tensor_copy(out=swp, in_=sbank[:, 0:512]),
                                  ins=[sbank[:]], outs=[swp])

                    def tail():
                        def tpy(e):
                            last = None
                            for j in range(4):
                                last = e.transpose(
                                    ps_tp[0][64 * (j % 2):64 * (j % 2) + 64,
                                             (j // 2) * 128:(j // 2) * 128 + 128],
                                    in_=ytp[:, j * 64:(j + 1) * 64], identity=ident)
                            return last
                        em.op("pe", tpy, ins=[ytp, ident], outs=[ps_tp[0][:]])
                        for pr in range(2):
                            cpr = heads[2 * pr] // 2
                            dst = lin_col[:, cpr * 512 + cc * 128:cpr * 512 + (cc + 1) * 128]
                            em.op("act", lambda e, pr=pr, dst=dst: e.activation(
                                out=dst, in_=ps_tp[0][:, pr * 128:(pr + 1) * 128],
                                func=AF.Copy),
                                ins=[ps_tp[0][:]], outs=[dst])
                        for pr in range(2):
                            h0 = heads[2 * pr]
                            cpr = h0 // 2
                            swp = swin[:, (par * 2 + pr) * 512:(par * 2 + pr + 1) * 512]
                            wm2 = mrow[:, 8 + pr * 2:8 + pr * 2 + 2]
                            em.op("dve", lambda e, swp=swp, wm2=wm2: e.tensor_reduce(
                                out=wm2, in_=swp.rearrange("p (h w) -> p h w", w=256),
                                axis=AX.X, op=ALU.max), ins=[swp], outs=[wm2])
                            wmb2 = mrow[:, 12 + pr * 2:12 + pr * 2 + 2]
                            em.op("dve", lambda e, wm2=wm2, wmb2=wmb2:
                                  e.tensor_scalar(
                                out=wmb2, in0=wm2, scalar1=-0.125, scalar2=None,
                                op0=ALU.mult), ins=[wm2], outs=[wmb2])
                            Pwp = Pw[:]
                            zw2 = mrow[:, 16 + pr * 2:16 + pr * 2 + 2]
                            for j3 in range(2):
                                em.op("act", lambda e, swp=swp, Pwp=Pwp, j3=j3,
                                      wmb2=wmb2, zw2=zw2: e.activation(
                                    out=Pwp[:, j3 * 256:(j3 + 1) * 256],
                                    in_=swp[:, j3 * 256:(j3 + 1) * 256],
                                    func=AF.Exp, scale=0.125,
                                    bias=wmb2[:, j3:j3 + 1],
                                    accum_out=zw2[:, j3:j3 + 1]),
                                    ins=[swp[:, j3 * 256:(j3 + 1) * 256],
                                         wmb2[:, j3:j3 + 1]],
                                    outs=[Pwp[:, j3 * 256:(j3 + 1) * 256],
                                          zw2[:, j3:j3 + 1]])
                            em.op("dve", lambda e, zw2=zw2: e.reciprocal(
                                out=zw2, in_=zw2), ins=[zw2], outs=[zw2])
                            em.op("pool", lambda e, Pwp=Pwp, zw2=zw2: e.tensor_tensor(
                                out=Pwp.rearrange("p (h w) -> p h w", w=256),
                                in0=Pwp.rearrange("p (h w) -> p h w", w=256),
                                in1=zw2.broadcast_to([P, 2, 256]), op=ALU.mult),
                                ins=[Pwp, zw2], outs=[Pwp])
                            def tpw(e, Pwp=Pwp):
                                last = None
                                for b2 in range(4):
                                    last = e.transpose(
                                        ps_tp[0][:, 256 + b2 * 128:256 + (b2 + 1) * 128],
                                        in_=Pwp[:, b2 * 128:(b2 + 1) * 128],
                                        identity=ident)
                                return last
                            em.op("pe", tpw, ins=[Pwp, ident], outs=[ps_tp[0][:]])
                            PTwp = PTw[:]
                            em.op("act", lambda e, PTwp=PTwp: e.activation(
                                out=PTwp, in_=ps_tp[0][:, 256:768], func=AF.Copy),
                                ins=[ps_tp[0][:]], outs=[PTwp])
                            for j2 in range(2):
                                h = h0 + j2
                                vprev_ap = (v_prev[:, h * HD:(h + 1) * HD]
                                            if cc == 0 else
                                            v65[:, (cc - 1) * 1040 + h * 65:
                                                (cc - 1) * 1040 + h * 65 + HD])
                                vcur_ap = v65[:, cc * 1040 + h * 65:
                                              cc * 1040 + h * 65 + HD]
                                emit_mm(em, ps_m1[64 * j2:64 * j2 + 64,
                                                  pr * 128:(pr + 1) * 128],
                                        [(vprev_ap, PTwp[:, j2 * 256:j2 * 256 + 128]),
                                         (vcur_ap,
                                          PTwp[:, j2 * 256 + 128:j2 * 256 + 256])],
                                        tag=f"wpv{n_glob}.{h}")
                            dstw = win_col[:, cpr * 512 + cc * 128:cpr * 512 + (cc + 1) * 128]
                            em.op("act", lambda e, pr=pr, dstw=dstw: e.activation(
                                out=dstw, in_=ps_m1[:, pr * 128:(pr + 1) * 128],
                                func=AF.Copy),
                                ins=[ps_m1[:]], outs=[dstw])

                    return front, tail
                spipe.push(*mkscan())
        spipe.flush()
        em.op("dve", lambda e: e.tensor_copy(
            out=k_prev[:, :].rearrange("p (c j) -> p c j", c=8),
            in_=k_col[:, :].rearrange("p (c j) -> p c j", j=512)[:, :, 384:512]),
            ins=[k_col[:]], outs=[k_prev[:]])
        em.op("dve", lambda e: e.tensor_copy(
            out=v_prev[:].rearrange("p (h w) -> p h w", w=64),
            in_=v65[:, 3 * 1040:].rearrange("p (h w) -> p h w", w=65)[:, :, 0:64]),
            ins=[v65[:]], outs=[v_prev[:]])
        for ts4 in range(4):
            tt = t4 * 4 + ts4
            em.dma("sp", gwork[:], dd["g1"][tt * 128:(tt + 1) * 128, :],
                   tag=f"g1{tt}")
            em.dma("sp", xwork[ts4 % 2], dd["x1"][tt * 128:(tt + 1) * 128, :],
                   tag=f"xb{tt}")
            em.op("dve", lambda e, ts4=ts4, tt=tt: e.tensor_scalar(
                out=xnw[:], in0=xwork[ts4 % 2], scalar1=stats[:, 2 * tt:2 * tt + 1],
                scalar2=stats[:, 2 * tt + 1:2 * tt + 2], op0=ALU.mult, op1=ALU.add),
                ins=[xwork[ts4 % 2], stats[:]], outs=[xnw[:]])
            for half in range(2):
                emit_mm(em, ps_o[:],
                        [(lin_col[:, c2 * 512 + ts4 * 128:c2 * 512 + (ts4 + 1) * 128],
                          wo_sb[:, c2 * D + half * 512:c2 * D + half * 512 + 512])
                         for c2 in range(8)] +
                        [(win_col[:, c2 * 512 + ts4 * 128:c2 * 512 + (ts4 + 1) * 128],
                          wo_sb[:, (8 + c2) * D + half * 512:(8 + c2) * D + half * 512 + 512])
                         for c2 in range(8)], tag=f"wo{tt}.{half}")
                em.op("dve", lambda e, half=half: e.tensor_tensor(
                    out=swork[:, half * 512:half * 512 + 512], in0=ps_o[:],
                    in1=xnw[:, half * 512:half * 512 + 512], op=ALU.add),
                    ins=[ps_o[:], xnw[:]],
                    outs=[swork[:, half * 512:half * 512 + 512]])
            em.op("dve", lambda e: e.tensor_tensor(out=gwork[:], in0=swork[:],
                                                   in1=gwork[:], op=ALU.mult),
                  ins=[swork[:], gwork[:]], outs=[gwork[:]])
            em.op("dve", lambda e, ts4=ts4: e.tensor_tensor(
                out=swork[:], in0=gwork[:], in1=xwork[ts4 % 2], op=ALU.add),
                ins=[gwork[:], xwork[ts4 % 2]], outs=[swork[:]])
            em.dma("sp", dd["y"][tt * 128:(tt + 1) * 128, :], swork[:],
                   tag=f"y{tt}")


import math

NCORES = 8


def _consts():
    iu, ju = np.triu_indices(16)
    a_map = np.array([16] + list(range(16)) + list(iu))
    b_map = np.array([16] + [16] * 16 + list(ju))
    scale = np.array([1.0] + [1.0] * 16 +
                     [0.5 if i == j else 1.0 / math.sqrt(2.0)
                      for i, j in zip(iu, ju)])
    SL = np.zeros((17, 153), np.float32)
    SR = np.zeros((17, 153), np.float32)
    for f in range(153):
        SL[a_map[f], f] = 1.0
        SR[b_map[f], f] = 1.0
    c17 = np.zeros((32, 640), np.float32)
    c17[0:17, 0:128] = SL[:, 0:128]
    c17[0:17, 128:256] = SR[:, 0:128]
    c17[0:17, 256:281] = SL[:, 128:153]
    c17[0:17, 288:313] = SR[:, 128:153]
    c17[0:17, 320:473] = SL
    c17[0:17, 480:633] = SR
    cbf = np.zeros((128, 384), np.float32)
    cbf[:, 0:128] = np.eye(128)
    cbf[:, 128:256] = (np.arange(128)[:, None] <= np.arange(128)[None, :])
    cbf[:, 256] = 1.0
    ii = np.arange(128)[:, None]
    jj = np.arange(256)[None, :]
    mwin = np.where((jj - ii >= 64) & (jj - ii <= 128), 0.0,
                    -1e30).astype(np.float32)
    mdiag = np.where(np.arange(128)[None, :] <= np.arange(128)[:, None],
                     0.0, -1e30).astype(np.float32)
    sq = (scale * scale).astype(np.float32)
    sq1 = np.zeros((128, 1), np.float32)
    sq1[0:128, 0] = sq[0:128]
    sq2 = np.zeros((128, 1), np.float32)
    sq2[0:25, 0] = sq[128:153]
    return {"c17": c17.astype(BF), "cbf": cbf.astype(BF), "mwin": mwin,
            "mdiag": mdiag, "sq1": sq1, "sq2": sq2}


def _build(phase):
    nc = bass.Bass("TRN2", target_bir_lowering=False)
    dm = {}

    def din(name, shape, dt=F32):
        dm[name] = nc.dram_tensor(name, shape, dt, kind="ExternalInput").ap()

    def dout(name, shape, dt=F32):
        dm[name] = nc.dram_tensor(name, shape, dt, kind="ExternalOutput").ap()

    if phase == "l0f":
        din("x", [T, D]); din("wg", [I0, D]); din("wu", [I0, D])
        din("wd", [D, I0]); dout("g0", [T, D])
        din("cst_bf", [128, 384], BF16)
    elif phase == "l0a":
        din("x", [T, D])
        for n in ("wqT", "wkT", "wvT", "woT"):
            din(n, [D, D])
        din("bqkv", [128, 24]); din("bvrep", [128, D]); din("borep", [128, D])
        din("g0", [T, D]); dout("x1", [T, D])
        din("cst_bf", [128, 384], BF16); din("cst_f32", [128, 256])
    elif phase == "l1f":
        din("x1", [T, D])
        din("wg1T", [D, I1], BF16); din("wu1T", [D, I1], BF16)
        din("wd1T", [I1, D], BF16)
        dout("g1", [T, D]); din("cst_bf", [128, 384], BF16)
    elif phase == "l1a":
        din("x1", [T, D])
        for n in ("wq1", "wk1", "wv1"):
            din(n, [D, D])
        din("wout", [2 * D, D]); din("wqf", [128, 16]); din("wkf", [128, 16])
        din("normw_rep", [128, D]); din("g1", [T, D]); dout("y", [T, D])
        din("cst_bf", [128, 384], BF16); din("cst17", [32, 640], BF16)
        din("cst_f32", [128, 258])

    with ExitStack() as es:
        em = Em(nc)
        sems = {d: es.enter_context(nc.semaphore(d)) for d in em.domains}
        cb = es.enter_context(nc.sbuf_tensor("cstb_sb", [128, 384], BF16))
        em.dma("sp", cb[:], dm["cst_bf"])
        C = {"ident": cb[:, 0:128], "mtri": cb[:, 128:256],
             "ones_col": cb[:, 256:257]}
        if "cst_f32" in dm:
            w_cf = int(dm["cst_f32"].shape[1])
            cf = es.enter_context(nc.sbuf_tensor("cstf_sb", [128, w_cf], F32))
            em.dma("sp", cf[:], dm["cst_f32"])
            C["mdiag"] = cf[:, 0:128]
            C["mwin"] = cf[:, 0:256]
            if w_cf >= 258:
                C["sq1"] = cf[:, 256:257]
                C["sq2"] = cf[0:32, 257:258]
        if "cst17" in dm:
            c17sb = es.enter_context(nc.sbuf_tensor("c17_sb", [32, 640], BF16))
            em.dma("sp", c17sb[:], dm["cst17"])
            C["c17"] = c17sb[:]
        {"l0f": phase_l0f, "l0a": phase_l0a,
         "l1f": phase_l1f, "l1a": phase_l1a}[phase](nc, em, es, dm, C)
        em.barrier()
        block = es.enter_context(nc.Block())
        em.finalize(block, sems)
    return nc


def _build_fused():
    nc = bass.Bass("TRN2", target_bir_lowering=False)
    dm = {}

    def din(name, shape, dt=F32):
        dm[name] = nc.dram_tensor(name, shape, dt, kind="ExternalInput").ap()

    def dint(name, shape, dt=F32):
        dm[name] = nc.dram_tensor(name, shape, dt, kind="Internal").ap()

    din("x", [T, D])
    din("wg", [I0, D]); din("wu", [I0, D]); din("wd", [D, I0])
    for n in ("wqT", "wkT", "wvT", "woT"):
        din(n, [D, D])
    din("bqkv", [128, 24]); din("bvrep", [128, D]); din("borep", [128, D])
    din("wg1T", [D, I1], BF16); din("wu1T", [D, I1], BF16)
    din("wd1T", [I1, D], BF16)
    for n in ("wq1", "wk1", "wv1"):
        din(n, [D, D])
    din("wout", [2 * D, D]); din("wqf", [128, 16]); din("wkf", [128, 16])
    din("normw_rep", [128, D])
    din("cst_bf", [128, 384], BF16); din("cst17", [32, 640], BF16)
    din("cst_f32", [128, 386])
    dint("g0", [T, D]); dint("x1", [T, D]); dint("g1", [T, D])
    dm["y"] = nc.dram_tensor("y", [T, D], F32, kind="ExternalOutput").ap()

    with ExitStack() as es:
        em = Em(nc)
        sems = {d: es.enter_context(nc.semaphore(d)) for d in em.domains}
        cb = es.enter_context(nc.sbuf_tensor("cstb_sb", [128, 384], BF16))
        cf = es.enter_context(nc.sbuf_tensor("cstf_sb", [128, 386], F32))
        c17sb = es.enter_context(nc.sbuf_tensor("c17_sb", [32, 640], BF16))
        em.dma("sp", cb[:], dm["cst_bf"])
        em.dma("sp", cf[:], dm["cst_f32"])
        em.dma("sp", c17sb[:], dm["cst17"])
        C = {"ident": cb[:, 0:128], "mtri": cb[:, 128:256],
             "ones_col": cb[:, 256:257], "mdiag": cf[:, 0:128],
             "mwin": cf[:, 128:384], "sq1": cf[:, 384:385],
             "sq2": cf[0:32, 385:386], "c17": c17sb[:]}
        d_l1 = dm
        with ExitStack() as es1:
            phase_l0f(nc, em, es1, dm, C)
        em.barrier()
        with ExitStack() as es2:
            phase_l0a(nc, em, es2, dm, C)
        em.barrier()
        with ExitStack() as es3:
            phase_l1f(nc, em, es3, d_l1, C)
        em.barrier()
        with ExitStack() as es4:
            phase_l1a(nc, em, es4, d_l1, C)
        em.barrier()
        block = es.enter_context(nc.Block())
        em.finalize(block, sems)
    return nc


_NC_CACHE = {}


def _get_nc(phase):
    if phase not in _NC_CACHE:
        _NC_CACHE[phase] = _build_fused() if phase == "fused" else _build(phase)
    return _NC_CACHE[phase]


def kernel(L0_attn_x, L0_attn_Wq, L0_attn_bq, L0_attn_Wk, L0_attn_bk,
           L0_attn_Wv, L0_attn_bv, L0_attn_Wo, L0_attn_bo,
           L0_ffn_w_gate, L0_ffn_w_up, L0_ffn_w_down,
           L1_attn_norm_w, L1_attn_Wq, L1_attn_Wk, L1_attn_Wv,
           L1_attn_Wqf, L1_attn_Wkf, L1_attn_Wout,
           L1_ffn_w_gate, L1_ffn_w_up, L1_ffn_w_down):
    f = lambda a: np.ascontiguousarray(np.asarray(a), dtype=np.float32)
    cs = _consts()
    x = f(L0_attn_x)
    com_l0f = {"wg": f(L0_ffn_w_gate), "wu": f(L0_ffn_w_up),
               "wd": f(L0_ffn_w_down), "cst_bf": cs["cbf"]}
    bqkv = np.zeros((128, 24), np.float32)
    for i, b in enumerate((L0_attn_bq, L0_attn_bk, L0_attn_bv)):
        bqkv[:, 8 * i:8 * (i + 1)] = f(b).reshape(8, 128).T
    cf32 = np.zeros((128, 256), np.float32)
    cf32[:, 0:128] = cs["mdiag"]
    com_l0a = {"wqT": f(np.asarray(L0_attn_Wq).T), "wkT": f(np.asarray(L0_attn_Wk).T),
               "wvT": f(np.asarray(L0_attn_Wv).T), "woT": f(np.asarray(L0_attn_Wo).T),
               "bqkv": bqkv, "bvrep": np.tile(f(L0_attn_bv)[None, :], (128, 1)),
               "borep": np.tile(f(L0_attn_bo)[None, :], (128, 1)),
               "cst_bf": cs["cbf"], "cst_f32": cf32}
    com_l1f = {"wg1T": np.ascontiguousarray(np.asarray(L1_ffn_w_gate).T).astype(BF),
               "wu1T": np.ascontiguousarray(np.asarray(L1_ffn_w_up).T).astype(BF),
               "wd1T": np.ascontiguousarray(np.asarray(L1_ffn_w_down).T).astype(BF),
               "cst_bf": cs["cbf"]}
    wqf = f(L1_attn_Wqf)
    wkf = f(L1_attn_Wkf)
    com_l1a = {"wq1": f(L1_attn_Wq), "wk1": f(L1_attn_Wk), "wv1": f(L1_attn_Wv),
               "wout": f(L1_attn_Wout),
               "wqf": np.concatenate([wqf, wqf], 0),
               "wkf": np.concatenate([wkf, wkf], 0),
               "normw_rep": np.tile(f(L1_attn_norm_w)[None, :], (128, 1)),
               "cst_bf": cs["cbf"], "cst17": cs["c17"],
               "cst_f32": np.concatenate(
                   [cs["mwin"], cs["sq1"], cs["sq2"]], 1)}
    cores = list(range(NCORES))
    nc = _get_nc("fused")
    cstf_fused = np.concatenate([cs["mdiag"], cs["mwin"], cs["sq1"], cs["sq2"]], 1)
    com = {}
    for d in (com_l0f, com_l0a, com_l1f, com_l1a):
        com.update(d)
    com["cst_f32"] = cstf_fused
    per_core = [dict(com, x=x[i]) for i in range(NCORES)]
    res = run_bass_kernel_spmd(nc, per_core, core_ids=cores)
    y = np.stack([res.results[i]["y"] for i in range(NCORES)], 0)
    return y.astype(np.float32)
